# version 18
# baseline (speedup 1.0000x reference)
import numpy as np
import concourse.bass as bass
import concourse.mybir as mybir
from concourse.bass_utils import run_bass_kernel_spmd

F32 = mybir.dt.float32
BF16 = mybir.dt.bfloat16
FP8 = mybir.dt.float8e4
I32 = mybir.dt.int32
AF = mybir.ActivationFunctionType
ALU = mybir.AluOpType
AX = mybir.AxisListType

D = 2048
KC = D // 128
TOK = 1024
PRE = 1024
I = 5632
IC = I // 128
IB = I // 512
MAGIC = float(np.float32(1.5 * 2 ** 23))
EPS = 1e-5
RSQRT_C = 0x5F3759DF
FQ = 4.0 / (448.0 * 448.0)


class SemC:
    def __init__(self, h):
        self.h = h
        self.n = 0

    def inc(self, k=1):
        self.n += k
        return (self.h, self.n)

    def cur(self):
        return (self.h, self.n)

    def lag(self, k):
        return (self.h, max(0, self.n - k))


class Rec:
    CHAINED = ("vector", "scalar", "gpsimd")

    def __init__(self):
        self.st = {e: [] for e in ("sync", "vector", "scalar", "gpsimd", "tensor")}
        self.last = {e: None for e in self.st}

    def emit(self, eng, fn, wait=None, sig=None, chain_sem=None, nochain=False):
        if eng in self.CHAINED and wait is not None:
            if wait[1] > 0:
                self.st[eng].append(("W", wait[0], wait[1]))
            wait = None
        last = self.last[eng] if (eng in self.CHAINED and not nochain) else None
        if sig is not None:
            sem, amount = sig
            tgt = sem.inc(amount)
            upd = (tgt[0], amount)
        elif eng in self.CHAINED and chain_sem is not None:
            tgt = chain_sem.inc(1)
            upd = (tgt[0], 1)
        else:
            tgt, upd = None, None
        self.st[eng].append(("I", fn, wait, last, upd))
        if eng in self.CHAINED and tgt is not None and not nochain:
            self.last[eng] = tgt
        return tgt

    def wait(self, eng, sem, val=None):
        v = sem.n if val is None else val
        h = sem.h if hasattr(sem, "h") else sem
        if v > 0:
            self.st[eng].append(("W", h, v))

    def replay(self, eng, engine):
        for item in self.st[eng]:
            if item[0] == "W":
                engine.wait_ge(item[1], item[2])
                continue
            _, fn, wait, last, upd = item
            ins = fn(engine)
            if wait is not None and wait[1] > 0:
                ins._wait_ge(wait[0], wait[1])
            elif last is not None:
                ins._wait_ge(last[0], last[1])
            if upd is not None:
                ins.then_inc(upd[0], upd[1])


def build_program():
    nc = bass.Bass("TRN2", target_bir_lowering=False, debug=False)

    d_in = {}
    for name, shape in [
        ("xpre", [PRE, D]), ("xown", [TOK, D]),
        ("wi_sw", [KC, 128, D]), ("wf_sw", [KC, 128, D]),
        ("wg_sw", [KC, 128, D]), ("woT_sw", [KC, 128, D]),
        ("si_img", [128, KC]), ("sf_img", [128, KC]), ("sg_img", [128, KC]),
        ("so_img", [128, KC]), ("gnw_img", [128, KC]),
        ("wgat_nat", [IC, 128, D]), ("wup_nat", [IC, 128, D]),
        ("wdn_nat", [KC, 128, I]),
    ]:
        d_in[name] = nc.dram_tensor(name, shape, F32, kind="ExternalInput").ap()
    y_out = nc.dram_tensor("y", [TOK, D], F32, kind="ExternalOutput").ap()

    qwg_sc = nc.dram_tensor("qwg_sc", [IB, 128, KC * 512], FP8, kind="Internal").ap()
    qwu_sc = nc.dram_tensor("qwu_sc", [IB, 128, KC * 512], FP8, kind="Internal").ap()
    qwd_sc = nc.dram_tensor("qwd_sc", [4, 128, IC * 512], FP8, kind="Internal").ap()
    woT_bf = nc.dram_tensor("woT_bf", [KC, 128, D], BF16, kind="Internal").ap()
    s_sc = nc.dram_tensor("s_sc", [TOK, D], F32, kind="Internal").ap()
    h_sc = nc.dram_tensor("h_sc", [KC, 128, TOK], F32, kind="Internal").ap()
    ws_dr = nc.dram_tensor("ws_dr", [3, IC, 128], F32, kind="Internal").ap()

    from contextlib import ExitStack
    es0 = ExitStack()

    def mk(stack):
        def sb(name, shape, dtype):
            return stack.enter_context(nc.sbuf_tensor(name, shape, dtype))

        def ps(name, shape, dtype=F32):
            return stack.enter_context(nc.psum_tensor(name, shape, dtype))
        return sb, ps

    sb0, ps0 = mk(es0)

    def sem(name):
        return SemC(es0.enter_context(nc.semaphore(name)))

    si = sb0("si", [128, KC], F32)
    sf = sb0("sf", [128, KC], F32)
    nsf = sb0("nsf", [128, KC], F32)
    sgv = sb0("sgv", [128, KC], F32)
    sov = sb0("sov", [128, KC], F32)
    gnw = sb0("gnw", [128, KC], F32)
    gso = sb0("gso", [128, KC], F32)
    rmax = sb0("rmax", [128, 8], F32)
    sgf = sb0("sgf", [128, 8], F32)
    sg2 = sb0("sg2", [128, 8], F32)
    tyf = sb0("tyf", [128, 8], F32)
    identb = sb0("identb", [128, 128], BF16)
    ident8 = sb0("ident8", [128, 128], FP8)
    identf = sb0("identf", [128, 128], F32)
    ones_col = sb0("ones_col", [128, 1], F32)
    ones_row = sb0("ones_row", [1, 128], F32)
    row_t = sb0("row_t", [1, 512], F32)
    wsg_cols = sb0("wsg_cols", [128, IC], F32)
    wsu_cols = sb0("wsu_cols", [128, IC], F32)
    wsd_cols = sb0("wsd_cols", [128, KC], F32)
    qxT3 = sb0("qxT3", [128, KC, TOK], FP8)
    ssum = sb0("ssum", [1, TOK], F32)
    rstd_r = sb0("rstd_r", [1, TOK], F32)
    ir1 = sb0("ir1", [1, TOK], I32)
    ir2 = sb0("ir2", [1, TOK], I32)
    wmaxc = sb0("wmaxc", [128, 1], F32)
    wrec = sb0("wrec", [128, 1], F32)
    wn1 = sb0("wn1", [128, 1], F32)
    wn2 = sb0("wn2", [128, 1], F32)
    li1 = sb0("li1", [128, 1], I32)
    li2 = sb0("li2", [128, 1], I32)
    gm1 = sb0("gm1", [128, 1], F32)
    gm2 = sb0("gm2", [128, 1], F32)
    wst_sb = sb0("wst_sb", [IC, 128], F32)
    wtmp = sb0("wtmp", [128, IC], F32)
    wtmp2 = sb0("wtmp2", [128, IC], F32)
    lnm_c = sb0("lnm_c", [128, 16], F32)
    lsq_c = sb0("lsq_c", [128, 16], F32)
    lrs_c = sb0("lrs_c", [128, 16], F32)
    lv1_c = sb0("lv1_c", [128, 16], F32)
    lv2_c = sb0("lv2_c", [128, 16], F32)
    lvi_c = sb0("lvi_c", [128, 16], I32)
    lvi2_c = sb0("lvi2_c", [128, 16], I32)
    pbank = [ps0(f"pbank{i_}", [128, 512], F32) for i_ in range(8)]

    ch_v, ch_s, ch_g = sem("ch_v"), sem("ch_s"), sem("ch_g")
    d_small = sem("d_small")
    d_x = [sem(f"d_x{i_}") for i_ in range(4)]
    d_wif = [sem(f"d_wif{i_}") for i_ in range(4)]
    d_wg = [sem(f"d_wg{i_}") for i_ in range(2)]
    d_wo2 = [sem(f"d_wo{i_}") for i_ in range(2)]
    d_nat = [sem(f"d_nat{i_}") for i_ in range(3)]
    d_natd = [sem(f"d_natd{i_}") for i_ in range(2)]
    d_natd2 = sem("d_natd2")
    d_scr2 = [sem(f"d_scr{i_}") for i_ in range(2)]
    d_scrd4 = [sem(f"d_scrd{i_}") for i_ in range(4)]
    d_q8 = [[sem(f"d_q8_{w_}{i_}") for i_ in range(3)] for w_ in range(2)]
    d_q8d = [sem(f"d_q8d{i_}") for i_ in range(4)]
    d_wc = sem("d_wc")
    d_sx = [sem(f"d_sx{i_}") for i_ in range(2)]
    d_ss = [sem(f"d_ss{i_}") for i_ in range(2)]
    d_so = [sem(f"d_so{i_}") for i_ in range(2)]
    d_hsp = [sem(f"d_hsp{i_}") for i_ in range(3)]
    d_hrl = [sem(f"d_hrl{i_}") for i_ in range(3)]
    d_y = [sem(f"d_y{i_}") for i_ in range(2)]
    d_wsr = sem("d_wsr")
    d_row = [sem(f"d_row{i_}") for i_ in range(2)]
    g_id = sem("g_id")
    g_pmax = sem("g_pmax")
    v_misc = sem("v_misc")
    v_ln = sem("v_ln")
    s_ln = sem("s_ln")
    v_x8T = sem("v_x8T")
    v_qxT = sem("v_qxT")
    s_x8T = sem("s_x8T")
    s_qxT = sem("s_qxT")
    t_trx = sem("t_trx")
    s_pq = sem("s_pq")
    t_ptr = sem("t_ptr")
    v_pev = sem("v_pev")
    s_pev = sem("s_pev")
    t_if = sem("t_if")
    s_att = sem("s_att")
    v_att = sem("v_att")
    v_scan = sem("v_scan")
    s_hsq = sem("s_hsq")
    t_h2 = sem("t_h2")
    t_bc = sem("t_bc")
    v_rb = sem("v_rb")
    t_g = sem("t_g")
    s_epi = sem("s_epi")
    v_epi = sem("v_epi")
    t_op = sem("t_op")
    v_sev = sem("v_sev")
    t_gu = sem("t_gu")
    v_ffn = sem("v_ffn")
    s_ffn = sem("s_ffn")
    v_tmax = sem("v_tmax")
    s_qt = sem("s_qt")
    t_trt = sem("t_trt")
    v_qtT = sem("v_qtT")
    t_dn = sem("t_dn")
    v_fin = sem("v_fin")
    t_wsb = sem("t_wsb")
    t_g4 = sem("t_g4")
    t_u4 = sem("t_u4")
    v_wsb = sem("v_wsb")

    R = Rec()

    def V(fn, wait=None, sig=None):
        return R.emit("vector", fn, wait=wait, sig=sig, chain_sem=ch_v)

    def S(fn, wait=None, sig=None):
        return R.emit("scalar", fn, wait=wait, sig=sig, chain_sem=ch_s)

    def G(fn, wait=None, sig=None):
        return R.emit("gpsimd", fn, wait=wait, sig=sig, chain_sem=ch_g)

    def T(fn, wait=None, sig=None):
        return R.emit("tensor", fn, wait=wait, sig=sig)

    def DMA(eng, fn, sig, wait=None):
        return R.emit(eng, fn, wait=wait, sig=sig, nochain=True)


    def recip_nr(dst, src, f1):
        V(lambda v: v.reciprocal(dst, src))
        V(lambda v: v.tensor_tensor(f1, dst, src, ALU.mult))
        V(lambda v: v.tensor_scalar(f1, f1, -1.0, 2.0, ALU.mult, ALU.add))
        V(lambda v: v.tensor_tensor(dst, dst, f1, ALU.mult))

    def rsqrt_nr(dst, src, it1, it2, f1, f2):
        V(lambda v: v.tensor_scalar(it1, src.bitcast(I32), 1, None,
                                    ALU.logical_shift_right))
        V(lambda v: v.tensor_scalar(it2, it1, -1, RSQRT_C, ALU.mult, ALU.add))
        V(lambda v: v.tensor_copy(dst, it2.bitcast(F32)))
        for _ in range(3):
            V(lambda v: v.tensor_tensor(f1, dst, dst, ALU.mult))
            V(lambda v: v.tensor_tensor(f2, f1, src, ALU.mult))
            V(lambda v: v.tensor_scalar(f2, f2, -0.5, 1.5, ALU.mult, ALU.add))
            V(lambda v: v.tensor_tensor(dst, dst, f2, ALU.mult))

    for tile, name in [(si, "si_img"), (sf, "sf_img"), (sgv, "sg_img"),
                       (sov, "so_img"), (gnw, "gnw_img")]:
        DMA("sync", lambda e, t=tile, n=name: e.dma_start(out=t[:], in_=d_in[n]),
            sig=(d_small, 16))
    for ident in (identb, ident8, identf):
        G(lambda g, t=ident: g.memset(t[:], 0.0))
        G(lambda g, t=ident: g.affine_select(
            out=t[:], in_=t[:], compare_op=ALU.not_equal, fill=1.0, base=0,
            pattern=[[-1, 128]], channel_multiplier=1))
    G(lambda g: g.engine_nop(), sig=(g_id, 1))
    V(lambda v: v.memset(ones_col[:], 1.0))
    V(lambda v: v.memset(ones_row[:], 1.0))
    R.wait("vector", d_small, 16 * 5)
    V(lambda v: v.tensor_mul(gso[:], gnw[:], sov[:]))
    V(lambda v: v.tensor_scalar_mul(nsf[:], sf[:], -1.0), sig=(v_misc, 1))

    esX = ExitStack()
    sbX, _ = mk(esX)
    x8T3 = sbX("x8T3", [128, KC, PRE + TOK], BF16)

    esP = ExitStack()
    sbP, psP = mk(esP)
    natb = [sbP(f"natb{i_}", [128, D], F32) for i_ in range(3)]
    wmax2 = [sbP(f"wmax2{i_}", [128, 1], F32) for i_ in range(2)]
    wrec2 = [sbP(f"wrec2{i_}", [128, 1], F32) for i_ in range(2)]
    wifb = [sbP(f"wifb{i_}", [128, D], BF16) for i_ in range(6)]
    q8n = [sbP(f"q8n{i_}", [128, D], FP8) for i_ in range(8)]
    pstg = [sbP(f"pstg{i_}", [128, 2048], FP8) for i_ in range(2)]
    pp_tr = [pbank[2], pbank[3]]

    esL = ExitStack()
    sbL, psL = mk(esL)
    xint = [sbL(f"xint{i_}", [128, D], F32) for i_ in range(2)]
    xnt = [sbL("xnt0", [128, D], F32)]
    x8s = [sbL(f"x8s{i_}", [128, D], BF16) for i_ in range(2)]
    q8s = [sbL(f"q8s{i_}", [128, D], FP8) for i_ in range(2)]
    lsum = sbL("lsum", [128, 1], F32)
    wrecl = [sbL(f"wrecl{i_}", [128, 1], F32) for i_ in range(2)]
    x4 = [xint[0], xint[1], xnt[0]]
    lssq = sbL("lssq", [128, 1], F32)
    lvar = sbL("lvar", [128, 1], F32)
    lrstd = sbL("lrstd", [128, 1], F32)
    lnm = sbL("lnm", [128, 1], F32)
    lrm = sbL("lrm", [128, 1], F32)
    pb_bf = [pbank[0], pbank[1]]
    pq_f = [pbank[2], pbank[3]]

    gu_groups = [("g", gi) for gi in range(IB)] + [("u", gi) for gi in range(IB)]

    prep_c = {"n": 0}
    tpr_led = {}
    prep_scr_led = {}

    def emit_prep_quant(kind, gi, gidx):
        nat = d_in["wgat_nat"] if kind == "g" else d_in["wup_nat"]
        cols = wsg_cols if kind == "g" else wsu_cols
        for j in range(4):
            c = gi * 4 + j
            gc = prep_c["n"]
            prep_c["n"] = gc + 1
            buf = natb[gc % 3]
            dns = d_nat[gc % 3]
            wmx = wmax2[gc % 2]
            wrc = wrec2[gc % 2]
            DMA("scalar", lambda e, b=buf, n=nat, c=c: e.dma_start(
                out=b[:], in_=n[c]), sig=(dns, 16), wait=s_pq.lag(2))
            R.wait("vector", dns, dns.n)
            R.wait("vector", s_pq, s_pq.lag(1)[1])
            V(lambda v, b=buf, m=wmx: v.reduce_max(m[:], b[:], axis=AX.X,
                                                   apply_absolute_value=True))
            V(lambda v, c=c, cl=cols, m=wmx: v.tensor_copy(cl[:, c:c + 1], m[:]),
              sig=(g_pmax, 1))
            recip_nr(wrc[:], wmx[:], wn1[:])
            V(lambda v, r=wrc: v.tensor_scalar_mul(r[:], r[:], 224.0),
              sig=(v_misc, 1))
            prior_tr = tpr_led.get(gidx - 2)
            if prior_tr is not None:
                R.wait("vector", t_ptr, prior_tr)
            V(lambda v, b=buf, sl=(gidx % 2) * 4 + j, r=wrc: v.tensor_scalar(
                q8n[sl][:], b[:], r[:], None, ALU.mult), sig=(s_pq, 1))

    def emit_prep_tr(kind, gi, gidx, spq_tgt):
        sc = qwg_sc if kind == "g" else qwu_sc
        s_start = s_pev.n
        evac_led = {}
        scr_led = prep_scr_led
        for quarter in range(4):
            for kk in range(4):
                k = quarter * 4 + kk
                bank = pp_tr[k % 2]
                for j in range(4):
                    w = None
                    if j == 0:
                        if k == 0:
                            R.wait("tensor", v_qxT, v_qxT.n)
                            R.wait("tensor", s_qxT, s_qxT.n)
                            w = (s_pq.h, spq_tgt)
                        elif k == 1:
                            w = (s_pev.h, s_start)
                        else:
                            w = evac_led[k - 2]
                    T(lambda t, j=j, k=k, b=bank, sl=(gidx % 2) * 4 + j:
                      t.matmul(
                        b[:, j * 128:(j + 1) * 128],
                        q8n[sl][:, k * 128:(k + 1) * 128],
                        ident8[:], start=True, stop=True),
                      wait=w, sig=(t_ptr, 1) if j == 3 else None)
                qg = gidx * 4 + quarter
                pbuf = pstg[qg % 2]
                if kk == 0 and qg >= 2:
                    wsem, wval = scr_led[qg - 2]
                    R.wait("vector", wsem, wval)
                    R.wait("scalar", wsem, wval)
                if k % 2 == 0:
                    V(lambda v, kk=kk, b=bank, pb=pbuf: v.tensor_copy(
                        pb[:, kk * 512:(kk + 1) * 512], b[:]),
                      wait=t_ptr.cur(), sig=(v_pev, 1))
                    evac_led[k] = (v_pev.h, v_pev.n)
                else:
                    S(lambda s_, kk=kk, b=bank, pb=pbuf: s_.copy(
                        pb[:, kk * 512:(kk + 1) * 512], b[:]),
                      wait=t_ptr.cur(), sig=(s_pev, 1))
                    evac_led[k] = (s_pev.h, s_pev.n)
            R.wait("sync", s_pev, s_pev.n)
            R.wait("sync", v_pev, v_pev.n)
            DMA("sync", lambda e, sc=sc, gi=gi, q=quarter,
                pb=pstg[(gidx * 4 + quarter) % 2]: e.dma_start(
                out=sc[gi][:, q * 2048:(q + 1) * 2048], in_=pb[:]),
                sig=(d_scr2[(gidx * 4 + quarter) % 2], 16))
            scr_led[gidx * 4 + quarter] = (
                d_scr2[(gidx * 4 + quarter) % 2], d_scr2[(gidx * 4 + quarter) % 2].n)
        tpr_led[gidx] = t_ptr.n

    prep_pending = []

    def emit_prep_group(kind, gi):
        gidx = len(prep_pending)
        if gidx >= 1:
            emit_prep_tr(*prep_pending[gidx - 1])
        emit_prep_quant(kind, gi, gidx)
        prep_pending.append((kind, gi, gidx, s_pq.n))

    def flush_prep():
        for gidx in range(len(prep_pending)):
            if gidx not in tpr_led:
                emit_prep_tr(*prep_pending[gidx])

    ln_end = []
    ln_x8_led = []
    x8_bank_led = {}
    qx_bank_led = {}
    trx_end = []

    def emit_ln_pass1(j):
        buf = x4[j % 3]
        src = d_in["xpre"] if j < 8 else d_in["xown"]
        row = (j % 8) * 128
        dxs = d_x[j % 3]
        DMA("sync", lambda e, b=buf, s=src, r=row: e.dma_start(
            out=b[:], in_=s[r:r + 128, :]), sig=(dxs, 16),
            wait=ln_end[j - 3] if j >= 3 else None)
        R.wait("vector", dxs, dxs.n)
        R.wait("scalar", dxs, dxs.n)
        V(lambda v, b=buf: v.reduce_sum(lsum[:], b[:], axis=AX.X))
        V(lambda v, j=j: v.tensor_scalar_mul(lnm_c[:, j:j + 1], lsum[:],
                                             -1.0 / D), sig=(v_ln, 1))
        S(lambda s, b=buf: s.activation(b[:], b[:], AF.Square,
                                        accum_out=lssq[:]), wait=v_ln.cur())
        S(lambda s, j=j: s.copy(lsq_c[:, j:j + 1], lssq[:]),
          sig=(s_ln, 1))
        ln_end.append((s_ln.h, s_ln.n))

    def emit_ln_stats():
        R.wait("vector", v_ln, v_ln.n)
        R.wait("vector", s_ln, s_ln.n)
        V(lambda v: v.tensor_tensor(lv1_c[:], lnm_c[:], lnm_c[:], ALU.mult))
        V(lambda v: v.tensor_scalar_mul(lv1_c[:], lv1_c[:], -1.0))
        V(lambda v: v.scalar_tensor_tensor(lv1_c[:], lsq_c[:], 1.0 / D,
                                           lv1_c[:], ALU.mult, ALU.add))
        V(lambda v: v.tensor_scalar_add(lv1_c[:], lv1_c[:], EPS))
        rsqrt_nr(lrs_c[:], lv1_c[:], lvi_c[:], lvi2_c[:], lv2_c[:], wtmp[:, 0:16])
        V(lambda v: v.engine_nop(), sig=(v_misc, 1))

    q8s_led = {}

    def emit_ln_pass2A(j):
        buf = x4[j % 3]
        src = d_in["xpre"] if j < 8 else d_in["xown"]
        row = (j % 8) * 128
        own = j >= 8
        dxs = d_x[j % 3]
        DMA("sync", lambda e, b=buf, s=src, r=row: e.dma_start(
            out=b[:], in_=s[r:r + 128, :]), sig=(dxs, 16),
            wait=ln_end[16 + j - 3] if j >= 3 else
            ln_end[{0: 15, 1: 13, 2: 14}[j % 3]])
        R.wait("vector", dxs, dxs.n)
        R.wait("scalar", dxs, dxs.n)
        V(lambda v, b=buf, j=j: v.tensor_scalar(
            b[:], b[:], lnm_c[:, j:j + 1], lrs_c[:, j:j + 1],
            ALU.add, ALU.mult), sig=(v_ln, 1))
        if own:
            wl = wrecl[j % 2]
            V(lambda v, b=buf: v.reduce_max(lrm[:], b[:], axis=AX.X,
                                            apply_absolute_value=True))
            V(lambda v, j=j: v.tensor_copy(rmax[:, j - 8:j - 7], lrm[:]))
            if j >= 10:
                R.wait("vector", s_ln, q8s_led[j - 2])
            recip_nr(wl[:], lrm[:], wn1[:])
            V(lambda v, w=wl: v.tensor_scalar_mul(w[:], w[:], 224.0),
              sig=(v_misc, 1))
            if j >= 10:
                R.wait("scalar", t_trx, trx_end[j - 2])
            S(lambda s, b=buf, bq=j % 2, w=wl: s.activation(
                q8s[bq][:], b[:], AF.Copy, scale=w[:]),
              wait=v_misc.cur(), sig=(s_ln, 1))
        q8s_led[j] = s_ln.n

    def emit_ln_pass2B(j):
        buf = x4[j % 3]
        tcol = j * 128
        own = j >= 8
        V(lambda v, b=buf: v.tensor_scalar(b[:], b[:], 256.0, MAGIC,
                                           ALU.mult, ALU.add),
          wait=(s_ln.h, q8s_led[j]) if own else None, sig=(v_ln, 1))
        if j >= 2:
            R.wait("scalar", t_trx, trx_end[j - 2])
        S(lambda s, b=buf, bx=j % 2: s.activation(
            x8s[bx][:], b[:], AF.Copy, scale=1.0 / 256.0, bias=-MAGIC / 256.0),
          wait=v_ln.cur(), sig=(s_ln, 1))
        sln_x8 = s_ln.n
        gb8 = j * 4
        gbq = (j - 8) * 4
        for kg in range(4):
            bank = pb_bf[kg % 2]
            g = gb8 + kg
            for dk in range(4):
                k = kg * 4 + dk
                w = None
                if dk == 0:
                    w = x8_bank_led.get(g - 2)
                    if kg == 0:
                        w = (s_ln.h, sln_x8)
                        if g >= 2:
                            bw = x8_bank_led[g - 2]
                            R.wait("tensor", bw[0], bw[1])
                T(lambda t, k=k, b=bank, bf=j % 2, dk=dk: t.matmul(
                    b[:, dk * 128:(dk + 1) * 128],
                    x8s[bf][:, k * 128:(k + 1) * 128], identb[:],
                    start=True, stop=True),
                  wait=w, sig=(t_trx, 1) if dk == 3 else None)
            if kg % 2 == 0:
                V(lambda v, kg=kg, b=bank, tc=tcol: v.tensor_copy(
                    x8T3[:, kg * 4:(kg + 1) * 4, tc:tc + 128], b[:]),
                  wait=t_trx.cur(), sig=(v_x8T, 1))
                x8_bank_led[g] = (v_x8T.h, v_x8T.n)
            else:
                S(lambda s_, kg=kg, b=bank, tc=tcol: s_.copy(
                    x8T3[:, kg * 4:(kg + 1) * 4, tc:tc + 128], b[:]),
                  wait=t_trx.cur(), sig=(s_x8T, 1))
                x8_bank_led[g] = (s_x8T.h, s_x8T.n)
        if own:
            for kg in range(4):
                qbank = pq_f[kg % 2]
                g = gbq + kg
                for dk in range(4):
                    k = kg * 4 + dk
                    w = None
                    if dk == 0:
                        w = qx_bank_led.get(g - 2)
                        if kg == 0:
                            w = (s_ln.h, q8s_led[j])
                            R.wait("tensor", v_pev, v_pev.n)
                            R.wait("tensor", s_pev, s_pev.n)
                            if g >= 2:
                                bw = qx_bank_led[g - 2]
                                R.wait("tensor", bw[0], bw[1])
                    T(lambda t, k=k, b=qbank, bq=j % 2, dk=dk: t.matmul(
                        b[:, dk * 128:(dk + 1) * 128],
                        q8s[bq][:, k * 128:(k + 1) * 128], ident8[:],
                        start=True, stop=True),
                      wait=w, sig=(t_trx, 1) if dk == 3 else None)
                if kg % 2 == 0:
                    V(lambda v, kg=kg, b=qbank, j=j: v.tensor_copy(
                        qxT3[:, kg * 4:(kg + 1) * 4, (j - 8) * 128:(j - 7) * 128],
                        b[:]), wait=t_trx.cur(), sig=(v_qxT, 1))
                    qx_bank_led[g] = (v_qxT.h, v_qxT.n)
                else:
                    S(lambda s_, kg=kg, b=qbank, j=j: s_.copy(
                        qxT3[:, kg * 4:(kg + 1) * 4, (j - 8) * 128:(j - 7) * 128],
                        b[:]), wait=t_trx.cur(), sig=(s_qxT, 1))
                    qx_bank_led[g] = (s_qxT.h, s_qxT.n)
        ln_end.append((s_ln.h, s_ln.n))
        trx_end.append(t_trx.n)
        ln_x8_led.append(((v_x8T.h, v_x8T.n), (s_x8T.h, s_x8T.n)))

    R.wait("tensor", g_id, 1)
    gu_i = 0
    for j in range(16):
        emit_ln_pass1(j)
    emit_ln_stats()
    for j in range(16):
        emit_ln_pass2A(j)
        if j >= 1:
            emit_ln_pass2B(j - 1)
    emit_ln_pass2B(15)
    esL.close()

    esA = ExitStack()
    sbA, psA = mk(esA)
    fbuf = sbA("fbuf", [128, 1024], F32)
    f2buf = sbA("f2buf", [128, 1024], F32)
    hpre = sbA("hpre", [128, PRE], F32)
    hsqb = [sbA(f"hsqb{i_}", [128, 1024], F32) for i_ in range(3)]
    Hrot = [sbA(f"Hrot{i_}", [128, TOK], F32) for i_ in range(3)]
    rstd_b = hsqb[0]
    pi_ps = [pbank[4], pbank[5]]
    pf_ps = [pbank[6], pbank[7]]
    ph2 = [pbank[0], pbank[1]]

    hsp_led = {}
    hrl_led = {}

    def emit_h2_sum(cc):
        R.wait("tensor", s_hsq, cc + 1)
        if cc == 0:
            R.wait("tensor", v_x8T, v_x8T.n)
            R.wait("tensor", s_x8T, s_x8T.n)
        for hh in range(2):
            T(lambda t, cc=cc, hh=hh: t.matmul(
                ph2[hh][0:1, :], ones_col[:],
                hsqb[cc % 3][:, hh * 512:(hh + 1) * 512],
                start=(cc == 0), stop=(cc == KC - 1), skip_group_check=True),
              sig=(t_h2, 1))

    def emit_attn_chunk(cc):
        bi, bf_ = wifb[(cc % 3) * 2], wifb[(cc % 3) * 2 + 1]
        dws = d_wif[cc % 3]
        wdma = (None if cc < 3 else t_if.lag(16))
        DMA("gpsimd", lambda e, cc=cc, b=bi: e.dma_start(
            out=b[:], in_=d_in["wi_sw"][cc]), sig=(dws, 16), wait=wdma)
        DMA("gpsimd", lambda e, cc=cc, b=bf_: e.dma_start(
            out=b[:], in_=d_in["wf_sw"][cc]), sig=(dws, 16))
        need_w = (dws, dws.n)
        for seg in range(2):
            fmm_led = None
            for wsel, wbuf, pss in ((1, bf_, pf_ps), (0, bi, pi_ps)):
                for hh in range(2):
                    colbase = seg * 1024 + hh * 512
                    bank = pss[hh]
                    for k in range(KC):
                        w = None
                        if k == 0:
                            R.wait("tensor", need_w[0], need_w[1])
                            blk = seg * 8 + hh * 4 + 3
                            if cc == 0:
                                R.wait("tensor", ln_x8_led[blk][0][0],
                                       ln_x8_led[blk][0][1])
                            R.wait("tensor", ln_x8_led[blk][1][0],
                                       ln_x8_led[blk][1][1])
                            w = v_att.cur() if wsel == 0 else s_att.cur()
                        T(lambda t, k=k, b=bank, wb=wbuf, cb=colbase: t.matmul(
                            b[:], wb[:, k * 128:(k + 1) * 128],
                            x8T3[:, k, cb:cb + 512],
                            start=(k == 0), stop=(k == KC - 1)),
                          wait=w if k == 0 else None,
                          sig=(t_if, 1) if k == KC - 1 else None)
                if wsel == 1:
                    fmm_led = (t_if.h, t_if.n)
            tif_now = t_if.n
            R.wait("scalar", t_if, fmm_led[1])
            R.wait("scalar", v_scan, v_scan.n)
            for hh in range(2):
                S(lambda s, cc=cc, hh=hh: s.activation(
                    fbuf[:, hh * 512:(hh + 1) * 512], pf_ps[hh][:], AF.Sigmoid,
                    scale=sf[:, cc:cc + 1]))
                S(lambda s, cc=cc, hh=hh: s.activation(
                    f2buf[:, hh * 512:(hh + 1) * 512], pf_ps[hh][:], AF.Sigmoid,
                    scale=nsf[:, cc:cc + 1]), sig=(s_att, 1))
            satt_now = s_att.n
            R.wait("vector", t_if, tif_now)
            R.wait("vector", s_att, satt_now)
            for hh in range(2):
                V(lambda v, cc=cc, hh=hh: v.scalar_tensor_tensor(
                    f2buf[:, hh * 512:(hh + 1) * 512], pi_ps[hh][:],
                    si[:, cc:cc + 1], f2buf[:, hh * 512:(hh + 1) * 512],
                    ALU.mult, ALU.mult), sig=(v_att, 1))
            if seg == 0:
                V(lambda v: v.tensor_tensor_scan(
                    hpre[:], fbuf[:], f2buf[:], 0.0, ALU.mult, ALU.add),
                  sig=(v_scan, 1))
            else:
                if cc >= 3:
                    R.wait("vector", d_hsp[cc % 3], hsp_led[cc - 3][1])
                V(lambda v, cc=cc: v.tensor_tensor_scan(
                    Hrot[cc % 3][:], fbuf[:], f2buf[:], hpre[:, PRE - 1:PRE],
                    ALU.mult, ALU.add), sig=(v_scan, 1))
        R.wait("scalar", t_h2, t_h2.n)
        S(lambda s, cc=cc: s.activation(hsqb[cc % 3][:], Hrot[cc % 3][:],
                                        AF.Square),
          wait=v_scan.cur(), sig=(s_hsq, 1))
        dhs = d_hsp[cc % 3]
        DMA("sync", lambda e, cc=cc: e.dma_start(
            out=h_sc[cc], in_=Hrot[cc % 3][:]), sig=(dhs, 16),
            wait=s_hsq.cur())
        hsp_led[cc] = (dhs.h, dhs.n)
        DMA("gpsimd", lambda e, cc=cc: e.dma_start(
            out=woT_bf[cc], in_=d_in["woT_sw"][cc]), sig=(d_wc, 16))
        if cc >= 2:
            emit_h2_sum(cc - 2)

    for cc in range(KC):
        emit_attn_chunk(cc)
        if gu_i < len(gu_groups):
            emit_prep_group(*gu_groups[gu_i])
            gu_i += 1
        if cc < 6 and gu_i < len(gu_groups):
            emit_prep_group(*gu_groups[gu_i])
            gu_i += 1
    emit_h2_sum(KC - 2)
    emit_h2_sum(KC - 1)
    while gu_i < len(gu_groups):
        emit_prep_group(*gu_groups[gu_i])
        gu_i += 1
    flush_prep()

    def emit_ws_row(cols, nchunks, dr_idx):
        n = nchunks
        R.wait("vector", g_pmax, g_pmax.n)
        R.wait("vector", t_ptr, t_ptr.n)
        V(lambda v, n=n, cl=cols: v.tensor_copy(wtmp[:, 0:n], cl[:, 0:n]),
          sig=(v_misc, 1))
        T(lambda t, n=n: t.matmul(pp_tr[0][0:n, 0:128], wtmp[:, 0:n],
                                  identf[:], start=True, stop=True),
          wait=v_misc.cur(), sig=(t_ptr, 1))
        R.wait("vector", d_wsr, d_wsr.n)
        V(lambda v, n=n: v.tensor_copy(wst_sb[0:n, :], pp_tr[0][0:n, 0:128]),
          wait=t_ptr.cur(), sig=(v_pev, 1))
        DMA("sync", lambda e, n=n, di=dr_idx: e.dma_start(
            out=ws_dr[di][0:n, :], in_=wst_sb[0:n, :]), sig=(d_wsr, 16),
            wait=v_pev.cur())

    emit_ws_row(wsg_cols, IC, 0)
    emit_ws_row(wsu_cols, IC, 1)

    R.wait("vector", t_h2, t_h2.n)
    V(lambda v: v.tensor_copy(ssum[0:1, 0:512], ph2[0][0:1, :]))
    V(lambda v: v.tensor_copy(ssum[0:1, 512:1024], ph2[1][0:1, :]))
    V(lambda v: v.tensor_scalar(ssum[:], ssum[:], 1.0 / D, EPS, ALU.mult,
                                ALU.add))
    rsqrt_nr(rstd_r[:], ssum[:], ir1[:], ir2[:], hpre[0:1, :], rstd_b[0:1, :])
    V(lambda v: v.engine_nop(), sig=(v_misc, 1))
    vm_rstd = v_misc.n
    for hh in range(2):
        T(lambda t, hh=hh: t.matmul(
            ph2[hh][:, :], ones_row[:], rstd_r[0:1, hh * 512:(hh + 1) * 512],
            start=True, stop=True, skip_group_check=True),
          wait=(v_misc.h, vm_rstd), sig=(t_bc, 1))
        V(lambda v, hh=hh: v.tensor_copy(
            rstd_b[:, hh * 512:(hh + 1) * 512], ph2[hh][:, :]),
          wait=t_bc.cur(), sig=(v_rb, 1))

    sgbuf = fbuf
    g2buf = f2buf
    o1buf = hpre
    R.wait("scalar", v_scan, v_scan.n)

    def emit_epi_chunk(cc):
        b = wifb[cc % 2]
        dgs = d_wg[cc % 2]
        wdma = (t_if.cur() if cc < 2 else t_g.lag(2))
        DMA("gpsimd", lambda e, cc=cc, b=b: e.dma_start(
            out=b[:], in_=d_in["wg_sw"][cc]), sig=(dgs, 16), wait=wdma)
        need = (dgs, dgs.n)
        hb = Hrot[cc % 3]
        dhr = d_hrl[cc % 3]
        last_spill = {0: 15, 1: 13, 2: 14}[cc % 3]
        DMA("sync", lambda e, cc=cc, h=hb: e.dma_start(
            out=h[:], in_=h_sc[cc]), sig=(dhr, 16),
            wait=(hsp_led[last_spill] if cc < 3 else v_epi.lag(2)))
        hrl_led[cc] = (dhr.h, dhr.n)
        bankset = pi_ps if cc % 2 == 0 else pf_ps
        for hh in range(2):
            bank = bankset[hh]
            for k in range(KC):
                w = None
                if k == 0:
                    R.wait("tensor", need[0], need[1])
                    w = (v_epi.lag(1) if cc >= 2 else
                         ((v_att.h, v_att.n) if cc == 0 else (s_att.h, s_att.n)))
                T(lambda t, k=k, b2=bank, wb=b, hh=hh: t.matmul(
                    b2[:], wb[:, k * 128:(k + 1) * 128],
                    x8T3[:, k, 1024 + hh * 512:1024 + (hh + 1) * 512],
                    start=(k == 0), stop=(k == KC - 1)),
                  wait=w if k == 0 else None,
                  sig=(t_g, 1) if k == KC - 1 else None)
        tg_now = t_g.n
        R.wait("scalar", t_g, tg_now)
        R.wait("scalar", v_epi, v_epi.n)
        for hh in range(2):
            S(lambda s, cc=cc, hh=hh, bk=bankset[hh]: s.activation(
                sgbuf[:, hh * 512:(hh + 1) * 512], bk[:], AF.Sigmoid,
                scale=sgv[:, cc:cc + 1]), sig=(s_epi, 1))
        sep_now = s_epi.n
        R.wait("vector", t_g, tg_now)
        R.wait("vector", s_epi, sep_now)
        R.wait("vector", d_hrl[cc % 3], hrl_led[cc][1])
        for hh in range(2):
            V(lambda v, cc=cc, hh=hh, bk=bankset[hh]: v.tensor_scalar_mul(
                g2buf[:, hh * 512:(hh + 1) * 512], bk[:],
                sgv[:, cc:cc + 1]))
        V(lambda v: v.tensor_tensor(g2buf[:], g2buf[:], sgbuf[:], ALU.mult))
        V(lambda v, cc=cc, h=hb: v.scalar_tensor_tensor(
            o1buf[:], h[:], gso[:, cc:cc + 1], rstd_b[:], ALU.mult, ALU.mult),
          wait=(v_rb.h, 2) if cc == 0 else None)
        V(lambda v, cc=cc: v.tensor_tensor(
            x8T3[:, cc, 0:1024], o1buf[:], g2buf[:], ALU.mult), sig=(v_epi, 1))

    for cc in range(KC):
        emit_epi_chunk(cc)
    esA.close()
    esP.close()

    esB = ExitStack()
    sbB, psB = mk(esB)
    woq2 = [[sbB(f"woq_{p_}_{k}", [128, 512], BF16) for k in range(KC)]
            for p_ in range(2)]
    sst = [sbB(f"sst{i_}", [128, 512], F32) for i_ in range(2)]
    smg = sbB("smg", [128, 512], F32)
    po_ps = [pbank[0], pbank[1]]
    natd = [sbB("natd0", [128, D], F32), sbB("natd1", [128, D], F32),
            sbB("natd2", [128, 1536], F32)]
    q8d = [sbB(f"q8d{i_}", [128, I], FP8) for i_ in range(4)]
    pstgd = sbB("pstgd", [128, IC * 512], FP8)
    ppd = [pbank[2], pbank[3]]

    dq_spq = {}
    dtr_led = {}
    dsd_led = {}

    def emit_down_quant(gi):
        widths = [(0, 2048), (2048, 2048), (4096, 1536)]
        if gi >= 1:
            R.wait("vector", t_ptr, dtr_led[gi - 1])
        for j in range(4):
            c = gi * 4 + j
            led = []
            for pi_, (off, w) in enumerate(widths):
                buf = natd[pi_]
                dnd = d_natd[pi_ % 2] if pi_ < 2 else d_natd2
                if gi == 0 and j == 0:
                    wdma = v_epi.cur() if pi_ == 0 else None
                else:
                    wdma = s_pq.lag(2 - pi_) if pi_ < 2 else s_pq.cur()
                DMA("scalar", lambda e, b=buf, c=c, o=off, w=w: e.dma_start(
                    out=b[:, 0:w], in_=d_in["wdn_nat"][c][:, o:o + w]),
                    sig=(dnd, 16), wait=wdma)
                led.append((dnd.h, dnd.n))
            for pi_, (off, w) in enumerate(widths):
                buf = natd[pi_]
                R.wait("vector", led[pi_][0], led[pi_][1])
                if pi_ == 0:
                    R.wait("vector", s_pq, s_pq.n)
                    V(lambda v, b=buf, w=w: v.reduce_max(
                        wmaxc[:], b[:, 0:w], axis=AX.X,
                        apply_absolute_value=True), sig=(g_pmax, 1))
                else:
                    V(lambda v, b=buf, w=w: v.reduce_max(
                        wn2[:], b[:, 0:w], axis=AX.X,
                        apply_absolute_value=True))
                    V(lambda v: v.tensor_max(wmaxc[:], wmaxc[:], wn2[:]),
                      sig=(g_pmax, 1))
            V(lambda v, c=c: v.tensor_copy(wsd_cols[:, c:c + 1], wmaxc[:]),
              sig=(g_pmax, 1))
            recip_nr(wrec[:], wmaxc[:], wn1[:])
            V(lambda v: v.tensor_scalar_mul(wrec[:], wrec[:], 224.0),
              sig=(v_misc, 1))
            for pi_, (off, w) in enumerate(widths):
                buf = natd[pi_]
                V(lambda v, b=buf, j=j, o=off, w=w: v.tensor_scalar(
                    q8d[j][:, o:o + w], b[:, 0:w], wrec[:], None, ALU.mult),
                  sig=(s_pq, 1))
        dq_spq[gi] = s_pq.n

    def emit_down_tr(gi):
        for icc in range(IC):
            bank = ppd[icc % 2]
            for j in range(4):
                w = None
                if j == 0:
                    w = (s_pq.h, dq_spq[gi]) if icc == 0 else v_pev.lag(1)
                T(lambda t, j=j, icc=icc, b=bank: t.matmul(
                    b[:, j * 128:(j + 1) * 128],
                    q8d[j][:, icc * 128:(icc + 1) * 128], ident8[:],
                    start=True, stop=True),
                  wait=w, sig=(t_ptr, 1) if j == 3 else None)
            if icc % 11 == 0 and gi >= 1:
                wsem, wval = dsd_led[(gi - 1, icc // 11)]
                R.wait("vector", wsem, wval)
            V(lambda v, icc=icc, b=bank: v.tensor_copy(
                pstgd[:, icc * 512:(icc + 1) * 512], b[:]),
              wait=t_ptr.cur(), sig=(v_pev, 1))
            if icc % 11 == 10:
                p = icc // 11
                DMA("sync", lambda e, gi=gi, p=p: e.dma_start(
                    out=qwd_sc[gi][:, p * 5632:(p + 1) * 5632],
                    in_=pstgd[:, p * 5632:(p + 1) * 5632]),
                    sig=(d_scrd4[p], 16), wait=v_pev.cur())
                dsd_led[(gi, p)] = (d_scrd4[p], d_scrd4[p].n)
        dtr_led[gi] = t_ptr.n

    for _ds in d_scr2:
        R.wait("sync", _ds, _ds.n)
        R.wait("scalar", _ds, _ds.n)
        R.wait("vector", _ds, _ds.n)
        R.wait("gpsimd", _ds, _ds.n)
    R.wait("sync", d_wsr, d_wsr.n)
    for _ds in d_scrd4:
        R.wait("sync", _ds, _ds.n)
    R.wait("sync", v_epi, KC)
    R.wait("gpsimd", d_wsr, d_wsr.n)
    R.wait("gpsimd", v_epi, KC)
    oproj_sst_led = {}
    oproj_top_led = {}

    def emit_oproj_od(od):
        wq = woq2[od % 2]
        dwo = d_wo2[od % 2]
        for k in range(KC):
            wdma = ((v_epi.h, KC) if od < 2
                    else (t_op.h, oproj_top_led[od - 2]))
            if od < 2 and k == 0:
                R.wait("sync", d_wc, 16 * KC)
            DMA("sync", lambda e, k=k, od=od, wq=wq: e.dma_start(
                out=wq[k][:],
                in_=woT_bf[k][:, od * 512:(od + 1) * 512]),
                sig=(dwo, 16), wait=wdma)
        need = (dwo.h, dwo.n)
        sst_led = oproj_sst_led
        for tj in range(8):
            bank = po_ps[tj % 2]
            for cc in range(KC):
                w = None
                if cc == 0:
                    R.wait("tensor", need[0], need[1])
                    if od == 0 and tj == 0:
                        R.wait("tensor", v_epi, v_epi.n)
                        R.wait("tensor", v_rb, 2)
                    w = v_sev.lag(1)
                T(lambda t, cc=cc, b=bank, tj=tj: t.matmul(
                    b[:], x8T3[:, cc, tj * 128:(tj + 1) * 128], wq[cc][:],
                    start=(cc == 0), stop=(cc == KC - 1)),
                  wait=w if cc == 0 else None,
                  sig=(t_op, 1) if cc == KC - 1 else None)
            top_now = t_op.n
            R.wait("vector", t_op, top_now)
            gsst = od * 8 + tj
            if gsst >= 2:
                R.wait("vector", d_so[gsst % 2], sst_led[gsst - 2][1])
            V(lambda v, b=bank: v.tensor_scalar(
                smg[:], b[:], 256.0, MAGIC, ALU.mult, ALU.add))
            V(lambda v, tj=tj: v.tensor_scalar(
                sst[tj % 2][:], smg[:], -MAGIC, 1.0 / 256.0, ALU.add, ALU.mult),
              sig=(v_sev, 1))
            dso = d_so[gsst % 2]
            DMA("gpsimd", lambda e, tj=tj, od=od: e.dma_start(
                out=s_sc[tj * 128:(tj + 1) * 128, od * 512:(od + 1) * 512],
                in_=sst[tj % 2][:]), sig=(dso, 16), wait=v_sev.cur())
            sst_led[gsst] = (dso.h, dso.n)
        oproj_top_led[od] = t_op.n

    for od in range(4):
        emit_oproj_od(od)
        if od >= 1:
            emit_down_tr(od - 1)
        emit_down_quant(od)
    emit_down_tr(3)
    emit_ws_row(wsd_cols, KC, 2)

    esB.close()
    esX.close()

    esC = ExitStack()
    sbC, psC = mk(esC)
    Tt = [sbC(f"Tt{i_}", [128, I], F32) for i_ in range(4)]
    qt8 = sbC("qt8", [128, I], FP8)
    qtT = [sbC(f"qtT{i_}", [128, I], FP8) for i_ in range(4)]
    wgq = [sbC(f"wgq{i_}", [128, 2048], FP8) for i_ in range(3)]
    wuq = [sbC(f"wuq{i_}", [128, 2048], FP8) for i_ in range(3)]
    wdq = [sbC(f"wdq{i_}", [128, 2048], FP8) for i_ in range(4)]
    wsb = [sbC(f"wsb{i_}", [128, 512], F32) for i_ in range(2)]
    ayb = sbC("ayb", [128, 512], F32)
    y2b = sbC("y2b", [128, 512], F32)
    sgm8 = sbC("sgm8", [128, 512], F32)
    xrl = [sbC(f"xrl{i_}", [128, 512], F32) for i_ in range(2)]
    srl = [sbC(f"srl{i_}", [128, 512], F32) for i_ in range(2)]
    outb = [sbC(f"outb{i_}", [128, 512], F32) for i_ in range(2)]
    tm1 = sbC("tm1", [128, 1], F32)
    tm2 = sbC("tm2", [128, 1], F32)
    pgu = pbank

    for _ds in d_scr2 + d_scrd4:
        R.wait("sync", _ds, _ds.n)
    R.wait("sync", d_wsr, d_wsr.n)
    R.wait("sync", d_so[0], d_so[0].n)
    R.wait("sync", d_so[1], d_so[1].n)
    R.wait("sync", t_op, t_op.n)
    R.wait("sync", v_sev, v_sev.n)
    R.wait("vector", d_so[0], d_so[0].n)
    R.wait("vector", d_so[1], d_so[1].n)
    R.wait("vector", d_wsr, d_wsr.n)
    for _ds in d_scrd4:
        R.wait("vector", _ds, _ds.n)
    R.wait("vector", t_op, t_op.n)
    R.wait("vector", v_pev, v_pev.n)
    R.wait("vector", s_pev, s_pev.n)
    R.wait("scalar", t_op, t_op.n)
    R.wait("scalar", v_pev, v_pev.n)
    R.wait("tensor", v_sev, v_sev.n)
    R.wait("tensor", v_pev, v_pev.n)
    R.wait("tensor", s_pev, s_pev.n)
    R.wait("tensor", v_qxT, v_qxT.n)
    R.wait("tensor", s_qxT, s_qxT.n)
    for _ds in d_scrd4:
        R.wait("gpsimd", _ds, _ds.n)

    V(lambda v: v.tensor_scalar_mul(sgf[:], rmax[:], FQ))
    V(lambda v: v.tensor_tensor(sg2[:], sgf[:], sgf[:], ALU.mult),
      sig=(v_misc, 1))
    vm_sgf = v_misc.n

    bc_state = {"n": 0}
    bc_use_led = {}

    def bcast_ws(dr_idx, base):
        n = bc_state["n"]
        bc_state["n"] = n + 1
        p = n % 2
        drs = d_row[p]
        DMA("sync", lambda e, di=dr_idx, b=base, p=p: e.dma_start(
            out=wsb[p][:],
            in_=ws_dr[di:di + 1, :, :].rearrange("a b c -> a (b c)")
                [0:1, b:b + 512].partition_broadcast(128)),
            sig=(drs, 16), wait=bc_use_led.get(n - 2))
        return p, (drs.h, drs.n)

    def bc_mark_used(n_offset=0):
        bc_use_led[bc_state["n"] - 1] = (ch_v.h, ch_v.n)

    mm_led = {}
    dq_led = {}
    dn_led = {}
    dqd_led = {}
    y_led = {}
    ag_led = {}
    up_led = {}
    v_ag = sem("v_ag")

    def load_quarter(wsel, gq):
        sc = qwg_sc if wsel == 0 else qwu_sc
        ring = wgq if wsel == 0 else wuq
        ic_ = (gq % (IB * 4)) // 4
        q = gq % 4
        wq = mm_led.get((wsel, gq - 3))
        dqs = d_q8[wsel][gq % 3]
        DMA("sync", lambda e, sc=sc, ic_=ic_, q=q, b=ring[gq % 3]: e.dma_start(
            out=b[:], in_=sc[ic_][:, q * 2048:(q + 1) * 2048]),
            sig=(dqs, 16), wait=wq)
        dq_led[(wsel, gq)] = (dqs.h, dqs.n)

    for wsel in range(2):
        for gq0 in range(3):
            load_quarter(wsel, gq0)

    GB, UB = pgu[0:4], pgu[4:8]
    for TS in range(2):
        R.wait("vector", s_qt, s_qt.n)
        R.wait("vector", t_dn, t_dn.n)
        for ic in range(IB):
            bc_g = bcast_ws(0, ic * 512)
            bc_u = bcast_ws(1, ic * 512)
            for wsel, bufs, sem4, banks in ((0, wgq, t_g4, GB),
                                            (1, wuq, t_u4, UB)):
                glb = TS * IB + ic
                guard = (ag_led if wsel == 0 else up_led).get(glb - 1)
                if guard is not None:
                    R.wait("tensor", guard[0], guard[1])
                for k in range(KC):
                    gq = TS * IB * 4 + ic * 4 + k // 4
                    for tj in range(4):
                        if k % 4 == 0 and tj == 0:
                            R.wait("tensor", dq_led[(wsel, gq)][0],
                                   dq_led[(wsel, gq)][1])
                        last_q = (k % 4 == 3 and tj == 3)
                        T(lambda t, k=k, tj=tj, bu=bufs, sl=gq % 3, TS=TS,
                          bks=banks: t.matmul(
                            bks[tj][:],
                            qxT3[:, k, TS * 512 + tj * 128:TS * 512 + (tj + 1) * 128],
                            bu[sl][:, (k % 4) * 512:(k % 4 + 1) * 512],
                            start=(k == 0), stop=(k == KC - 1)),
                          sig=(sem4, 1) if last_q else None)
                        if last_q:
                            mm_led[(wsel, gq)] = (sem4.h, sem4.n)
                            if gq + 3 < 2 * IB * 4:
                                load_quarter(wsel, gq + 3)
            glb = TS * IB + ic
            R.wait("vector", t_g4, t_g4.n)
            R.wait("vector", bc_g[1][0], bc_g[1][1])
            for tj in range(4):
                V(lambda v, tj=tj, ic=ic, p=bc_g[0]: v.tensor_tensor(
                    Tt[tj][:, ic * 512:(ic + 1) * 512], GB[tj][:],
                    wsb[p][:], ALU.mult),
                  wait=(v_misc.h, vm_sgf) if (TS == 0 and ic == 0
                                              and tj == 0) else None,
                  sig=(v_ag, 1))
            ag_led[glb] = (v_ag.h, v_ag.n)
            bc_use_led[bc_state["n"] - 2] = (v_ag.h, v_ag.n)
            R.wait("scalar", t_u4, t_u4.n)
            R.wait("scalar", v_ffn, v_ffn.n)
            R.wait("vector", t_u4, t_u4.n)
            R.wait("vector", bc_u[1][0], bc_u[1][1])
            for tj in range(4):
                S(lambda s, tj=tj, ic=ic, TS=TS: s.activation(
                    sgm8[:], Tt[tj][:, ic * 512:(ic + 1) * 512],
                    AF.Sigmoid,
                    scale=sgf[:, TS * 4 + tj:TS * 4 + tj + 1]),
                  wait=v_ag.cur() if tj == 0 else v_ffn.cur(), sig=(s_ffn, 1))
                V(lambda v, tj=tj, p=bc_u[0]: v.tensor_tensor(
                    ayb[:], UB[tj][:], wsb[p][:], ALU.mult))
                V(lambda v, tj=tj, ic=ic, TS=TS: v.scalar_tensor_tensor(
                    y2b[:], Tt[tj][:, ic * 512:(ic + 1) * 512],
                    sg2[:, TS * 4 + tj:TS * 4 + tj + 1], ayb[:],
                    ALU.mult, ALU.mult))
                V(lambda v, tj=tj, ic=ic: v.tensor_tensor(
                    Tt[tj][:, ic * 512:(ic + 1) * 512], y2b[:], sgm8[:],
                    ALU.mult), wait=s_ffn.cur(), sig=(v_ffn, 1))
            up_led[glb] = (v_ffn.h, v_ffn.n)
            bc_use_led[bc_state["n"] - 1] = (v_ffn.h, v_ffn.n)
        for tj in range(4):
            V(lambda v, tj=tj: v.reduce_max(tm1[:], Tt[tj][:], axis=AX.X,
                                            apply_absolute_value=True))
            V(lambda v, tj=tj, TS=TS: v.tensor_scalar_mul(
                tyf[:, TS * 4 + tj:TS * 4 + tj + 1], tm1[:], FQ))
            R.wait("vector", s_qt, s_qt.n)
            recip_nr(tm2[:], tm1[:], wn1[:])
            V(lambda v: v.tensor_scalar_mul(tm2[:], tm2[:], 224.0),
              sig=(v_tmax, 1))
            R.wait("scalar", t_trt, t_trt.n)
            S(lambda s, tj=tj: s.activation(qt8[:], Tt[tj][:], AF.Copy,
                                            scale=tm2[:]),
              wait=v_tmax.cur(), sig=(s_qt, 1))
            sqt_now = s_qt.n
            if TS == 0 and tj == 0:
                R.wait("tensor", v_ffn, v_ffn.n)
            for icc in range(IC):
                bank = pgu[icc % 2]
                w = (s_qt.h, sqt_now) if icc == 0 else (
                    v_qtT.lag(1) if icc >= 2 else None)
                T(lambda t, icc=icc, b=bank: t.matmul(
                    b[:, 0:128], qt8[:, icc * 128:(icc + 1) * 128], ident8[:],
                    start=True, stop=True), wait=w, sig=(t_trt, 1))
                V(lambda v, icc=icc, b=bank, tj=tj: v.tensor_copy(
                    qtT[tj][:, icc * 128:(icc + 1) * 128], b[:, 0:128]),
                  wait=t_trt.cur(), sig=(v_qtT, 1))
        for hc in range(4):
            bc_d = bcast_ws(2, hc * 512)
            for ib_ in range(IB):
                gb = (TS * 4 + hc) * IB + ib_
                dqd = d_q8d[gb % 4]
                DMA("sync", lambda e, ib_=ib_, hc=hc, b=wdq[gb % 4]:
                    e.dma_start(out=b[:],
                                in_=qwd_sc[hc][:, ib_ * 2048:(ib_ + 1) * 2048]),
                    sig=(dqd, 16), wait=dn_led.get(gb - 4))
                dqd_led[gb] = (dqd.h, dqd.n)
                R.wait("tensor", dqd_led[gb][0], dqd_led[gb][1])
                if ib_ == 0:
                    R.wait("tensor", v_qtT, v_qtT.n)
                    R.wait("tensor", v_fin, v_fin.n)
                for sl4 in range(4):
                    icc = ib_ * 4 + sl4
                    for tj in range(4):
                        T(lambda t, icc=icc, tj=tj, gb=gb, sl4=sl4: t.matmul(
                            UB[tj][:], qtT[tj][:, icc * 128:(icc + 1) * 128],
                            wdq[gb % 4][:, sl4 * 512:(sl4 + 1) * 512],
                            start=(icc == 0), stop=(icc == IC - 1)),
                          sig=(t_dn, 1))
                dn_led[gb] = (t_dn.h, t_dn.n)
            tdn_now = t_dn.n
            for tj in range(4):
                rowb = TS * 512 + tj * 128
                gfin = (TS * 4 + hc) * 4 + tj
                dsxs = d_sx[tj % 2]
                dsss = d_ss[tj % 2]
                DMA("sync", lambda e, rowb=rowb, hc=hc, b=xrl[tj % 2]:
                    e.dma_start(out=b[:],
                                in_=d_in["xown"][rowb:rowb + 128,
                                                 hc * 512:(hc + 1) * 512]),
                    sig=(dsxs, 16), wait=v_fin.lag(1))
                DMA("sync", lambda e, rowb=rowb, hc=hc, b=srl[tj % 2]:
                    e.dma_start(out=b[:],
                                in_=s_sc[rowb:rowb + 128,
                                         hc * 512:(hc + 1) * 512]),
                    sig=(dsss, 16), wait=v_fin.lag(1))
                dss_now = (dsss, dsss.n)
                dsx_now = (dsxs, dsxs.n)
                R.wait("vector", t_dn, tdn_now)
                R.wait("vector", bc_d[1][0], bc_d[1][1])
                V(lambda v, tj=tj, p=bc_d[0]: v.tensor_tensor(
                    ayb[:], UB[tj][:], wsb[p][:], ALU.mult))
                S(lambda s, tj=tj, TS=TS: s.activation(
                    sgm8[:], ayb[:], AF.Sigmoid,
                    scale=tyf[:, TS * 4 + tj:TS * 4 + tj + 1]),
                  wait=(ch_v.h, ch_v.n), sig=(s_ffn, 1))
                V(lambda v, tj=tj, TS=TS: v.tensor_scalar_mul(
                    y2b[:], ayb[:], tyf[:, TS * 4 + tj:TS * 4 + tj + 1]))
                V(lambda v: v.tensor_tensor(y2b[:], y2b[:], sgm8[:], ALU.mult),
                  wait=s_ffn.cur())
                R.wait("vector", dss_now[0], dss_now[1])
                R.wait("vector", dsx_now[0], dsx_now[1])
                V(lambda v, tj=tj: v.tensor_tensor(y2b[:], y2b[:],
                                                   srl[tj % 2][:], ALU.mult))
                V(lambda v, tj=tj: v.tensor_tensor(outb[tj % 2][:], y2b[:],
                                                   xrl[tj % 2][:], ALU.add),
                  wait=y_led.get(gfin - 2), sig=(v_fin, 1))
                dys = d_y[gfin % 2]
                DMA("gpsimd", lambda e, rowb=rowb, hc=hc, b=outb[tj % 2]:
                    e.dma_start(out=y_out[rowb:rowb + 128,
                                          hc * 512:(hc + 1) * 512], in_=b[:]),
                    sig=(dys, 16), wait=v_fin.cur())
                y_led[gfin] = (dys.h, dys.n)
            bc_use_led[bc_state["n"] - 1] = (ch_v.h, ch_v.n)

    R.wait("gpsimd", d_y[0], d_y[0].n)
    R.wait("gpsimd", d_y[1], d_y[1].n)
    R.wait("sync", d_y[0], d_y[0].n)
    R.wait("sync", d_y[1], d_y[1].n)

    with nc.Block() as block:
        @block.sync
        def _(e):
            R.replay("sync", e)

        @block.vector
        def _(e):
            R.replay("vector", e)

        @block.scalar
        def _(e):
            R.replay("scalar", e)

        @block.gpsimd
        def _(e):
            R.replay("gpsimd", e)

        @block.tensor
        def _(e):
            R.replay("tensor", e)

    esC.close()
    es0.close()
    return nc


def _host_prepare(ins):
    x = np.ascontiguousarray(ins["L0_attn_hidden_states"], dtype=np.float32)
    w_i, w_f, w_g, w_o = (np.asarray(ins[f"L0_attn_w_{k}"], dtype=np.float32)
                          for k in "ifgo")
    down_idx = np.asarray(ins["L0_ffn_down_reorder_index"]).astype(np.int64)

    def strip_sw(w):
        t = w.reshape(KC, 128, KC, 128)
        t = np.transpose(t, (2, 1, 0, 3))
        return np.ascontiguousarray(t.reshape(KC, 128, D), dtype=np.float32)

    def vec_img(v):
        return np.ascontiguousarray(
            np.asarray(v, dtype=np.float32).reshape(KC, 128).T)

    shared = {
        "wi_sw": strip_sw(w_i), "wf_sw": strip_sw(w_f), "wg_sw": strip_sw(w_g),
        "woT_sw": np.ascontiguousarray(w_o.T.reshape(KC, 128, D),
                                       dtype=np.float32),
        "si_img": vec_img(ins["L0_attn_s_i"]),
        "sf_img": vec_img(ins["L0_attn_s_f"]),
        "sg_img": vec_img(ins["L0_attn_s_g"]),
        "so_img": vec_img(ins["L0_attn_s_o"]),
        "gnw_img": vec_img(ins["L0_attn_g_norm_w"]),
        "wgat_nat": np.ascontiguousarray(
            np.asarray(ins["L0_ffn_w_gate"], dtype=np.float32)[down_idx]
            .reshape(IC, 128, D)),
        "wup_nat": np.ascontiguousarray(
            np.asarray(ins["L0_ffn_w_up"], dtype=np.float32)[down_idx]
            .reshape(IC, 128, D)),
        "wdn_nat": np.ascontiguousarray(
            np.asarray(ins["L0_ffn_w_down"], dtype=np.float32)
            .reshape(KC, 128, I)),
    }
    zeros_pre = np.zeros((PRE, D), dtype=np.float32)
    in_maps = []
    for core in range(8):
        b, half = core // 2, core % 2
        m = dict(shared)
        m["xown"] = np.ascontiguousarray(x[b, half * TOK:(half + 1) * TOK, :])
        m["xpre"] = (np.ascontiguousarray(x[b, 0:PRE, :]) if half == 1
                     else zeros_pre)
        in_maps.append(m)
    return in_maps


def kernel(**inputs):
    in_maps = _host_prepare(inputs)
    nc = build_program()
    res = run_bass_kernel_spmd(nc, in_maps, core_ids=list(range(8)))
    out = np.empty((4, 2048, D), dtype=np.float32)
    for core in range(8):
        b, half = core // 2, core % 2
        out[b, half * TOK:(half + 1) * TOK, :] = np.asarray(res.results[core]["y"])
    return out


# version 19
# speedup vs baseline: 1.0228x; 1.0228x over previous
import numpy as np
import concourse.bass as bass
import concourse.mybir as mybir
from concourse.bass_utils import run_bass_kernel_spmd

F32 = mybir.dt.float32
BF16 = mybir.dt.bfloat16
FP8 = mybir.dt.float8e4
I32 = mybir.dt.int32
AF = mybir.ActivationFunctionType
ALU = mybir.AluOpType
AX = mybir.AxisListType

D = 2048
KC = D // 128
TOK = 1024
PRE = 1024
I = 5632
IC = I // 128
IB = I // 512
MAGIC = float(np.float32(1.5 * 2 ** 23))
EPS = 1e-5
RSQRT_C = 0x5F3759DF
FQ = 4.0 / (448.0 * 448.0)


class SemC:
    def __init__(self, h):
        self.h = h
        self.n = 0

    def inc(self, k=1):
        self.n += k
        return (self.h, self.n)

    def cur(self):
        return (self.h, self.n)

    def lag(self, k):
        return (self.h, max(0, self.n - k))


class Rec:
    CHAINED = ("vector", "scalar", "gpsimd")

    def __init__(self):
        self.st = {e: [] for e in ("sync", "vector", "scalar", "gpsimd", "tensor")}
        self.last = {e: None for e in self.st}

    def emit(self, eng, fn, wait=None, sig=None, chain_sem=None, nochain=False):
        if eng in self.CHAINED and wait is not None:
            if wait[1] > 0:
                self.st[eng].append(("W", wait[0], wait[1]))
            wait = None
        last = self.last[eng] if (eng in self.CHAINED and not nochain) else None
        if sig is not None:
            sem, amount = sig
            tgt = sem.inc(amount)
            upd = (tgt[0], amount)
        elif eng in self.CHAINED and chain_sem is not None:
            tgt = chain_sem.inc(1)
            upd = (tgt[0], 1)
        else:
            tgt, upd = None, None
        self.st[eng].append(("I", fn, wait, last, upd))
        if eng in self.CHAINED and tgt is not None and not nochain:
            self.last[eng] = tgt
        return tgt

    def wait(self, eng, sem, val=None):
        v = sem.n if val is None else val
        h = sem.h if hasattr(sem, "h") else sem
        if v > 0:
            self.st[eng].append(("W", h, v))

    def replay(self, eng, engine):
        for item in self.st[eng]:
            if item[0] == "W":
                engine.wait_ge(item[1], item[2])
                continue
            _, fn, wait, last, upd = item
            ins = fn(engine)
            if wait is not None and wait[1] > 0:
                ins._wait_ge(wait[0], wait[1])
            elif last is not None:
                ins._wait_ge(last[0], last[1])
            if upd is not None:
                ins.then_inc(upd[0], upd[1])


def build_program():
    nc = bass.Bass("TRN2", target_bir_lowering=False, debug=False)

    d_in = {}
    for name, shape in [
        ("xpre", [PRE, D]), ("xown", [TOK, D]),
        ("wi_sw", [KC, 128, D]), ("wf_sw", [KC, 128, D]),
        ("wg_sw", [KC, 128, D]), ("woT_sw", [KC, 128, D]),
        ("si_img", [128, KC]), ("sf_img", [128, KC]), ("sg_img", [128, KC]),
        ("so_img", [128, KC]), ("gnw_img", [128, KC]),
        ("wgat_nat", [IC, 128, D]), ("wup_nat", [IC, 128, D]),
        ("wdn_nat", [KC, 128, I]),
    ]:
        d_in[name] = nc.dram_tensor(name, shape, F32, kind="ExternalInput").ap()
    y_out = nc.dram_tensor("y", [TOK, D], F32, kind="ExternalOutput").ap()

    qwg_sc = nc.dram_tensor("qwg_sc", [IB, 128, KC * 512], FP8, kind="Internal").ap()
    qwu_sc = nc.dram_tensor("qwu_sc", [IB, 128, KC * 512], FP8, kind="Internal").ap()
    qwd_sc = nc.dram_tensor("qwd_sc", [4, 128, IC * 512], FP8, kind="Internal").ap()
    woT_bf = nc.dram_tensor("woT_bf", [KC, 128, D], BF16, kind="Internal").ap()
    s_sc = nc.dram_tensor("s_sc", [TOK, D], F32, kind="Internal").ap()
    h_sc = nc.dram_tensor("h_sc", [KC, 128, TOK], F32, kind="Internal").ap()
    ws_dr = nc.dram_tensor("ws_dr", [3, IC, 128], F32, kind="Internal").ap()

    from contextlib import ExitStack
    es0 = ExitStack()

    def mk(stack):
        def sb(name, shape, dtype):
            return stack.enter_context(nc.sbuf_tensor(name, shape, dtype))

        def ps(name, shape, dtype=F32):
            return stack.enter_context(nc.psum_tensor(name, shape, dtype))
        return sb, ps

    sb0, ps0 = mk(es0)

    def sem(name):
        return SemC(es0.enter_context(nc.semaphore(name)))

    si = sb0("si", [128, KC], F32)
    sf = sb0("sf", [128, KC], F32)
    nsf = sb0("nsf", [128, KC], F32)
    sgv = sb0("sgv", [128, KC], F32)
    sov = sb0("sov", [128, KC], F32)
    gnw = sb0("gnw", [128, KC], F32)
    gso = sb0("gso", [128, KC], F32)
    rmax = sb0("rmax", [128, 8], F32)
    sgf = sb0("sgf", [128, 8], F32)
    sg2 = sb0("sg2", [128, 8], F32)
    tyf = sb0("tyf", [128, 8], F32)
    identb = sb0("identb", [128, 128], BF16)
    ident8 = sb0("ident8", [128, 128], FP8)
    identf = sb0("identf", [128, 128], F32)
    ones_col = sb0("ones_col", [128, 1], F32)
    ones_row = sb0("ones_row", [1, 128], F32)
    row_t = sb0("row_t", [1, 512], F32)
    wsg_cols = sb0("wsg_cols", [128, IC], F32)
    wsu_cols = sb0("wsu_cols", [128, IC], F32)
    wsd_cols = sb0("wsd_cols", [128, KC], F32)
    qxT3 = sb0("qxT3", [128, KC, TOK], FP8)
    ssum = sb0("ssum", [1, TOK], F32)
    rstd_r = sb0("rstd_r", [1, TOK], F32)
    ir1 = sb0("ir1", [1, TOK], I32)
    ir2 = sb0("ir2", [1, TOK], I32)
    wmaxc = sb0("wmaxc", [128, 1], F32)
    wrec = sb0("wrec", [128, 1], F32)
    wn1 = sb0("wn1", [128, 1], F32)
    wn2 = sb0("wn2", [128, 1], F32)
    li1 = sb0("li1", [128, 1], I32)
    li2 = sb0("li2", [128, 1], I32)
    gm1 = sb0("gm1", [128, 1], F32)
    gm2 = sb0("gm2", [128, 1], F32)
    wst_sb = sb0("wst_sb", [IC, 128], F32)
    wtmp = sb0("wtmp", [128, IC], F32)
    wtmp2 = sb0("wtmp2", [128, IC], F32)
    lnm_c = sb0("lnm_c", [128, 16], F32)
    lsq_c = sb0("lsq_c", [128, 16], F32)
    lrs_c = sb0("lrs_c", [128, 16], F32)
    lv1_c = sb0("lv1_c", [128, 16], F32)
    lv2_c = sb0("lv2_c", [128, 16], F32)
    lvi_c = sb0("lvi_c", [128, 16], I32)
    lvi2_c = sb0("lvi2_c", [128, 16], I32)
    pbank = [ps0(f"pbank{i_}", [128, 512], F32) for i_ in range(8)]

    ch_v, ch_s, ch_g = sem("ch_v"), sem("ch_s"), sem("ch_g")
    d_small = sem("d_small")
    d_x = [sem(f"d_x{i_}") for i_ in range(4)]
    d_wif = [sem(f"d_wif{i_}") for i_ in range(4)]
    d_wg = [sem(f"d_wg{i_}") for i_ in range(2)]
    d_wo2 = [sem(f"d_wo{i_}") for i_ in range(2)]
    d_nat = [sem(f"d_nat{i_}") for i_ in range(3)]
    d_natd = [sem(f"d_natd{i_}") for i_ in range(2)]
    d_natd2 = sem("d_natd2")
    d_scr2 = [sem(f"d_scr{i_}") for i_ in range(2)]
    d_scrd4 = [sem(f"d_scrd{i_}") for i_ in range(4)]
    d_q8 = [[sem(f"d_q8_{w_}{i_}") for i_ in range(3)] for w_ in range(2)]
    d_q8d = [sem(f"d_q8d{i_}") for i_ in range(4)]
    d_wc = sem("d_wc")
    d_sx = [sem(f"d_sx{i_}") for i_ in range(2)]
    d_ss = [sem(f"d_ss{i_}") for i_ in range(2)]
    d_so = [sem(f"d_so{i_}") for i_ in range(2)]
    d_hsp = [sem(f"d_hsp{i_}") for i_ in range(3)]
    d_hrl = [sem(f"d_hrl{i_}") for i_ in range(3)]
    d_y = [sem(f"d_y{i_}") for i_ in range(2)]
    d_wsr = sem("d_wsr")
    d_row = [sem(f"d_row{i_}") for i_ in range(2)]
    g_id = sem("g_id")
    g_pmax = sem("g_pmax")
    v_misc = sem("v_misc")
    v_ln = sem("v_ln")
    s_ln = sem("s_ln")
    v_x8T = sem("v_x8T")
    v_qxT = sem("v_qxT")
    s_x8T = sem("s_x8T")
    s_qxT = sem("s_qxT")
    t_trx = sem("t_trx")
    s_pq = sem("s_pq")
    t_ptr = sem("t_ptr")
    v_pev = sem("v_pev")
    s_pev = sem("s_pev")
    t_if = sem("t_if")
    s_att = sem("s_att")
    v_att = sem("v_att")
    v_scan = sem("v_scan")
    s_hsq = sem("s_hsq")
    t_h2 = sem("t_h2")
    t_bc = sem("t_bc")
    v_rb = sem("v_rb")
    t_g = sem("t_g")
    s_epi = sem("s_epi")
    v_epi = sem("v_epi")
    t_op = sem("t_op")
    v_sev = sem("v_sev")
    t_gu = sem("t_gu")
    v_ffn = sem("v_ffn")
    s_ffn = sem("s_ffn")
    v_tmax = sem("v_tmax")
    s_qt = sem("s_qt")
    t_trt = sem("t_trt")
    v_qtT = sem("v_qtT")
    t_dn = sem("t_dn")
    v_fin = sem("v_fin")
    t_wsb = sem("t_wsb")
    t_g4 = sem("t_g4")
    t_u4 = sem("t_u4")
    v_wsb = sem("v_wsb")

    R = Rec()

    def V(fn, wait=None, sig=None):
        return R.emit("vector", fn, wait=wait, sig=sig, chain_sem=ch_v)

    def S(fn, wait=None, sig=None):
        return R.emit("scalar", fn, wait=wait, sig=sig, chain_sem=ch_s)

    def G(fn, wait=None, sig=None):
        return R.emit("gpsimd", fn, wait=wait, sig=sig, chain_sem=ch_g)

    def T(fn, wait=None, sig=None):
        return R.emit("tensor", fn, wait=wait, sig=sig)

    def DMA(eng, fn, sig, wait=None):
        return R.emit(eng, fn, wait=wait, sig=sig, nochain=True)


    def recip_nr(dst, src, f1):
        V(lambda v: v.reciprocal(dst, src))
        V(lambda v: v.tensor_tensor(f1, dst, src, ALU.mult))
        V(lambda v: v.tensor_scalar(f1, f1, -1.0, 2.0, ALU.mult, ALU.add))
        V(lambda v: v.tensor_tensor(dst, dst, f1, ALU.mult))

    def rsqrt_nr(dst, src, it1, it2, f1, f2):
        V(lambda v: v.tensor_scalar(it1, src.bitcast(I32), 1, None,
                                    ALU.logical_shift_right))
        V(lambda v: v.tensor_scalar(it2, it1, -1, RSQRT_C, ALU.mult, ALU.add))
        V(lambda v: v.tensor_copy(dst, it2.bitcast(F32)))
        for _ in range(3):
            V(lambda v: v.tensor_tensor(f1, dst, dst, ALU.mult))
            V(lambda v: v.tensor_tensor(f2, f1, src, ALU.mult))
            V(lambda v: v.tensor_scalar(f2, f2, -0.5, 1.5, ALU.mult, ALU.add))
            V(lambda v: v.tensor_tensor(dst, dst, f2, ALU.mult))

    for tile, name in [(si, "si_img"), (sf, "sf_img"), (sgv, "sg_img"),
                       (sov, "so_img"), (gnw, "gnw_img")]:
        DMA("sync", lambda e, t=tile, n=name: e.dma_start(out=t[:], in_=d_in[n]),
            sig=(d_small, 16))
    for ident in (identb, ident8, identf):
        G(lambda g, t=ident: g.memset(t[:], 0.0))
        G(lambda g, t=ident: g.affine_select(
            out=t[:], in_=t[:], compare_op=ALU.not_equal, fill=1.0, base=0,
            pattern=[[-1, 128]], channel_multiplier=1))
    G(lambda g: g.engine_nop(), sig=(g_id, 1))
    V(lambda v: v.memset(ones_col[:], 1.0))
    V(lambda v: v.memset(ones_row[:], 1.0))
    R.wait("vector", d_small, 16 * 5)
    V(lambda v: v.tensor_mul(gso[:], gnw[:], sov[:]))
    V(lambda v: v.tensor_scalar_mul(nsf[:], sf[:], -1.0), sig=(v_misc, 1))

    esX = ExitStack()
    sbX, _ = mk(esX)
    x8T3 = sbX("x8T3", [128, KC, PRE + TOK], BF16)

    esP = ExitStack()
    sbP, psP = mk(esP)
    natb = [sbP(f"natb{i_}", [128, D], F32) for i_ in range(3)]
    wmax2 = [sbP(f"wmax2{i_}", [128, 1], F32) for i_ in range(2)]
    wrec2 = [sbP(f"wrec2{i_}", [128, 1], F32) for i_ in range(2)]
    wifb = [sbP(f"wifb{i_}", [128, D], BF16) for i_ in range(6)]
    q8n = [sbP(f"q8n{i_}", [128, D], FP8) for i_ in range(8)]
    pstg = [sbP(f"pstg{i_}", [128, 2048], FP8) for i_ in range(2)]
    pp_tr = [pbank[2], pbank[3]]

    esL = ExitStack()
    sbL, psL = mk(esL)
    xint = [sbL(f"xint{i_}", [128, D], F32) for i_ in range(2)]
    xnt = [sbL("xnt0", [128, D], F32)]
    x8s = [sbL(f"x8s{i_}", [128, D], BF16) for i_ in range(2)]
    q8s = [sbL(f"q8s{i_}", [128, D], FP8) for i_ in range(2)]
    lsum = sbL("lsum", [128, 1], F32)
    wrecl = [sbL(f"wrecl{i_}", [128, 1], F32) for i_ in range(2)]
    x4 = [xint[0], xint[1], xnt[0]]
    lssq = sbL("lssq", [128, 1], F32)
    lvar = sbL("lvar", [128, 1], F32)
    lrstd = sbL("lrstd", [128, 1], F32)
    lnm = sbL("lnm", [128, 1], F32)
    lrm = sbL("lrm", [128, 1], F32)
    pb_bf = [pbank[0], pbank[1]]
    pq_f = [pbank[2], pbank[3]]

    gu_groups = [("g", gi) for gi in range(IB)] + [("u", gi) for gi in range(IB)]

    prep_c = {"n": 0}
    tpr_led = {}
    prep_scr_led = {}

    def emit_prep_quant(kind, gi, gidx):
        nat = d_in["wgat_nat"] if kind == "g" else d_in["wup_nat"]
        cols = wsg_cols if kind == "g" else wsu_cols
        for j in range(4):
            c = gi * 4 + j
            gc = prep_c["n"]
            prep_c["n"] = gc + 1
            buf = natb[gc % 3]
            dns = d_nat[gc % 3]
            wmx = wmax2[gc % 2]
            wrc = wrec2[gc % 2]
            DMA("scalar", lambda e, b=buf, n=nat, c=c: e.dma_start(
                out=b[:], in_=n[c]), sig=(dns, 16), wait=s_pq.lag(2))
            R.wait("vector", dns, dns.n)
            R.wait("vector", s_pq, s_pq.lag(1)[1])
            V(lambda v, b=buf, m=wmx: v.reduce_max(m[:], b[:], axis=AX.X,
                                                   apply_absolute_value=True))
            V(lambda v, c=c, cl=cols, m=wmx: v.tensor_copy(cl[:, c:c + 1], m[:]),
              sig=(g_pmax, 1))
            recip_nr(wrc[:], wmx[:], wn1[:])
            V(lambda v, r=wrc: v.tensor_scalar_mul(r[:], r[:], 224.0),
              sig=(v_misc, 1))
            prior_tr = tpr_led.get(gidx - 2)
            if prior_tr is not None:
                R.wait("vector", t_ptr, prior_tr)
            V(lambda v, b=buf, sl=(gidx % 2) * 4 + j, r=wrc: v.tensor_scalar(
                q8n[sl][:], b[:], r[:], None, ALU.mult), sig=(s_pq, 1))

    def emit_prep_tr(kind, gi, gidx, spq_tgt):
        sc = qwg_sc if kind == "g" else qwu_sc
        s_start = s_pev.n
        evac_led = {}
        scr_led = prep_scr_led
        for quarter in range(4):
            for kk in range(4):
                k = quarter * 4 + kk
                bank = pp_tr[k % 2]
                for j in range(4):
                    w = None
                    if j == 0:
                        if k == 0:
                            R.wait("tensor", v_qxT, v_qxT.n)
                            R.wait("tensor", s_qxT, s_qxT.n)
                            w = (s_pq.h, spq_tgt)
                        elif k == 1:
                            w = (s_pev.h, s_start)
                        else:
                            w = evac_led[k - 2]
                    T(lambda t, j=j, k=k, b=bank, sl=(gidx % 2) * 4 + j:
                      t.matmul(
                        b[:, j * 128:(j + 1) * 128],
                        q8n[sl][:, k * 128:(k + 1) * 128],
                        ident8[:], start=True, stop=True),
                      wait=w, sig=(t_ptr, 1) if j == 3 else None)
                qg = gidx * 4 + quarter
                pbuf = pstg[qg % 2]
                if kk == 0 and qg >= 2:
                    wsem, wval = scr_led[qg - 2]
                    R.wait("vector", wsem, wval)
                    R.wait("scalar", wsem, wval)
                if k % 2 == 0:
                    V(lambda v, kk=kk, b=bank, pb=pbuf: v.tensor_copy(
                        pb[:, kk * 512:(kk + 1) * 512], b[:]),
                      wait=t_ptr.cur(), sig=(v_pev, 1))
                    evac_led[k] = (v_pev.h, v_pev.n)
                else:
                    S(lambda s_, kk=kk, b=bank, pb=pbuf: s_.copy(
                        pb[:, kk * 512:(kk + 1) * 512], b[:]),
                      wait=t_ptr.cur(), sig=(s_pev, 1))
                    evac_led[k] = (s_pev.h, s_pev.n)
            R.wait("sync", s_pev, s_pev.n)
            R.wait("sync", v_pev, v_pev.n)
            DMA("sync", lambda e, sc=sc, gi=gi, q=quarter,
                pb=pstg[(gidx * 4 + quarter) % 2]: e.dma_start(
                out=sc[gi][:, q * 2048:(q + 1) * 2048], in_=pb[:]),
                sig=(d_scr2[(gidx * 4 + quarter) % 2], 16))
            scr_led[gidx * 4 + quarter] = (
                d_scr2[(gidx * 4 + quarter) % 2], d_scr2[(gidx * 4 + quarter) % 2].n)
        tpr_led[gidx] = t_ptr.n

    prep_pending = []

    def emit_prep_group(kind, gi):
        gidx = len(prep_pending)
        if gidx >= 1:
            emit_prep_tr(*prep_pending[gidx - 1])
        emit_prep_quant(kind, gi, gidx)
        prep_pending.append((kind, gi, gidx, s_pq.n))

    def flush_prep():
        for gidx in range(len(prep_pending)):
            if gidx not in tpr_led:
                emit_prep_tr(*prep_pending[gidx])

    ln_end = []
    ln_x8_led = []
    x8_bank_led = {}
    qx_bank_led = {}
    trx_end = []

    def emit_ln_pass1(j):
        buf = x4[j % 3]
        src = d_in["xpre"] if j < 8 else d_in["xown"]
        row = (j % 8) * 128
        dxs = d_x[j % 3]
        DMA("sync", lambda e, b=buf, s=src, r=row: e.dma_start(
            out=b[:], in_=s[r:r + 128, :]), sig=(dxs, 16),
            wait=ln_end[j - 3] if j >= 3 else None)
        R.wait("vector", dxs, dxs.n)
        R.wait("scalar", dxs, dxs.n)
        V(lambda v, b=buf: v.reduce_sum(lsum[:], b[:], axis=AX.X))
        V(lambda v, j=j: v.tensor_scalar_mul(lnm_c[:, j:j + 1], lsum[:],
                                             -1.0 / D), sig=(v_ln, 1))
        S(lambda s, b=buf: s.activation(b[:], b[:], AF.Square,
                                        accum_out=lssq[:]), wait=v_ln.cur())
        S(lambda s, j=j: s.copy(lsq_c[:, j:j + 1], lssq[:]),
          sig=(s_ln, 1))
        ln_end.append((s_ln.h, s_ln.n))

    def emit_ln_stats():
        R.wait("vector", v_ln, v_ln.n)
        R.wait("vector", s_ln, s_ln.n)
        V(lambda v: v.tensor_tensor(lv1_c[:], lnm_c[:], lnm_c[:], ALU.mult))
        V(lambda v: v.tensor_scalar_mul(lv1_c[:], lv1_c[:], -1.0))
        V(lambda v: v.scalar_tensor_tensor(lv1_c[:], lsq_c[:], 1.0 / D,
                                           lv1_c[:], ALU.mult, ALU.add))
        V(lambda v: v.tensor_scalar_add(lv1_c[:], lv1_c[:], EPS))
        rsqrt_nr(lrs_c[:], lv1_c[:], lvi_c[:], lvi2_c[:], lv2_c[:], wtmp[:, 0:16])
        V(lambda v: v.engine_nop(), sig=(v_misc, 1))

    q8s_led = {}

    def emit_ln_pass2A(j):
        buf = x4[j % 3]
        src = d_in["xpre"] if j < 8 else d_in["xown"]
        row = (j % 8) * 128
        own = j >= 8
        dxs = d_x[j % 3]
        DMA("sync", lambda e, b=buf, s=src, r=row: e.dma_start(
            out=b[:], in_=s[r:r + 128, :]), sig=(dxs, 16),
            wait=ln_end[16 + j - 3] if j >= 3 else
            ln_end[{0: 15, 1: 13, 2: 14}[j % 3]])
        R.wait("vector", dxs, dxs.n)
        R.wait("scalar", dxs, dxs.n)
        V(lambda v, b=buf, j=j: v.tensor_scalar(
            b[:], b[:], lnm_c[:, j:j + 1], lrs_c[:, j:j + 1],
            ALU.add, ALU.mult), sig=(v_ln, 1))
        if own:
            wl = wrecl[j % 2]
            V(lambda v, b=buf: v.reduce_max(lrm[:], b[:], axis=AX.X,
                                            apply_absolute_value=True))
            V(lambda v, j=j: v.tensor_copy(rmax[:, j - 8:j - 7], lrm[:]))
            if j >= 10:
                R.wait("vector", s_ln, q8s_led[j - 2])
            recip_nr(wl[:], lrm[:], wn1[:])
            V(lambda v, w=wl: v.tensor_scalar_mul(w[:], w[:], 224.0),
              sig=(v_misc, 1))
            if j >= 10:
                R.wait("scalar", t_trx, trx_end[j - 2])
            S(lambda s, b=buf, bq=j % 2, w=wl: s.activation(
                q8s[bq][:], b[:], AF.Copy, scale=w[:]),
              wait=v_misc.cur(), sig=(s_ln, 1))
        q8s_led[j] = s_ln.n

    def emit_ln_pass2B(j):
        buf = x4[j % 3]
        tcol = j * 128
        own = j >= 8
        V(lambda v, b=buf: v.tensor_scalar(b[:], b[:], 256.0, MAGIC,
                                           ALU.mult, ALU.add),
          wait=(s_ln.h, q8s_led[j]) if own else None, sig=(v_ln, 1))
        if j >= 2:
            R.wait("scalar", t_trx, trx_end[j - 2])
        S(lambda s, b=buf, bx=j % 2: s.activation(
            x8s[bx][:], b[:], AF.Copy, scale=1.0 / 256.0, bias=-MAGIC / 256.0),
          wait=v_ln.cur(), sig=(s_ln, 1))
        sln_x8 = s_ln.n
        gb8 = j * 4
        gbq = (j - 8) * 4
        for kg in range(4):
            bank = pb_bf[kg % 2]
            g = gb8 + kg
            for dk in range(4):
                k = kg * 4 + dk
                w = None
                if dk == 0:
                    w = x8_bank_led.get(g - 2)
                    if kg == 0:
                        w = (s_ln.h, sln_x8)
                        if g >= 2:
                            bw = x8_bank_led[g - 2]
                            R.wait("tensor", bw[0], bw[1])
                T(lambda t, k=k, b=bank, bf=j % 2, dk=dk: t.matmul(
                    b[:, dk * 128:(dk + 1) * 128],
                    x8s[bf][:, k * 128:(k + 1) * 128], identb[:],
                    start=True, stop=True),
                  wait=w, sig=(t_trx, 1) if dk == 3 else None)
            if kg % 2 == 0:
                V(lambda v, kg=kg, b=bank, tc=tcol: v.tensor_copy(
                    x8T3[:, kg * 4:(kg + 1) * 4, tc:tc + 128], b[:]),
                  wait=t_trx.cur(), sig=(v_x8T, 1))
                x8_bank_led[g] = (v_x8T.h, v_x8T.n)
            else:
                S(lambda s_, kg=kg, b=bank, tc=tcol: s_.copy(
                    x8T3[:, kg * 4:(kg + 1) * 4, tc:tc + 128], b[:]),
                  wait=t_trx.cur(), sig=(s_x8T, 1))
                x8_bank_led[g] = (s_x8T.h, s_x8T.n)
        if own:
            for kg in range(4):
                qbank = pq_f[kg % 2]
                g = gbq + kg
                for dk in range(4):
                    k = kg * 4 + dk
                    w = None
                    if dk == 0:
                        w = qx_bank_led.get(g - 2)
                        if kg == 0:
                            w = (s_ln.h, q8s_led[j])
                            R.wait("tensor", v_pev, v_pev.n)
                            R.wait("tensor", s_pev, s_pev.n)
                            if g >= 2:
                                bw = qx_bank_led[g - 2]
                                R.wait("tensor", bw[0], bw[1])
                    T(lambda t, k=k, b=qbank, bq=j % 2, dk=dk: t.matmul(
                        b[:, dk * 128:(dk + 1) * 128],
                        q8s[bq][:, k * 128:(k + 1) * 128], ident8[:],
                        start=True, stop=True),
                      wait=w, sig=(t_trx, 1) if dk == 3 else None)
                if kg % 2 == 0:
                    V(lambda v, kg=kg, b=qbank, j=j: v.tensor_copy(
                        qxT3[:, kg * 4:(kg + 1) * 4, (j - 8) * 128:(j - 7) * 128],
                        b[:]), wait=t_trx.cur(), sig=(v_qxT, 1))
                    qx_bank_led[g] = (v_qxT.h, v_qxT.n)
                else:
                    S(lambda s_, kg=kg, b=qbank, j=j: s_.copy(
                        qxT3[:, kg * 4:(kg + 1) * 4, (j - 8) * 128:(j - 7) * 128],
                        b[:]), wait=t_trx.cur(), sig=(s_qxT, 1))
                    qx_bank_led[g] = (s_qxT.h, s_qxT.n)
        ln_end.append((s_ln.h, s_ln.n))
        trx_end.append(t_trx.n)
        ln_x8_led.append(((v_x8T.h, v_x8T.n), (s_x8T.h, s_x8T.n)))

    R.wait("tensor", g_id, 1)
    gu_i = 0
    for j in range(16):
        emit_ln_pass1(j)
    emit_ln_stats()
    for j in range(16):
        emit_ln_pass2A(j)
        if j >= 1:
            emit_ln_pass2B(j - 1)
    emit_ln_pass2B(15)
    esL.close()

    esA = ExitStack()
    sbA, psA = mk(esA)
    fbuf = sbA("fbuf", [128, 1024], F32)
    f2buf = sbA("f2buf", [128, 1024], F32)
    hpre = sbA("hpre", [128, PRE], F32)
    hsqb = [sbA(f"hsqb{i_}", [128, 1024], F32) for i_ in range(3)]
    Hrot = [sbA(f"Hrot{i_}", [128, TOK], F32) for i_ in range(3)]
    rstd_b = hsqb[0]
    pi_ps = [pbank[4], pbank[5]]
    pf_ps = [pbank[6], pbank[7]]
    ph2 = [pbank[0], pbank[1]]

    hsp_led = {}
    hrl_led = {}

    def emit_h2_sum(cc):
        R.wait("tensor", s_hsq, cc + 1)
        if cc == 0:
            R.wait("tensor", v_x8T, v_x8T.n)
            R.wait("tensor", s_x8T, s_x8T.n)
        for hh in range(2):
            T(lambda t, cc=cc, hh=hh: t.matmul(
                ph2[hh][0:1, :], ones_col[:],
                hsqb[cc % 3][:, hh * 512:(hh + 1) * 512],
                start=(cc == 0), stop=(cc == KC - 1), skip_group_check=True),
              sig=(t_h2, 1))

    def emit_attn_chunk(cc):
        bi, bf_ = wifb[(cc % 3) * 2], wifb[(cc % 3) * 2 + 1]
        dws = d_wif[cc % 3]
        wdma = (None if cc < 3 else t_if.lag(16))
        DMA("gpsimd", lambda e, cc=cc, b=bi: e.dma_start(
            out=b[:], in_=d_in["wi_sw"][cc]), sig=(dws, 16), wait=wdma)
        DMA("gpsimd", lambda e, cc=cc, b=bf_: e.dma_start(
            out=b[:], in_=d_in["wf_sw"][cc]), sig=(dws, 16))
        need_w = (dws, dws.n)
        for seg in range(2):
            fmm_led = None
            for wsel, wbuf, pss in ((1, bf_, pf_ps), (0, bi, pi_ps)):
                for hh in range(2):
                    colbase = seg * 1024 + hh * 512
                    bank = pss[hh]
                    for k in range(KC):
                        w = None
                        if k == 0:
                            R.wait("tensor", need_w[0], need_w[1])
                            blk = seg * 8 + hh * 4 + 3
                            if cc == 0:
                                R.wait("tensor", ln_x8_led[blk][0][0],
                                       ln_x8_led[blk][0][1])
                            R.wait("tensor", ln_x8_led[blk][1][0],
                                       ln_x8_led[blk][1][1])
                            w = v_att.cur() if wsel == 0 else s_att.cur()
                        T(lambda t, k=k, b=bank, wb=wbuf, cb=colbase: t.matmul(
                            b[:], wb[:, k * 128:(k + 1) * 128],
                            x8T3[:, k, cb:cb + 512],
                            start=(k == 0), stop=(k == KC - 1)),
                          wait=w if k == 0 else None,
                          sig=(t_if, 1) if k == KC - 1 else None)
                if wsel == 1:
                    fmm_led = (t_if.h, t_if.n)
            tif_now = t_if.n
            R.wait("scalar", t_if, fmm_led[1])
            R.wait("scalar", v_scan, v_scan.n)
            for hh in range(2):
                S(lambda s, cc=cc, hh=hh: s.activation(
                    fbuf[:, hh * 512:(hh + 1) * 512], pf_ps[hh][:], AF.Sigmoid,
                    scale=sf[:, cc:cc + 1]))
                S(lambda s, cc=cc, hh=hh: s.activation(
                    f2buf[:, hh * 512:(hh + 1) * 512], pf_ps[hh][:], AF.Sigmoid,
                    scale=nsf[:, cc:cc + 1]), sig=(s_att, 1))
            satt_now = s_att.n
            R.wait("vector", t_if, tif_now)
            R.wait("vector", s_att, satt_now)
            for hh in range(2):
                V(lambda v, cc=cc, hh=hh: v.scalar_tensor_tensor(
                    f2buf[:, hh * 512:(hh + 1) * 512], pi_ps[hh][:],
                    si[:, cc:cc + 1], f2buf[:, hh * 512:(hh + 1) * 512],
                    ALU.mult, ALU.mult), sig=(v_att, 1))
            if seg == 0:
                V(lambda v: v.tensor_tensor_scan(
                    hpre[:], fbuf[:], f2buf[:], 0.0, ALU.mult, ALU.add),
                  sig=(v_scan, 1))
            else:
                if cc >= 3:
                    R.wait("vector", d_hsp[cc % 3], hsp_led[cc - 3][1])
                V(lambda v, cc=cc: v.tensor_tensor_scan(
                    Hrot[cc % 3][:], fbuf[:], f2buf[:], hpre[:, PRE - 1:PRE],
                    ALU.mult, ALU.add), sig=(v_scan, 1))
        R.wait("scalar", t_h2, t_h2.n)
        S(lambda s, cc=cc: s.activation(hsqb[cc % 3][:], Hrot[cc % 3][:],
                                        AF.Square),
          wait=v_scan.cur(), sig=(s_hsq, 1))
        dhs = d_hsp[cc % 3]
        DMA("sync", lambda e, cc=cc: e.dma_start(
            out=h_sc[cc], in_=Hrot[cc % 3][:]), sig=(dhs, 16),
            wait=s_hsq.cur())
        hsp_led[cc] = (dhs.h, dhs.n)
        if cc == 2:
            for k_ in range(KC):
                DMA("gpsimd", lambda e, k_=k_: e.dma_start(
                    out=woT_bf[k_], in_=d_in["woT_sw"][k_]), sig=(d_wc, 16))
        if cc >= 2:
            emit_h2_sum(cc - 2)

    for cc in range(KC):
        emit_attn_chunk(cc)
        if gu_i < len(gu_groups):
            emit_prep_group(*gu_groups[gu_i])
            gu_i += 1
        if cc < 6 and gu_i < len(gu_groups):
            emit_prep_group(*gu_groups[gu_i])
            gu_i += 1
    emit_h2_sum(KC - 2)
    emit_h2_sum(KC - 1)
    while gu_i < len(gu_groups):
        emit_prep_group(*gu_groups[gu_i])
        gu_i += 1
    flush_prep()

    def emit_ws_row(cols, nchunks, dr_idx):
        n = nchunks
        R.wait("vector", g_pmax, g_pmax.n)
        R.wait("vector", t_ptr, t_ptr.n)
        V(lambda v, n=n, cl=cols: v.tensor_copy(wtmp[:, 0:n], cl[:, 0:n]),
          sig=(v_misc, 1))
        T(lambda t, n=n: t.matmul(pp_tr[0][0:n, 0:128], wtmp[:, 0:n],
                                  identf[:], start=True, stop=True),
          wait=v_misc.cur(), sig=(t_ptr, 1))
        R.wait("vector", d_wsr, d_wsr.n)
        V(lambda v, n=n: v.tensor_copy(wst_sb[0:n, :], pp_tr[0][0:n, 0:128]),
          wait=t_ptr.cur(), sig=(v_pev, 1))
        DMA("sync", lambda e, n=n, di=dr_idx: e.dma_start(
            out=ws_dr[di][0:n, :], in_=wst_sb[0:n, :]), sig=(d_wsr, 16),
            wait=v_pev.cur())

    emit_ws_row(wsg_cols, IC, 0)
    emit_ws_row(wsu_cols, IC, 1)

    R.wait("vector", t_h2, t_h2.n)
    V(lambda v: v.tensor_copy(ssum[0:1, 0:512], ph2[0][0:1, :]))
    V(lambda v: v.tensor_copy(ssum[0:1, 512:1024], ph2[1][0:1, :]))
    V(lambda v: v.tensor_scalar(ssum[:], ssum[:], 1.0 / D, EPS, ALU.mult,
                                ALU.add))
    rsqrt_nr(rstd_r[:], ssum[:], ir1[:], ir2[:], hpre[0:1, :], rstd_b[0:1, :])
    V(lambda v: v.engine_nop(), sig=(v_misc, 1))
    vm_rstd = v_misc.n
    for hh in range(2):
        T(lambda t, hh=hh: t.matmul(
            ph2[hh][:, :], ones_row[:], rstd_r[0:1, hh * 512:(hh + 1) * 512],
            start=True, stop=True, skip_group_check=True),
          wait=(v_misc.h, vm_rstd), sig=(t_bc, 1))
        V(lambda v, hh=hh: v.tensor_copy(
            rstd_b[:, hh * 512:(hh + 1) * 512], ph2[hh][:, :]),
          wait=t_bc.cur(), sig=(v_rb, 1))

    sgbuf = fbuf
    g2buf = f2buf
    o1buf = hpre
    R.wait("scalar", v_scan, v_scan.n)

    def emit_epi_chunk(cc):
        b = wifb[cc % 2]
        dgs = d_wg[cc % 2]
        wdma = (t_if.cur() if cc < 2 else t_g.lag(2))
        DMA("gpsimd", lambda e, cc=cc, b=b: e.dma_start(
            out=b[:], in_=d_in["wg_sw"][cc]), sig=(dgs, 16), wait=wdma)
        need = (dgs, dgs.n)
        hb = Hrot[cc % 3]
        dhr = d_hrl[cc % 3]
        last_spill = {0: 15, 1: 13, 2: 14}[cc % 3]
        DMA("sync", lambda e, cc=cc, h=hb: e.dma_start(
            out=h[:], in_=h_sc[cc]), sig=(dhr, 16),
            wait=(hsp_led[last_spill] if cc < 3 else v_epi.lag(2)))
        hrl_led[cc] = (dhr.h, dhr.n)
        bankset = pi_ps if cc % 2 == 0 else pf_ps
        for hh in range(2):
            bank = bankset[hh]
            for k in range(KC):
                w = None
                if k == 0:
                    R.wait("tensor", need[0], need[1])
                    w = (v_epi.lag(1) if cc >= 2 else
                         ((v_att.h, v_att.n) if cc == 0 else (s_att.h, s_att.n)))
                T(lambda t, k=k, b2=bank, wb=b, hh=hh: t.matmul(
                    b2[:], wb[:, k * 128:(k + 1) * 128],
                    x8T3[:, k, 1024 + hh * 512:1024 + (hh + 1) * 512],
                    start=(k == 0), stop=(k == KC - 1)),
                  wait=w if k == 0 else None,
                  sig=(t_g, 1) if k == KC - 1 else None)
        tg_now = t_g.n
        R.wait("scalar", t_g, tg_now)
        R.wait("scalar", v_epi, v_epi.n)
        for hh in range(2):
            S(lambda s, cc=cc, hh=hh, bk=bankset[hh]: s.activation(
                sgbuf[:, hh * 512:(hh + 1) * 512], bk[:], AF.Sigmoid,
                scale=sgv[:, cc:cc + 1]), sig=(s_epi, 1))
        sep_now = s_epi.n
        R.wait("vector", t_g, tg_now)
        R.wait("vector", s_epi, sep_now)
        R.wait("vector", d_hrl[cc % 3], hrl_led[cc][1])
        for hh in range(2):
            V(lambda v, cc=cc, hh=hh, bk=bankset[hh]: v.tensor_scalar_mul(
                g2buf[:, hh * 512:(hh + 1) * 512], bk[:],
                sgv[:, cc:cc + 1]))
        V(lambda v: v.tensor_tensor(g2buf[:], g2buf[:], sgbuf[:], ALU.mult))
        V(lambda v, cc=cc, h=hb: v.scalar_tensor_tensor(
            o1buf[:], h[:], gso[:, cc:cc + 1], rstd_b[:], ALU.mult, ALU.mult),
          wait=(v_rb.h, 2) if cc == 0 else None)
        V(lambda v, cc=cc: v.tensor_tensor(
            x8T3[:, cc, 0:1024], o1buf[:], g2buf[:], ALU.mult), sig=(v_epi, 1))

    for cc in range(KC):
        emit_epi_chunk(cc)
    esA.close()
    esP.close()

    esB = ExitStack()
    sbB, psB = mk(esB)
    woq2 = [[sbB(f"woq_{p_}_{k}", [128, 512], BF16) for k in range(KC)]
            for p_ in range(2)]
    sst = [sbB(f"sst{i_}", [128, 512], F32) for i_ in range(2)]
    smg = sbB("smg", [128, 512], F32)
    po_ps = [pbank[0], pbank[1]]
    natd = [sbB("natd0", [128, D], F32), sbB("natd1", [128, D], F32),
            sbB("natd2", [128, 1536], F32)]
    q8d = [sbB(f"q8d{i_}", [128, I], FP8) for i_ in range(4)]
    pstgd = sbB("pstgd", [128, IC * 512], FP8)
    ppd = [pbank[2], pbank[3]]

    dq_spq = {}
    dtr_led = {}
    dsd_led = {}

    def emit_down_quant(gi):
        widths = [(0, 2048), (2048, 2048), (4096, 1536)]
        if gi >= 1:
            R.wait("vector", t_ptr, dtr_led[gi - 1])
        for j in range(4):
            c = gi * 4 + j
            led = []
            for pi_, (off, w) in enumerate(widths):
                buf = natd[pi_]
                dnd = d_natd[pi_ % 2] if pi_ < 2 else d_natd2
                if gi == 0 and j == 0:
                    wdma = v_epi.cur() if pi_ == 0 else None
                else:
                    wdma = s_pq.lag(2 - pi_) if pi_ < 2 else s_pq.cur()
                DMA("scalar", lambda e, b=buf, c=c, o=off, w=w: e.dma_start(
                    out=b[:, 0:w], in_=d_in["wdn_nat"][c][:, o:o + w]),
                    sig=(dnd, 16), wait=wdma)
                led.append((dnd.h, dnd.n))
            for pi_, (off, w) in enumerate(widths):
                buf = natd[pi_]
                R.wait("vector", led[pi_][0], led[pi_][1])
                if pi_ == 0:
                    R.wait("vector", s_pq, s_pq.n)
                    V(lambda v, b=buf, w=w: v.reduce_max(
                        wmaxc[:], b[:, 0:w], axis=AX.X,
                        apply_absolute_value=True), sig=(g_pmax, 1))
                else:
                    V(lambda v, b=buf, w=w: v.reduce_max(
                        wn2[:], b[:, 0:w], axis=AX.X,
                        apply_absolute_value=True))
                    V(lambda v: v.tensor_max(wmaxc[:], wmaxc[:], wn2[:]),
                      sig=(g_pmax, 1))
            V(lambda v, c=c: v.tensor_copy(wsd_cols[:, c:c + 1], wmaxc[:]),
              sig=(g_pmax, 1))
            recip_nr(wrec[:], wmaxc[:], wn1[:])
            V(lambda v: v.tensor_scalar_mul(wrec[:], wrec[:], 224.0),
              sig=(v_misc, 1))
            for pi_, (off, w) in enumerate(widths):
                buf = natd[pi_]
                V(lambda v, b=buf, j=j, o=off, w=w: v.tensor_scalar(
                    q8d[j][:, o:o + w], b[:, 0:w], wrec[:], None, ALU.mult),
                  sig=(s_pq, 1))
        dq_spq[gi] = s_pq.n

    def emit_down_tr(gi):
        for icc in range(IC):
            bank = ppd[icc % 2]
            for j in range(4):
                w = None
                if j == 0:
                    w = (s_pq.h, dq_spq[gi]) if icc == 0 else v_pev.lag(1)
                T(lambda t, j=j, icc=icc, b=bank: t.matmul(
                    b[:, j * 128:(j + 1) * 128],
                    q8d[j][:, icc * 128:(icc + 1) * 128], ident8[:],
                    start=True, stop=True),
                  wait=w, sig=(t_ptr, 1) if j == 3 else None)
            if icc % 11 == 0 and gi >= 1:
                wsem, wval = dsd_led[(gi - 1, icc // 11)]
                R.wait("vector", wsem, wval)
            V(lambda v, icc=icc, b=bank: v.tensor_copy(
                pstgd[:, icc * 512:(icc + 1) * 512], b[:]),
              wait=t_ptr.cur(), sig=(v_pev, 1))
            if icc % 11 == 10:
                p = icc // 11
                DMA("sync", lambda e, gi=gi, p=p: e.dma_start(
                    out=qwd_sc[gi][:, p * 5632:(p + 1) * 5632],
                    in_=pstgd[:, p * 5632:(p + 1) * 5632]),
                    sig=(d_scrd4[p], 16), wait=v_pev.cur())
                dsd_led[(gi, p)] = (d_scrd4[p], d_scrd4[p].n)
        dtr_led[gi] = t_ptr.n

    for _ds in d_scr2:
        R.wait("sync", _ds, _ds.n)
        R.wait("scalar", _ds, _ds.n)
        R.wait("vector", _ds, _ds.n)
        R.wait("gpsimd", _ds, _ds.n)
    R.wait("sync", d_wsr, d_wsr.n)
    for _ds in d_scrd4:
        R.wait("sync", _ds, _ds.n)
    R.wait("sync", v_epi, KC)
    R.wait("gpsimd", d_wsr, d_wsr.n)
    R.wait("gpsimd", v_epi, KC)
    oproj_sst_led = {}
    oproj_top_led = {}

    def emit_oproj_od(od):
        wq = woq2[od % 2]
        dwo = d_wo2[od % 2]
        for k in range(KC):
            wdma = ((v_epi.h, KC) if od < 2
                    else (t_op.h, oproj_top_led[od - 2]))
            if od < 2 and k == 0:
                R.wait("sync", d_wc, 16 * KC)
            DMA("sync", lambda e, k=k, od=od, wq=wq: e.dma_start(
                out=wq[k][:],
                in_=woT_bf[k][:, od * 512:(od + 1) * 512]),
                sig=(dwo, 16), wait=wdma)
        need = (dwo.h, dwo.n)
        sst_led = oproj_sst_led
        for tj in range(8):
            bank = po_ps[tj % 2]
            for cc in range(KC):
                w = None
                if cc == 0:
                    R.wait("tensor", need[0], need[1])
                    if od == 0 and tj == 0:
                        R.wait("tensor", v_epi, v_epi.n)
                        R.wait("tensor", v_rb, 2)
                    w = v_sev.lag(1)
                T(lambda t, cc=cc, b=bank, tj=tj: t.matmul(
                    b[:], x8T3[:, cc, tj * 128:(tj + 1) * 128], wq[cc][:],
                    start=(cc == 0), stop=(cc == KC - 1)),
                  wait=w if cc == 0 else None,
                  sig=(t_op, 1) if cc == KC - 1 else None)
            top_now = t_op.n
            R.wait("vector", t_op, top_now)
            gsst = od * 8 + tj
            if gsst >= 2:
                R.wait("vector", d_so[gsst % 2], sst_led[gsst - 2][1])
            V(lambda v, b=bank: v.tensor_scalar(
                smg[:], b[:], 256.0, MAGIC, ALU.mult, ALU.add))
            V(lambda v, tj=tj: v.tensor_scalar(
                sst[tj % 2][:], smg[:], -MAGIC, 1.0 / 256.0, ALU.add, ALU.mult),
              sig=(v_sev, 1))
            dso = d_so[gsst % 2]
            DMA("gpsimd", lambda e, tj=tj, od=od: e.dma_start(
                out=s_sc[tj * 128:(tj + 1) * 128, od * 512:(od + 1) * 512],
                in_=sst[tj % 2][:]), sig=(dso, 16), wait=v_sev.cur())
            sst_led[gsst] = (dso.h, dso.n)
        oproj_top_led[od] = t_op.n

    for od in range(4):
        emit_oproj_od(od)
        if od >= 1:
            emit_down_tr(od - 1)
        emit_down_quant(od)
    emit_down_tr(3)
    emit_ws_row(wsd_cols, KC, 2)

    esB.close()
    esX.close()

    esC = ExitStack()
    sbC, psC = mk(esC)
    Tt = [sbC(f"Tt{i_}", [128, I], F32) for i_ in range(4)]
    qt8 = sbC("qt8", [128, I], FP8)
    qtT = [sbC(f"qtT{i_}", [128, I], FP8) for i_ in range(4)]
    wgq = [sbC(f"wgq{i_}", [128, 2048], FP8) for i_ in range(3)]
    wuq = [sbC(f"wuq{i_}", [128, 2048], FP8) for i_ in range(3)]
    wdq = [sbC(f"wdq{i_}", [128, 2048], FP8) for i_ in range(4)]
    wsb = [sbC(f"wsb{i_}", [128, 512], F32) for i_ in range(2)]
    ayb = sbC("ayb", [128, 512], F32)
    y2b = sbC("y2b", [128, 512], F32)
    sgm8 = sbC("sgm8", [128, 512], F32)
    xrl = [sbC(f"xrl{i_}", [128, 512], F32) for i_ in range(2)]
    srl = [sbC(f"srl{i_}", [128, 512], F32) for i_ in range(2)]
    outb = [sbC(f"outb{i_}", [128, 512], F32) for i_ in range(2)]
    tm1 = sbC("tm1", [128, 1], F32)
    tm2 = sbC("tm2", [128, 1], F32)
    pgu = pbank

    for _ds in d_scr2 + d_scrd4:
        R.wait("sync", _ds, _ds.n)
    R.wait("sync", d_wsr, d_wsr.n)
    R.wait("sync", d_so[0], d_so[0].n)
    R.wait("sync", d_so[1], d_so[1].n)
    R.wait("sync", t_op, t_op.n)
    R.wait("sync", v_sev, v_sev.n)
    R.wait("vector", d_so[0], d_so[0].n)
    R.wait("vector", d_so[1], d_so[1].n)
    R.wait("vector", d_wsr, d_wsr.n)
    for _ds in d_scrd4:
        R.wait("vector", _ds, _ds.n)
    R.wait("vector", t_op, t_op.n)
    R.wait("vector", v_pev, v_pev.n)
    R.wait("vector", s_pev, s_pev.n)
    R.wait("scalar", t_op, t_op.n)
    R.wait("scalar", v_pev, v_pev.n)
    R.wait("tensor", v_sev, v_sev.n)
    R.wait("tensor", v_pev, v_pev.n)
    R.wait("tensor", s_pev, s_pev.n)
    R.wait("tensor", v_qxT, v_qxT.n)
    R.wait("tensor", s_qxT, s_qxT.n)
    for _ds in d_scrd4:
        R.wait("gpsimd", _ds, _ds.n)

    V(lambda v: v.tensor_scalar_mul(sgf[:], rmax[:], FQ))
    V(lambda v: v.tensor_tensor(sg2[:], sgf[:], sgf[:], ALU.mult),
      sig=(v_misc, 1))
    vm_sgf = v_misc.n

    bc_state = {"n": 0}
    bc_use_led = {}

    def bcast_ws(dr_idx, base):
        n = bc_state["n"]
        bc_state["n"] = n + 1
        p = n % 2
        drs = d_row[p]
        DMA("sync", lambda e, di=dr_idx, b=base, p=p: e.dma_start(
            out=wsb[p][:],
            in_=ws_dr[di:di + 1, :, :].rearrange("a b c -> a (b c)")
                [0:1, b:b + 512].partition_broadcast(128)),
            sig=(drs, 16), wait=bc_use_led.get(n - 2))
        return p, (drs.h, drs.n)

    def bc_mark_used(n_offset=0):
        bc_use_led[bc_state["n"] - 1] = (ch_v.h, ch_v.n)

    mm_led = {}
    dq_led = {}
    dn_led = {}
    dqd_led = {}
    y_led = {}
    ag_led = {}
    up_led = {}
    v_ag = sem("v_ag")

    def load_quarter(wsel, gq):
        sc = qwg_sc if wsel == 0 else qwu_sc
        ring = wgq if wsel == 0 else wuq
        ic_ = (gq % (IB * 4)) // 4
        q = gq % 4
        wq = mm_led.get((wsel, gq - 3))
        dqs = d_q8[wsel][gq % 3]
        DMA("sync", lambda e, sc=sc, ic_=ic_, q=q, b=ring[gq % 3]: e.dma_start(
            out=b[:], in_=sc[ic_][:, q * 2048:(q + 1) * 2048]),
            sig=(dqs, 16), wait=wq)
        dq_led[(wsel, gq)] = (dqs.h, dqs.n)

    for wsel in range(2):
        for gq0 in range(3):
            load_quarter(wsel, gq0)

    GB, UB = pgu[0:4], pgu[4:8]
    for TS in range(2):
        R.wait("vector", s_qt, s_qt.n)
        R.wait("vector", t_dn, t_dn.n)
        for ic in range(IB):
            bc_g = bcast_ws(0, ic * 512)
            bc_u = bcast_ws(1, ic * 512)
            for wsel, bufs, sem4, banks in ((0, wgq, t_g4, GB),
                                            (1, wuq, t_u4, UB)):
                glb = TS * IB + ic
                guard = (ag_led if wsel == 0 else up_led).get(glb - 1)
                if guard is not None:
                    R.wait("tensor", guard[0], guard[1])
                for k in range(KC):
                    gq = TS * IB * 4 + ic * 4 + k // 4
                    for tj in range(4):
                        if k % 4 == 0 and tj == 0:
                            R.wait("tensor", dq_led[(wsel, gq)][0],
                                   dq_led[(wsel, gq)][1])
                        last_q = (k % 4 == 3 and tj == 3)
                        T(lambda t, k=k, tj=tj, bu=bufs, sl=gq % 3, TS=TS,
                          bks=banks: t.matmul(
                            bks[tj][:],
                            qxT3[:, k, TS * 512 + tj * 128:TS * 512 + (tj + 1) * 128],
                            bu[sl][:, (k % 4) * 512:(k % 4 + 1) * 512],
                            start=(k == 0), stop=(k == KC - 1)),
                          sig=(sem4, 1) if last_q else None)
                        if last_q:
                            mm_led[(wsel, gq)] = (sem4.h, sem4.n)
                            if gq + 3 < 2 * IB * 4:
                                load_quarter(wsel, gq + 3)
            glb = TS * IB + ic
            R.wait("vector", t_g4, t_g4.n)
            R.wait("vector", bc_g[1][0], bc_g[1][1])
            for tj in range(4):
                V(lambda v, tj=tj, ic=ic, p=bc_g[0]: v.tensor_tensor(
                    Tt[tj][:, ic * 512:(ic + 1) * 512], GB[tj][:],
                    wsb[p][:], ALU.mult),
                  wait=(v_misc.h, vm_sgf) if (TS == 0 and ic == 0
                                              and tj == 0) else None,
                  sig=(v_ag, 1))
            ag_led[glb] = (v_ag.h, v_ag.n)
            bc_use_led[bc_state["n"] - 2] = (v_ag.h, v_ag.n)
            R.wait("scalar", t_u4, t_u4.n)
            R.wait("scalar", v_ffn, v_ffn.n)
            R.wait("vector", t_u4, t_u4.n)
            R.wait("vector", bc_u[1][0], bc_u[1][1])
            for tj in range(4):
                S(lambda s, tj=tj, ic=ic, TS=TS: s.activation(
                    sgm8[:], Tt[tj][:, ic * 512:(ic + 1) * 512],
                    AF.Sigmoid,
                    scale=sgf[:, TS * 4 + tj:TS * 4 + tj + 1]),
                  wait=v_ag.cur() if tj == 0 else v_ffn.cur(), sig=(s_ffn, 1))
                V(lambda v, tj=tj, p=bc_u[0]: v.tensor_tensor(
                    ayb[:], UB[tj][:], wsb[p][:], ALU.mult))
                V(lambda v, tj=tj, ic=ic, TS=TS: v.scalar_tensor_tensor(
                    y2b[:], Tt[tj][:, ic * 512:(ic + 1) * 512],
                    sg2[:, TS * 4 + tj:TS * 4 + tj + 1], ayb[:],
                    ALU.mult, ALU.mult))
                V(lambda v, tj=tj, ic=ic: v.tensor_tensor(
                    Tt[tj][:, ic * 512:(ic + 1) * 512], y2b[:], sgm8[:],
                    ALU.mult), wait=s_ffn.cur(), sig=(v_ffn, 1))
            up_led[glb] = (v_ffn.h, v_ffn.n)
            bc_use_led[bc_state["n"] - 1] = (v_ffn.h, v_ffn.n)
        for tj in range(4):
            V(lambda v, tj=tj: v.reduce_max(tm1[:], Tt[tj][:], axis=AX.X,
                                            apply_absolute_value=True))
            V(lambda v, tj=tj, TS=TS: v.tensor_scalar_mul(
                tyf[:, TS * 4 + tj:TS * 4 + tj + 1], tm1[:], FQ))
            R.wait("vector", s_qt, s_qt.n)
            recip_nr(tm2[:], tm1[:], wn1[:])
            V(lambda v: v.tensor_scalar_mul(tm2[:], tm2[:], 224.0),
              sig=(v_tmax, 1))
            R.wait("scalar", t_trt, t_trt.n)
            S(lambda s, tj=tj: s.activation(qt8[:], Tt[tj][:], AF.Copy,
                                            scale=tm2[:]),
              wait=v_tmax.cur(), sig=(s_qt, 1))
            sqt_now = s_qt.n
            if TS == 0 and tj == 0:
                R.wait("tensor", v_ffn, v_ffn.n)
            for icc in range(IC):
                bank = pgu[icc % 2]
                w = (s_qt.h, sqt_now) if icc == 0 else (
                    v_qtT.lag(1) if icc >= 2 else None)
                T(lambda t, icc=icc, b=bank: t.matmul(
                    b[:, 0:128], qt8[:, icc * 128:(icc + 1) * 128], ident8[:],
                    start=True, stop=True), wait=w, sig=(t_trt, 1))
                V(lambda v, icc=icc, b=bank, tj=tj: v.tensor_copy(
                    qtT[tj][:, icc * 128:(icc + 1) * 128], b[:, 0:128]),
                  wait=t_trt.cur(), sig=(v_qtT, 1))
        for hc in range(4):
            bc_d = bcast_ws(2, hc * 512)
            for ib_ in range(IB):
                gb = (TS * 4 + hc) * IB + ib_
                dqd = d_q8d[gb % 4]
                DMA("sync", lambda e, ib_=ib_, hc=hc, b=wdq[gb % 4]:
                    e.dma_start(out=b[:],
                                in_=qwd_sc[hc][:, ib_ * 2048:(ib_ + 1) * 2048]),
                    sig=(dqd, 16), wait=dn_led.get(gb - 4))
                dqd_led[gb] = (dqd.h, dqd.n)
                R.wait("tensor", dqd_led[gb][0], dqd_led[gb][1])
                if ib_ == 0:
                    R.wait("tensor", v_qtT, v_qtT.n)
                    R.wait("tensor", v_fin, v_fin.n)
                for sl4 in range(4):
                    icc = ib_ * 4 + sl4
                    for tj in range(4):
                        T(lambda t, icc=icc, tj=tj, gb=gb, sl4=sl4: t.matmul(
                            UB[tj][:], qtT[tj][:, icc * 128:(icc + 1) * 128],
                            wdq[gb % 4][:, sl4 * 512:(sl4 + 1) * 512],
                            start=(icc == 0), stop=(icc == IC - 1)),
                          sig=(t_dn, 1))
                dn_led[gb] = (t_dn.h, t_dn.n)
            tdn_now = t_dn.n
            for tj in range(4):
                rowb = TS * 512 + tj * 128
                gfin = (TS * 4 + hc) * 4 + tj
                dsxs = d_sx[tj % 2]
                dsss = d_ss[tj % 2]
                DMA("sync", lambda e, rowb=rowb, hc=hc, b=xrl[tj % 2]:
                    e.dma_start(out=b[:],
                                in_=d_in["xown"][rowb:rowb + 128,
                                                 hc * 512:(hc + 1) * 512]),
                    sig=(dsxs, 16), wait=v_fin.lag(1))
                DMA("sync", lambda e, rowb=rowb, hc=hc, b=srl[tj % 2]:
                    e.dma_start(out=b[:],
                                in_=s_sc[rowb:rowb + 128,
                                         hc * 512:(hc + 1) * 512]),
                    sig=(dsss, 16), wait=v_fin.lag(1))
                dss_now = (dsss, dsss.n)
                dsx_now = (dsxs, dsxs.n)
                R.wait("vector", t_dn, tdn_now)
                R.wait("vector", bc_d[1][0], bc_d[1][1])
                V(lambda v, tj=tj, p=bc_d[0]: v.tensor_tensor(
                    ayb[:], UB[tj][:], wsb[p][:], ALU.mult))
                S(lambda s, tj=tj, TS=TS: s.activation(
                    sgm8[:], ayb[:], AF.Sigmoid,
                    scale=tyf[:, TS * 4 + tj:TS * 4 + tj + 1]),
                  wait=(ch_v.h, ch_v.n), sig=(s_ffn, 1))
                V(lambda v, tj=tj, TS=TS: v.tensor_scalar_mul(
                    y2b[:], ayb[:], tyf[:, TS * 4 + tj:TS * 4 + tj + 1]))
                V(lambda v: v.tensor_tensor(y2b[:], y2b[:], sgm8[:], ALU.mult),
                  wait=s_ffn.cur())
                R.wait("vector", dss_now[0], dss_now[1])
                R.wait("vector", dsx_now[0], dsx_now[1])
                V(lambda v, tj=tj: v.tensor_tensor(y2b[:], y2b[:],
                                                   srl[tj % 2][:], ALU.mult))
                V(lambda v, tj=tj: v.tensor_tensor(outb[tj % 2][:], y2b[:],
                                                   xrl[tj % 2][:], ALU.add),
                  wait=y_led.get(gfin - 2), sig=(v_fin, 1))
                dys = d_y[gfin % 2]
                DMA("gpsimd", lambda e, rowb=rowb, hc=hc, b=outb[tj % 2]:
                    e.dma_start(out=y_out[rowb:rowb + 128,
                                          hc * 512:(hc + 1) * 512], in_=b[:]),
                    sig=(dys, 16), wait=v_fin.cur())
                y_led[gfin] = (dys.h, dys.n)
            bc_use_led[bc_state["n"] - 1] = (ch_v.h, ch_v.n)

    R.wait("gpsimd", d_y[0], d_y[0].n)
    R.wait("gpsimd", d_y[1], d_y[1].n)
    R.wait("sync", d_y[0], d_y[0].n)
    R.wait("sync", d_y[1], d_y[1].n)

    with nc.Block() as block:
        @block.sync
        def _(e):
            R.replay("sync", e)

        @block.vector
        def _(e):
            R.replay("vector", e)

        @block.scalar
        def _(e):
            R.replay("scalar", e)

        @block.gpsimd
        def _(e):
            R.replay("gpsimd", e)

        @block.tensor
        def _(e):
            R.replay("tensor", e)

    esC.close()
    es0.close()
    return nc


def _host_prepare(ins):
    x = np.ascontiguousarray(ins["L0_attn_hidden_states"], dtype=np.float32)
    w_i, w_f, w_g, w_o = (np.asarray(ins[f"L0_attn_w_{k}"], dtype=np.float32)
                          for k in "ifgo")
    down_idx = np.asarray(ins["L0_ffn_down_reorder_index"]).astype(np.int64)

    def strip_sw(w):
        t = w.reshape(KC, 128, KC, 128)
        t = np.transpose(t, (2, 1, 0, 3))
        return np.ascontiguousarray(t.reshape(KC, 128, D), dtype=np.float32)

    def vec_img(v):
        return np.ascontiguousarray(
            np.asarray(v, dtype=np.float32).reshape(KC, 128).T)

    shared = {
        "wi_sw": strip_sw(w_i), "wf_sw": strip_sw(w_f), "wg_sw": strip_sw(w_g),
        "woT_sw": np.ascontiguousarray(w_o.T.reshape(KC, 128, D),
                                       dtype=np.float32),
        "si_img": vec_img(ins["L0_attn_s_i"]),
        "sf_img": vec_img(ins["L0_attn_s_f"]),
        "sg_img": vec_img(ins["L0_attn_s_g"]),
        "so_img": vec_img(ins["L0_attn_s_o"]),
        "gnw_img": vec_img(ins["L0_attn_g_norm_w"]),
        "wgat_nat": np.ascontiguousarray(
            np.asarray(ins["L0_ffn_w_gate"], dtype=np.float32)[down_idx]
            .reshape(IC, 128, D)),
        "wup_nat": np.ascontiguousarray(
            np.asarray(ins["L0_ffn_w_up"], dtype=np.float32)[down_idx]
            .reshape(IC, 128, D)),
        "wdn_nat": np.ascontiguousarray(
            np.asarray(ins["L0_ffn_w_down"], dtype=np.float32)
            .reshape(KC, 128, I)),
    }
    zeros_pre = np.zeros((PRE, D), dtype=np.float32)
    in_maps = []
    for core in range(8):
        b, half = core // 2, core % 2
        m = dict(shared)
        m["xown"] = np.ascontiguousarray(x[b, half * TOK:(half + 1) * TOK, :])
        m["xpre"] = (np.ascontiguousarray(x[b, 0:PRE, :]) if half == 1
                     else zeros_pre)
        in_maps.append(m)
    return in_maps


def kernel(**inputs):
    in_maps = _host_prepare(inputs)
    nc = build_program()
    res = run_bass_kernel_spmd(nc, in_maps, core_ids=list(range(8)))
    out = np.empty((4, 2048, D), dtype=np.float32)
    for core in range(8):
        b, half = core // 2, core % 2
        out[b, half * TOK:(half + 1) * TOK, :] = np.asarray(res.results[core]["y"])
    return out


# version 20
# speedup vs baseline: 1.0316x; 1.0085x over previous
import numpy as np
import concourse.bass as bass
import concourse.mybir as mybir
from concourse.bass_utils import run_bass_kernel_spmd

F32 = mybir.dt.float32
BF16 = mybir.dt.bfloat16
FP8 = mybir.dt.float8e4
I32 = mybir.dt.int32
AF = mybir.ActivationFunctionType
ALU = mybir.AluOpType
AX = mybir.AxisListType

D = 2048
KC = D // 128
TOK = 1024
PRE = 1024
I = 5632
IC = I // 128
IB = I // 512
MAGIC = float(np.float32(1.5 * 2 ** 23))
EPS = 1e-5
RSQRT_C = 0x5F3759DF
FQ = 4.0 / (448.0 * 448.0)


class SemC:
    def __init__(self, h):
        self.h = h
        self.n = 0

    def inc(self, k=1):
        self.n += k
        return (self.h, self.n)

    def cur(self):
        return (self.h, self.n)

    def lag(self, k):
        return (self.h, max(0, self.n - k))


class Rec:
    CHAINED = ("vector", "scalar", "gpsimd")

    def __init__(self):
        self.st = {e: [] for e in ("sync", "vector", "scalar", "gpsimd", "tensor")}
        self.last = {e: None for e in self.st}

    def emit(self, eng, fn, wait=None, sig=None, chain_sem=None, nochain=False):
        if eng in self.CHAINED and wait is not None:
            if wait[1] > 0:
                self.st[eng].append(("W", wait[0], wait[1]))
            wait = None
        last = self.last[eng] if (eng in self.CHAINED and not nochain) else None
        if sig is not None:
            sem, amount = sig
            tgt = sem.inc(amount)
            upd = (tgt[0], amount)
        elif eng in self.CHAINED and chain_sem is not None:
            tgt = chain_sem.inc(1)
            upd = (tgt[0], 1)
        else:
            tgt, upd = None, None
        self.st[eng].append(("I", fn, wait, last, upd))
        if eng in self.CHAINED and tgt is not None and not nochain:
            self.last[eng] = tgt
        return tgt

    def wait(self, eng, sem, val=None):
        v = sem.n if val is None else val
        h = sem.h if hasattr(sem, "h") else sem
        if v > 0:
            self.st[eng].append(("W", h, v))

    def replay(self, eng, engine):
        for item in self.st[eng]:
            if item[0] == "W":
                engine.wait_ge(item[1], item[2])
                continue
            _, fn, wait, last, upd = item
            ins = fn(engine)
            if wait is not None and wait[1] > 0:
                ins._wait_ge(wait[0], wait[1])
            elif last is not None:
                ins._wait_ge(last[0], last[1])
            if upd is not None:
                ins.then_inc(upd[0], upd[1])


def build_program():
    nc = bass.Bass("TRN2", target_bir_lowering=False, debug=False)

    d_in = {}
    for name, shape in [
        ("xpre", [PRE, D]), ("xown", [TOK, D]),
        ("wi_sw", [KC, 128, D]), ("wf_sw", [KC, 128, D]),
        ("wg_sw", [KC, 128, D]), ("woT_sw", [KC, 128, D]),
        ("si_img", [128, KC]), ("sf_img", [128, KC]), ("sg_img", [128, KC]),
        ("so_img", [128, KC]), ("gnw_img", [128, KC]),
        ("wgat_nat", [IC, 128, D]), ("wup_nat", [IC, 128, D]),
        ("wdn_nat", [KC, 128, I]),
    ]:
        d_in[name] = nc.dram_tensor(name, shape, F32, kind="ExternalInput").ap()
    y_out = nc.dram_tensor("y", [TOK, D], F32, kind="ExternalOutput").ap()

    qwg_sc = nc.dram_tensor("qwg_sc", [IB, 128, KC * 512], FP8, kind="Internal").ap()
    qwu_sc = nc.dram_tensor("qwu_sc", [IB, 128, KC * 512], FP8, kind="Internal").ap()
    qwd_sc = nc.dram_tensor("qwd_sc", [4, 128, IC * 512], FP8, kind="Internal").ap()
    woT_bf = nc.dram_tensor("woT_bf", [KC, 128, D], BF16, kind="Internal").ap()
    s_sc = nc.dram_tensor("s_sc", [TOK, D], F32, kind="Internal").ap()
    h_sc = nc.dram_tensor("h_sc", [KC, 128, TOK], F32, kind="Internal").ap()
    ws_dr = nc.dram_tensor("ws_dr", [3, IC, 128], F32, kind="Internal").ap()

    from contextlib import ExitStack
    es0 = ExitStack()

    def mk(stack):
        def sb(name, shape, dtype):
            return stack.enter_context(nc.sbuf_tensor(name, shape, dtype))

        def ps(name, shape, dtype=F32):
            return stack.enter_context(nc.psum_tensor(name, shape, dtype))
        return sb, ps

    sb0, ps0 = mk(es0)

    def sem(name):
        return SemC(es0.enter_context(nc.semaphore(name)))

    si = sb0("si", [128, KC], F32)
    sf = sb0("sf", [128, KC], F32)
    nsf = sb0("nsf", [128, KC], F32)
    sgv = sb0("sgv", [128, KC], F32)
    sov = sb0("sov", [128, KC], F32)
    gnw = sb0("gnw", [128, KC], F32)
    gso = sb0("gso", [128, KC], F32)
    rmax = sb0("rmax", [128, 8], F32)
    sgf = sb0("sgf", [128, 8], F32)
    sg2 = sb0("sg2", [128, 8], F32)
    tyf = sb0("tyf", [128, 8], F32)
    identb = sb0("identb", [128, 128], BF16)
    ident8 = sb0("ident8", [128, 128], FP8)
    identf = sb0("identf", [128, 128], F32)
    ones_col = sb0("ones_col", [128, 1], F32)
    ones_row = sb0("ones_row", [1, 128], F32)
    row_t = sb0("row_t", [1, 512], F32)
    wsg_cols = sb0("wsg_cols", [128, IC], F32)
    wsu_cols = sb0("wsu_cols", [128, IC], F32)
    wsd_cols = sb0("wsd_cols", [128, KC], F32)
    qxT3 = sb0("qxT3", [128, KC, TOK], FP8)
    ssum = sb0("ssum", [1, TOK], F32)
    rstd_r = sb0("rstd_r", [1, TOK], F32)
    ir1 = sb0("ir1", [1, TOK], I32)
    ir2 = sb0("ir2", [1, TOK], I32)
    wmaxc = sb0("wmaxc", [128, 1], F32)
    wrec = sb0("wrec", [128, 1], F32)
    wn1 = sb0("wn1", [128, 1], F32)
    wn2 = sb0("wn2", [128, 1], F32)
    li1 = sb0("li1", [128, 1], I32)
    li2 = sb0("li2", [128, 1], I32)
    gm1 = sb0("gm1", [128, 1], F32)
    gm2 = sb0("gm2", [128, 1], F32)
    wst_sb = sb0("wst_sb", [IC, 128], F32)
    wtmp = sb0("wtmp", [128, IC], F32)
    wtmp2 = sb0("wtmp2", [128, IC], F32)
    lnm_c = sb0("lnm_c", [128, 16], F32)
    lsq_c = sb0("lsq_c", [128, 16], F32)
    lrs_c = sb0("lrs_c", [128, 16], F32)
    lv1_c = sb0("lv1_c", [128, 16], F32)
    lv2_c = sb0("lv2_c", [128, 16], F32)
    lvi_c = sb0("lvi_c", [128, 16], I32)
    lvi2_c = sb0("lvi2_c", [128, 16], I32)
    pbank = [ps0(f"pbank{i_}", [128, 512], F32) for i_ in range(8)]

    ch_v, ch_s, ch_g = sem("ch_v"), sem("ch_s"), sem("ch_g")
    d_small = sem("d_small")
    d_x = [sem(f"d_x{i_}") for i_ in range(4)]
    d_wif = [sem(f"d_wif{i_}") for i_ in range(4)]
    d_wg = [sem(f"d_wg{i_}") for i_ in range(2)]
    d_wo2 = [sem(f"d_wo{i_}") for i_ in range(2)]
    d_nat = [sem(f"d_nat{i_}") for i_ in range(3)]
    d_natd = [sem(f"d_natd{i_}") for i_ in range(2)]
    d_natd2 = sem("d_natd2")
    d_scr2 = [sem(f"d_scr{i_}") for i_ in range(2)]
    d_scrd4 = [sem(f"d_scrd{i_}") for i_ in range(4)]
    d_q8 = [[sem(f"d_q8_{w_}{i_}") for i_ in range(3)] for w_ in range(2)]
    d_q8d = [sem(f"d_q8d{i_}") for i_ in range(4)]
    d_wc = sem("d_wc")
    d_sx = [sem(f"d_sx{i_}") for i_ in range(2)]
    d_ss = [sem(f"d_ss{i_}") for i_ in range(2)]
    d_so = [sem(f"d_so{i_}") for i_ in range(2)]
    d_hsp = [sem(f"d_hsp{i_}") for i_ in range(3)]
    d_hrl = [sem(f"d_hrl{i_}") for i_ in range(3)]
    d_y = [sem(f"d_y{i_}") for i_ in range(2)]
    d_wsr = sem("d_wsr")
    d_row = [sem(f"d_row{i_}") for i_ in range(2)]
    g_id = sem("g_id")
    g_pmax = sem("g_pmax")
    v_misc = sem("v_misc")
    v_ln = sem("v_ln")
    s_ln = sem("s_ln")
    v_x8T = sem("v_x8T")
    v_qxT = sem("v_qxT")
    s_x8T = sem("s_x8T")
    s_qxT = sem("s_qxT")
    t_trx = sem("t_trx")
    s_pq = sem("s_pq")
    t_ptr = sem("t_ptr")
    v_pev = sem("v_pev")
    s_pev = sem("s_pev")
    t_if = sem("t_if")
    s_att = sem("s_att")
    v_att = sem("v_att")
    v_scan = sem("v_scan")
    s_hsq = sem("s_hsq")
    t_h2 = sem("t_h2")
    t_bc = sem("t_bc")
    v_rb = sem("v_rb")
    t_g = sem("t_g")
    s_epi = sem("s_epi")
    v_epi = sem("v_epi")
    t_op = sem("t_op")
    v_sev = sem("v_sev")
    t_gu = sem("t_gu")
    v_ffn = sem("v_ffn")
    s_ffn = sem("s_ffn")
    v_tmax = sem("v_tmax")
    s_qt = sem("s_qt")
    t_trt = sem("t_trt")
    v_qtT = sem("v_qtT")
    t_dn = sem("t_dn")
    v_fin = sem("v_fin")
    t_wsb = sem("t_wsb")
    t_g4 = sem("t_g4")
    t_u4 = sem("t_u4")
    v_wsb = sem("v_wsb")

    R = Rec()

    def V(fn, wait=None, sig=None):
        return R.emit("vector", fn, wait=wait, sig=sig, chain_sem=ch_v)

    def S(fn, wait=None, sig=None):
        return R.emit("scalar", fn, wait=wait, sig=sig, chain_sem=ch_s)

    def G(fn, wait=None, sig=None):
        return R.emit("gpsimd", fn, wait=wait, sig=sig, chain_sem=ch_g)

    def T(fn, wait=None, sig=None):
        return R.emit("tensor", fn, wait=wait, sig=sig)

    def DMA(eng, fn, sig, wait=None):
        return R.emit(eng, fn, wait=wait, sig=sig, nochain=True)


    def recip_nr(dst, src, f1):
        V(lambda v: v.reciprocal(dst, src))
        V(lambda v: v.tensor_tensor(f1, dst, src, ALU.mult))
        V(lambda v: v.tensor_scalar(f1, f1, -1.0, 2.0, ALU.mult, ALU.add))
        V(lambda v: v.tensor_tensor(dst, dst, f1, ALU.mult))

    def rsqrt_nr(dst, src, it1, it2, f1, f2):
        V(lambda v: v.tensor_scalar(it1, src.bitcast(I32), 1, None,
                                    ALU.logical_shift_right))
        V(lambda v: v.tensor_scalar(it2, it1, -1, RSQRT_C, ALU.mult, ALU.add))
        V(lambda v: v.tensor_copy(dst, it2.bitcast(F32)))
        for _ in range(3):
            V(lambda v: v.tensor_tensor(f1, dst, dst, ALU.mult))
            V(lambda v: v.tensor_tensor(f2, f1, src, ALU.mult))
            V(lambda v: v.tensor_scalar(f2, f2, -0.5, 1.5, ALU.mult, ALU.add))
            V(lambda v: v.tensor_tensor(dst, dst, f2, ALU.mult))

    for tile, name in [(si, "si_img"), (sf, "sf_img"), (sgv, "sg_img"),
                       (sov, "so_img"), (gnw, "gnw_img")]:
        DMA("sync", lambda e, t=tile, n=name: e.dma_start(out=t[:], in_=d_in[n]),
            sig=(d_small, 16))
    for ident in (identb, ident8, identf):
        G(lambda g, t=ident: g.memset(t[:], 0.0))
        G(lambda g, t=ident: g.affine_select(
            out=t[:], in_=t[:], compare_op=ALU.not_equal, fill=1.0, base=0,
            pattern=[[-1, 128]], channel_multiplier=1))
    G(lambda g: g.engine_nop(), sig=(g_id, 1))
    V(lambda v: v.memset(ones_col[:], 1.0))
    V(lambda v: v.memset(ones_row[:], 1.0))
    R.wait("vector", d_small, 16 * 5)
    V(lambda v: v.tensor_mul(gso[:], gnw[:], sov[:]))
    V(lambda v: v.tensor_scalar_mul(nsf[:], sf[:], -1.0), sig=(v_misc, 1))

    esX = ExitStack()
    sbX, _ = mk(esX)
    x8T3 = sbX("x8T3", [128, KC, PRE + TOK], BF16)

    esP = ExitStack()
    sbP, psP = mk(esP)
    natb = [sbP(f"natb{i_}", [128, D], F32) for i_ in range(3)]
    wmax2 = [sbP(f"wmax2{i_}", [128, 1], F32) for i_ in range(2)]
    wrec2 = [sbP(f"wrec2{i_}", [128, 1], F32) for i_ in range(2)]
    wifb = [sbP(f"wifb{i_}", [128, D], BF16) for i_ in range(6)]
    q8n = [sbP(f"q8n{i_}", [128, D], FP8) for i_ in range(8)]
    pstg = [sbP(f"pstg{i_}", [128, 2048], FP8) for i_ in range(2)]
    pp_tr = [pbank[2], pbank[3]]

    esL = ExitStack()
    sbL, psL = mk(esL)
    xint = [sbL(f"xint{i_}", [128, D], F32) for i_ in range(2)]
    xnt = [sbL("xnt0", [128, D], F32)]
    x8s = [sbL(f"x8s{i_}", [128, D], BF16) for i_ in range(2)]
    q8s = [sbL(f"q8s{i_}", [128, D], FP8) for i_ in range(2)]
    lsum = sbL("lsum", [128, 1], F32)
    wrecl = [sbL(f"wrecl{i_}", [128, 1], F32) for i_ in range(2)]
    x4 = [xint[0], xint[1], xnt[0]]
    lssq = sbL("lssq", [128, 1], F32)
    lvar = sbL("lvar", [128, 1], F32)
    lrstd = sbL("lrstd", [128, 1], F32)
    lnm = sbL("lnm", [128, 1], F32)
    lrm = sbL("lrm", [128, 1], F32)
    pb_bf = [pbank[0], pbank[1]]
    pq_f = [pbank[2], pbank[3]]

    gu_groups = [("g", gi) for gi in range(IB)] + [("u", gi) for gi in range(IB)]

    prep_c = {"n": 0}
    tpr_led = {}
    prep_scr_led = {}

    def emit_prep_quant(kind, gi, gidx):
        nat = d_in["wgat_nat"] if kind == "g" else d_in["wup_nat"]
        cols = wsg_cols if kind == "g" else wsu_cols
        for j in range(4):
            c = gi * 4 + j
            gc = prep_c["n"]
            prep_c["n"] = gc + 1
            buf = natb[gc % 3]
            dns = d_nat[gc % 3]
            wmx = wmax2[gc % 2]
            wrc = wrec2[gc % 2]
            DMA("scalar", lambda e, b=buf, n=nat, c=c: e.dma_start(
                out=b[:], in_=n[c]), sig=(dns, 16), wait=s_pq.lag(2))
            R.wait("vector", dns, dns.n)
            R.wait("vector", s_pq, s_pq.lag(1)[1])
            V(lambda v, b=buf, m=wmx: v.reduce_max(m[:], b[:], axis=AX.X,
                                                   apply_absolute_value=True))
            V(lambda v, c=c, cl=cols, m=wmx: v.tensor_copy(cl[:, c:c + 1], m[:]),
              sig=(g_pmax, 1))
            recip_nr(wrc[:], wmx[:], wn1[:])
            V(lambda v, r=wrc: v.tensor_scalar_mul(r[:], r[:], 224.0),
              sig=(v_misc, 1))
            prior_tr = tpr_led.get(gidx - 2)
            if prior_tr is not None:
                R.wait("vector", t_ptr, prior_tr)
            V(lambda v, b=buf, sl=(gidx % 2) * 4 + j, r=wrc: v.tensor_scalar(
                q8n[sl][:], b[:], r[:], None, ALU.mult), sig=(s_pq, 1))

    def emit_prep_tr(kind, gi, gidx, spq_tgt):
        sc = qwg_sc if kind == "g" else qwu_sc
        s_start = s_pev.n
        evac_led = {}
        scr_led = prep_scr_led
        for quarter in range(4):
            for kk in range(4):
                k = quarter * 4 + kk
                bank = pp_tr[k % 2]
                for j in range(4):
                    w = None
                    if j == 0:
                        if k == 0:
                            R.wait("tensor", v_qxT, v_qxT.n)
                            R.wait("tensor", s_qxT, s_qxT.n)
                            w = (s_pq.h, spq_tgt)
                        elif k == 1:
                            w = (s_pev.h, s_start)
                        else:
                            w = evac_led[k - 2]
                    T(lambda t, j=j, k=k, b=bank, sl=(gidx % 2) * 4 + j:
                      t.matmul(
                        b[:, j * 128:(j + 1) * 128],
                        q8n[sl][:, k * 128:(k + 1) * 128],
                        ident8[:], start=True, stop=True),
                      wait=w, sig=(t_ptr, 1) if j == 3 else None)
                qg = gidx * 4 + quarter
                pbuf = pstg[qg % 2]
                if kk == 0 and qg >= 2:
                    wsem, wval = scr_led[qg - 2]
                    R.wait("vector", wsem, wval)
                    R.wait("scalar", wsem, wval)
                if k % 2 == 0:
                    V(lambda v, kk=kk, b=bank, pb=pbuf: v.tensor_copy(
                        pb[:, kk * 512:(kk + 1) * 512], b[:]),
                      wait=t_ptr.cur(), sig=(v_pev, 1))
                    evac_led[k] = (v_pev.h, v_pev.n)
                else:
                    S(lambda s_, kk=kk, b=bank, pb=pbuf: s_.copy(
                        pb[:, kk * 512:(kk + 1) * 512], b[:]),
                      wait=t_ptr.cur(), sig=(s_pev, 1))
                    evac_led[k] = (s_pev.h, s_pev.n)
            R.wait("sync", s_pev, s_pev.n)
            R.wait("sync", v_pev, v_pev.n)
            DMA("sync", lambda e, sc=sc, gi=gi, q=quarter,
                pb=pstg[(gidx * 4 + quarter) % 2]: e.dma_start(
                out=sc[gi][:, q * 2048:(q + 1) * 2048], in_=pb[:]),
                sig=(d_scr2[(gidx * 4 + quarter) % 2], 16))
            scr_led[gidx * 4 + quarter] = (
                d_scr2[(gidx * 4 + quarter) % 2], d_scr2[(gidx * 4 + quarter) % 2].n)
        tpr_led[gidx] = t_ptr.n

    prep_pending = []

    def emit_prep_group(kind, gi):
        gidx = len(prep_pending)
        if gidx >= 1:
            emit_prep_tr(*prep_pending[gidx - 1])
        emit_prep_quant(kind, gi, gidx)
        prep_pending.append((kind, gi, gidx, s_pq.n))

    def flush_prep():
        for gidx in range(len(prep_pending)):
            if gidx not in tpr_led:
                emit_prep_tr(*prep_pending[gidx])

    ln_end = []
    ln_x8_led = []
    x8_bank_led = {}
    qx_bank_led = {}
    trx_end = []

    def emit_ln_pass1(j):
        buf = x4[j % 3]
        src = d_in["xpre"] if j < 8 else d_in["xown"]
        row = (j % 8) * 128
        dxs = d_x[j % 3]
        DMA("sync", lambda e, b=buf, s=src, r=row: e.dma_start(
            out=b[:], in_=s[r:r + 128, :]), sig=(dxs, 16),
            wait=ln_end[j - 3] if j >= 3 else None)
        R.wait("vector", dxs, dxs.n)
        R.wait("scalar", dxs, dxs.n)
        V(lambda v, b=buf: v.reduce_sum(lsum[:], b[:], axis=AX.X))
        V(lambda v, j=j: v.tensor_scalar_mul(lnm_c[:, j:j + 1], lsum[:],
                                             -1.0 / D), sig=(v_ln, 1))
        S(lambda s, b=buf: s.activation(b[:], b[:], AF.Square,
                                        accum_out=lssq[:]), wait=v_ln.cur())
        S(lambda s, j=j: s.copy(lsq_c[:, j:j + 1], lssq[:]),
          sig=(s_ln, 1))
        ln_end.append((s_ln.h, s_ln.n))

    def emit_ln_stats():
        R.wait("vector", v_ln, v_ln.n)
        R.wait("vector", s_ln, s_ln.n)
        V(lambda v: v.tensor_tensor(lv1_c[:], lnm_c[:], lnm_c[:], ALU.mult))
        V(lambda v: v.tensor_scalar_mul(lv1_c[:], lv1_c[:], -1.0))
        V(lambda v: v.scalar_tensor_tensor(lv1_c[:], lsq_c[:], 1.0 / D,
                                           lv1_c[:], ALU.mult, ALU.add))
        V(lambda v: v.tensor_scalar_add(lv1_c[:], lv1_c[:], EPS))
        rsqrt_nr(lrs_c[:], lv1_c[:], lvi_c[:], lvi2_c[:], lv2_c[:], wtmp[:, 0:16])
        V(lambda v: v.engine_nop(), sig=(v_misc, 1))

    q8s_led = {}

    def emit_ln_pass2A(j):
        buf = x4[j % 3]
        src = d_in["xpre"] if j < 8 else d_in["xown"]
        row = (j % 8) * 128
        own = j >= 8
        dxs = d_x[j % 3]
        DMA("sync", lambda e, b=buf, s=src, r=row: e.dma_start(
            out=b[:], in_=s[r:r + 128, :]), sig=(dxs, 16),
            wait=ln_end[16 + j - 3] if j >= 3 else
            ln_end[{0: 15, 1: 13, 2: 14}[j % 3]])
        R.wait("vector", dxs, dxs.n)
        R.wait("scalar", dxs, dxs.n)
        V(lambda v, b=buf, j=j: v.tensor_scalar(
            b[:], b[:], lnm_c[:, j:j + 1], lrs_c[:, j:j + 1],
            ALU.add, ALU.mult), sig=(v_ln, 1))
        if own:
            wl = wrecl[j % 2]
            V(lambda v, b=buf: v.reduce_max(lrm[:], b[:], axis=AX.X,
                                            apply_absolute_value=True))
            V(lambda v, j=j: v.tensor_copy(rmax[:, j - 8:j - 7], lrm[:]))
            if j >= 10:
                R.wait("vector", s_ln, q8s_led[j - 2])
            recip_nr(wl[:], lrm[:], wn1[:])
            V(lambda v, w=wl: v.tensor_scalar_mul(w[:], w[:], 224.0),
              sig=(v_misc, 1))
            if j >= 10:
                R.wait("scalar", t_trx, trx_end[j - 2])
            S(lambda s, b=buf, bq=j % 2, w=wl: s.activation(
                q8s[bq][:], b[:], AF.Copy, scale=w[:]),
              wait=v_misc.cur(), sig=(s_ln, 1))
        q8s_led[j] = s_ln.n

    def emit_ln_pass2B(j):
        buf = x4[j % 3]
        tcol = j * 128
        own = j >= 8
        V(lambda v, b=buf: v.tensor_scalar(b[:], b[:], 256.0, MAGIC,
                                           ALU.mult, ALU.add),
          wait=(s_ln.h, q8s_led[j]) if own else None, sig=(v_ln, 1))
        if j >= 2:
            R.wait("scalar", t_trx, trx_end[j - 2])
        S(lambda s, b=buf, bx=j % 2: s.activation(
            x8s[bx][:], b[:], AF.Copy, scale=1.0 / 256.0, bias=-MAGIC / 256.0),
          wait=v_ln.cur(), sig=(s_ln, 1))
        sln_x8 = s_ln.n
        gb8 = j * 4
        gbq = (j - 8) * 4
        for kg in range(4):
            bank = pb_bf[kg % 2]
            g = gb8 + kg
            for dk in range(4):
                k = kg * 4 + dk
                w = None
                if dk == 0:
                    w = x8_bank_led.get(g - 2)
                    if kg == 0:
                        w = (s_ln.h, sln_x8)
                        if g >= 2:
                            bw = x8_bank_led[g - 2]
                            R.wait("tensor", bw[0], bw[1])
                T(lambda t, k=k, b=bank, bf=j % 2, dk=dk: t.matmul(
                    b[:, dk * 128:(dk + 1) * 128],
                    x8s[bf][:, k * 128:(k + 1) * 128], identb[:],
                    start=True, stop=True),
                  wait=w, sig=(t_trx, 1) if dk == 3 else None)
            if kg % 2 == 0:
                V(lambda v, kg=kg, b=bank, tc=tcol: v.tensor_copy(
                    x8T3[:, kg * 4:(kg + 1) * 4, tc:tc + 128], b[:]),
                  wait=t_trx.cur(), sig=(v_x8T, 1))
                x8_bank_led[g] = (v_x8T.h, v_x8T.n)
            else:
                S(lambda s_, kg=kg, b=bank, tc=tcol: s_.copy(
                    x8T3[:, kg * 4:(kg + 1) * 4, tc:tc + 128], b[:]),
                  wait=t_trx.cur(), sig=(s_x8T, 1))
                x8_bank_led[g] = (s_x8T.h, s_x8T.n)
        if own:
            for kg in range(4):
                qbank = pq_f[kg % 2]
                g = gbq + kg
                for dk in range(4):
                    k = kg * 4 + dk
                    w = None
                    if dk == 0:
                        w = qx_bank_led.get(g - 2)
                        if kg == 0:
                            w = (s_ln.h, q8s_led[j])
                            R.wait("tensor", v_pev, v_pev.n)
                            R.wait("tensor", s_pev, s_pev.n)
                            if g >= 2:
                                bw = qx_bank_led[g - 2]
                                R.wait("tensor", bw[0], bw[1])
                    T(lambda t, k=k, b=qbank, bq=j % 2, dk=dk: t.matmul(
                        b[:, dk * 128:(dk + 1) * 128],
                        q8s[bq][:, k * 128:(k + 1) * 128], ident8[:],
                        start=True, stop=True),
                      wait=w, sig=(t_trx, 1) if dk == 3 else None)
                if kg % 2 == 0:
                    V(lambda v, kg=kg, b=qbank, j=j: v.tensor_copy(
                        qxT3[:, kg * 4:(kg + 1) * 4, (j - 8) * 128:(j - 7) * 128],
                        b[:]), wait=t_trx.cur(), sig=(v_qxT, 1))
                    qx_bank_led[g] = (v_qxT.h, v_qxT.n)
                else:
                    S(lambda s_, kg=kg, b=qbank, j=j: s_.copy(
                        qxT3[:, kg * 4:(kg + 1) * 4, (j - 8) * 128:(j - 7) * 128],
                        b[:]), wait=t_trx.cur(), sig=(s_qxT, 1))
                    qx_bank_led[g] = (s_qxT.h, s_qxT.n)
        ln_end.append((s_ln.h, s_ln.n))
        trx_end.append(t_trx.n)
        ln_x8_led.append(((v_x8T.h, v_x8T.n), (s_x8T.h, s_x8T.n)))

    R.wait("tensor", g_id, 1)
    gu_i = 0
    for j in range(16):
        emit_ln_pass1(j)
    emit_ln_stats()
    for j in range(16):
        emit_ln_pass2A(j)
        if j >= 1:
            emit_ln_pass2B(j - 1)
    emit_ln_pass2B(15)
    esL.close()

    esA = ExitStack()
    sbA, psA = mk(esA)
    fbuf = sbA("fbuf", [128, 1024], F32)
    f2buf = sbA("f2buf", [128, 1024], F32)
    hpre = sbA("hpre", [128, PRE], F32)
    hsqb = [sbA(f"hsqb{i_}", [128, 1024], F32) for i_ in range(3)]
    Hrot = [sbA(f"Hrot{i_}", [128, TOK], F32) for i_ in range(3)]
    rstd_b = hsqb[0]
    pi_ps = [pbank[4], pbank[5]]
    pf_ps = [pbank[6], pbank[7]]
    ph2 = [pbank[0], pbank[1]]

    hsp_led = {}
    hrl_led = {}

    def emit_h2_sum(cc):
        R.wait("tensor", s_hsq, cc + 1)
        if cc == 0:
            R.wait("tensor", v_x8T, v_x8T.n)
            R.wait("tensor", s_x8T, s_x8T.n)
        for hh in range(2):
            T(lambda t, cc=cc, hh=hh: t.matmul(
                ph2[hh][0:1, :], ones_col[:],
                hsqb[cc % 3][:, hh * 512:(hh + 1) * 512],
                start=(cc == 0), stop=(cc == KC - 1), skip_group_check=True),
              sig=(t_h2, 1))

    def emit_attn_chunk(cc):
        bi, bf_ = wifb[(cc % 3) * 2], wifb[(cc % 3) * 2 + 1]
        dws = d_wif[cc % 3]
        wdma = (None if cc < 3 else t_if.lag(16))
        DMA("gpsimd", lambda e, cc=cc, b=bi: e.dma_start(
            out=b[:], in_=d_in["wi_sw"][cc]), sig=(dws, 16), wait=wdma)
        DMA("gpsimd", lambda e, cc=cc, b=bf_: e.dma_start(
            out=b[:], in_=d_in["wf_sw"][cc]), sig=(dws, 16))
        need_w = (dws, dws.n)
        for seg in range(2):
            fmm_led = None
            for wsel, wbuf, pss in ((1, bf_, pf_ps), (0, bi, pi_ps)):
                for hh in range(2):
                    colbase = seg * 1024 + hh * 512
                    bank = pss[hh]
                    for k in range(KC):
                        w = None
                        if k == 0:
                            R.wait("tensor", need_w[0], need_w[1])
                            blk = seg * 8 + hh * 4 + 3
                            if cc == 0:
                                R.wait("tensor", ln_x8_led[blk][0][0],
                                       ln_x8_led[blk][0][1])
                            R.wait("tensor", ln_x8_led[blk][1][0],
                                       ln_x8_led[blk][1][1])
                            w = v_att.cur() if wsel == 0 else s_att.cur()
                        T(lambda t, k=k, b=bank, wb=wbuf, cb=colbase: t.matmul(
                            b[:], wb[:, k * 128:(k + 1) * 128],
                            x8T3[:, k, cb:cb + 512],
                            start=(k == 0), stop=(k == KC - 1)),
                          wait=w if k == 0 else None,
                          sig=(t_if, 1) if k == KC - 1 else None)
                if wsel == 1:
                    fmm_led = (t_if.h, t_if.n)
            tif_now = t_if.n
            R.wait("scalar", t_if, fmm_led[1])
            R.wait("scalar", v_scan, v_scan.n)
            for hh in range(2):
                S(lambda s, cc=cc, hh=hh: s.activation(
                    fbuf[:, hh * 512:(hh + 1) * 512], pf_ps[hh][:], AF.Sigmoid,
                    scale=sf[:, cc:cc + 1]))
                S(lambda s, cc=cc, hh=hh: s.activation(
                    f2buf[:, hh * 512:(hh + 1) * 512], pf_ps[hh][:], AF.Sigmoid,
                    scale=nsf[:, cc:cc + 1]), sig=(s_att, 1))
            satt_now = s_att.n
            R.wait("vector", t_if, tif_now)
            R.wait("vector", s_att, satt_now)
            for hh in range(2):
                V(lambda v, cc=cc, hh=hh: v.scalar_tensor_tensor(
                    f2buf[:, hh * 512:(hh + 1) * 512], pi_ps[hh][:],
                    si[:, cc:cc + 1], f2buf[:, hh * 512:(hh + 1) * 512],
                    ALU.mult, ALU.mult), sig=(v_att, 1))
            if seg == 0:
                V(lambda v: v.tensor_tensor_scan(
                    hpre[:], fbuf[:], f2buf[:], 0.0, ALU.mult, ALU.add),
                  sig=(v_scan, 1))
            else:
                if cc >= 3:
                    R.wait("vector", d_hsp[cc % 3], hsp_led[cc - 3][1])
                V(lambda v, cc=cc: v.tensor_tensor_scan(
                    Hrot[cc % 3][:], fbuf[:], f2buf[:], hpre[:, PRE - 1:PRE],
                    ALU.mult, ALU.add), sig=(v_scan, 1))
        R.wait("scalar", t_h2, t_h2.n)
        S(lambda s, cc=cc: s.activation(hsqb[cc % 3][:], Hrot[cc % 3][:],
                                        AF.Square),
          wait=v_scan.cur(), sig=(s_hsq, 1))
        dhs = d_hsp[cc % 3]
        DMA("sync", lambda e, cc=cc: e.dma_start(
            out=h_sc[cc], in_=Hrot[cc % 3][:]), sig=(dhs, 16),
            wait=s_hsq.cur())
        hsp_led[cc] = (dhs.h, dhs.n)
        if cc >= 2:
            emit_h2_sum(cc - 2)

    for cc in range(KC):
        emit_attn_chunk(cc)
        if gu_i < len(gu_groups):
            emit_prep_group(*gu_groups[gu_i])
            gu_i += 1
        if cc < 6 and gu_i < len(gu_groups):
            emit_prep_group(*gu_groups[gu_i])
            gu_i += 1
    emit_h2_sum(KC - 2)
    emit_h2_sum(KC - 1)
    while gu_i < len(gu_groups):
        emit_prep_group(*gu_groups[gu_i])
        gu_i += 1
    flush_prep()

    def emit_ws_row(cols, nchunks, dr_idx):
        n = nchunks
        R.wait("vector", g_pmax, g_pmax.n)
        R.wait("vector", t_ptr, t_ptr.n)
        V(lambda v, n=n, cl=cols: v.tensor_copy(wtmp[:, 0:n], cl[:, 0:n]),
          sig=(v_misc, 1))
        T(lambda t, n=n: t.matmul(pp_tr[0][0:n, 0:128], wtmp[:, 0:n],
                                  identf[:], start=True, stop=True),
          wait=v_misc.cur(), sig=(t_ptr, 1))
        R.wait("vector", d_wsr, d_wsr.n)
        V(lambda v, n=n: v.tensor_copy(wst_sb[0:n, :], pp_tr[0][0:n, 0:128]),
          wait=t_ptr.cur(), sig=(v_pev, 1))
        DMA("sync", lambda e, n=n, di=dr_idx: e.dma_start(
            out=ws_dr[di][0:n, :], in_=wst_sb[0:n, :]), sig=(d_wsr, 16),
            wait=v_pev.cur())

    emit_ws_row(wsg_cols, IC, 0)
    emit_ws_row(wsu_cols, IC, 1)

    R.wait("vector", t_h2, t_h2.n)
    V(lambda v: v.tensor_copy(ssum[0:1, 0:512], ph2[0][0:1, :]))
    V(lambda v: v.tensor_copy(ssum[0:1, 512:1024], ph2[1][0:1, :]))
    V(lambda v: v.tensor_scalar(ssum[:], ssum[:], 1.0 / D, EPS, ALU.mult,
                                ALU.add))
    rsqrt_nr(rstd_r[:], ssum[:], ir1[:], ir2[:], hpre[0:1, :], rstd_b[0:1, :])
    V(lambda v: v.engine_nop(), sig=(v_misc, 1))
    vm_rstd = v_misc.n
    for hh in range(2):
        T(lambda t, hh=hh: t.matmul(
            ph2[hh][:, :], ones_row[:], rstd_r[0:1, hh * 512:(hh + 1) * 512],
            start=True, stop=True, skip_group_check=True),
          wait=(v_misc.h, vm_rstd), sig=(t_bc, 1))
        V(lambda v, hh=hh: v.tensor_copy(
            rstd_b[:, hh * 512:(hh + 1) * 512], ph2[hh][:, :]),
          wait=t_bc.cur(), sig=(v_rb, 1))

    sgbuf = fbuf
    g2buf = f2buf
    o1buf = hpre
    R.wait("scalar", v_scan, v_scan.n)

    def emit_epi_chunk(cc):
        b = wifb[cc % 2]
        dgs = d_wg[cc % 2]
        wdma = (t_if.cur() if cc < 2 else t_g.lag(2))
        DMA("gpsimd", lambda e, cc=cc, b=b: e.dma_start(
            out=b[:], in_=d_in["wg_sw"][cc]), sig=(dgs, 16), wait=wdma)
        need = (dgs, dgs.n)
        hb = Hrot[cc % 3]
        dhr = d_hrl[cc % 3]
        last_spill = {0: 15, 1: 13, 2: 14}[cc % 3]
        DMA("sync", lambda e, cc=cc, h=hb: e.dma_start(
            out=h[:], in_=h_sc[cc]), sig=(dhr, 16),
            wait=(hsp_led[last_spill] if cc < 3 else v_epi.lag(2)))
        hrl_led[cc] = (dhr.h, dhr.n)
        bankset = pi_ps if cc % 2 == 0 else pf_ps
        for hh in range(2):
            bank = bankset[hh]
            for k in range(KC):
                w = None
                if k == 0:
                    R.wait("tensor", need[0], need[1])
                    w = (v_epi.lag(1) if cc >= 2 else
                         ((v_att.h, v_att.n) if cc == 0 else (s_att.h, s_att.n)))
                T(lambda t, k=k, b2=bank, wb=b, hh=hh: t.matmul(
                    b2[:], wb[:, k * 128:(k + 1) * 128],
                    x8T3[:, k, 1024 + hh * 512:1024 + (hh + 1) * 512],
                    start=(k == 0), stop=(k == KC - 1)),
                  wait=w if k == 0 else None,
                  sig=(t_g, 1) if k == KC - 1 else None)
        tg_now = t_g.n
        R.wait("scalar", t_g, tg_now)
        R.wait("scalar", v_epi, v_epi.n)
        for hh in range(2):
            S(lambda s, cc=cc, hh=hh, bk=bankset[hh]: s.activation(
                sgbuf[:, hh * 512:(hh + 1) * 512], bk[:], AF.Sigmoid,
                scale=sgv[:, cc:cc + 1]), sig=(s_epi, 1))
        sep_now = s_epi.n
        R.wait("vector", t_g, tg_now)
        R.wait("vector", s_epi, sep_now)
        R.wait("vector", d_hrl[cc % 3], hrl_led[cc][1])
        for hh in range(2):
            V(lambda v, cc=cc, hh=hh, bk=bankset[hh]: v.tensor_scalar_mul(
                g2buf[:, hh * 512:(hh + 1) * 512], bk[:],
                sgv[:, cc:cc + 1]))
        V(lambda v: v.tensor_tensor(g2buf[:], g2buf[:], sgbuf[:], ALU.mult))
        V(lambda v, cc=cc, h=hb: v.scalar_tensor_tensor(
            o1buf[:], h[:], gso[:, cc:cc + 1], rstd_b[:], ALU.mult, ALU.mult),
          wait=(v_rb.h, 2) if cc == 0 else None)
        V(lambda v, cc=cc: v.tensor_tensor(
            x8T3[:, cc, 0:1024], o1buf[:], g2buf[:], ALU.mult), sig=(v_epi, 1))

    for cc in range(KC):
        emit_epi_chunk(cc)
    esA.close()
    esP.close()

    esB = ExitStack()
    sbB, psB = mk(esB)
    woq2 = [[sbB(f"woq_{p_}_{k}", [128, 512], BF16) for k in range(KC)]
            for p_ in range(2)]
    sst = [sbB(f"sst{i_}", [128, 512], F32) for i_ in range(2)]
    smg = sbB("smg", [128, 512], F32)
    po_ps = [pbank[0], pbank[1]]
    natd = [sbB("natd0", [128, D], F32), sbB("natd1", [128, D], F32),
            sbB("natd2", [128, 1536], F32)]
    q8d = [sbB(f"q8d{i_}", [128, I], FP8) for i_ in range(4)]
    pstgd = sbB("pstgd", [128, IC * 512], FP8)
    ppd = [pbank[2], pbank[3]]

    dq_spq = {}
    dtr_led = {}
    dsd_led = {}

    def emit_down_quant(gi):
        widths = [(0, 2048), (2048, 2048), (4096, 1536)]
        if gi >= 1:
            R.wait("vector", t_ptr, dtr_led[gi - 1])
        for j in range(4):
            c = gi * 4 + j
            led = []
            for pi_, (off, w) in enumerate(widths):
                buf = natd[pi_]
                dnd = d_natd[pi_ % 2] if pi_ < 2 else d_natd2
                if gi == 0 and j == 0:
                    wdma = v_epi.cur() if pi_ == 0 else None
                else:
                    wdma = s_pq.lag(2 - pi_) if pi_ < 2 else s_pq.cur()
                DMA("scalar", lambda e, b=buf, c=c, o=off, w=w: e.dma_start(
                    out=b[:, 0:w], in_=d_in["wdn_nat"][c][:, o:o + w]),
                    sig=(dnd, 16), wait=wdma)
                led.append((dnd.h, dnd.n))
            for pi_, (off, w) in enumerate(widths):
                buf = natd[pi_]
                R.wait("vector", led[pi_][0], led[pi_][1])
                if pi_ == 0:
                    R.wait("vector", s_pq, s_pq.n)
                    V(lambda v, b=buf, w=w: v.reduce_max(
                        wmaxc[:], b[:, 0:w], axis=AX.X,
                        apply_absolute_value=True), sig=(g_pmax, 1))
                else:
                    V(lambda v, b=buf, w=w: v.reduce_max(
                        wn2[:], b[:, 0:w], axis=AX.X,
                        apply_absolute_value=True))
                    V(lambda v: v.tensor_max(wmaxc[:], wmaxc[:], wn2[:]),
                      sig=(g_pmax, 1))
            V(lambda v, c=c: v.tensor_copy(wsd_cols[:, c:c + 1], wmaxc[:]),
              sig=(g_pmax, 1))
            recip_nr(wrec[:], wmaxc[:], wn1[:])
            V(lambda v: v.tensor_scalar_mul(wrec[:], wrec[:], 224.0),
              sig=(v_misc, 1))
            for pi_, (off, w) in enumerate(widths):
                buf = natd[pi_]
                V(lambda v, b=buf, j=j, o=off, w=w: v.tensor_scalar(
                    q8d[j][:, o:o + w], b[:, 0:w], wrec[:], None, ALU.mult),
                  sig=(s_pq, 1))
        dq_spq[gi] = s_pq.n

    def emit_down_tr(gi):
        for icc in range(IC):
            bank = ppd[icc % 2]
            for j in range(4):
                w = None
                if j == 0:
                    w = (s_pq.h, dq_spq[gi]) if icc == 0 else v_pev.lag(1)
                T(lambda t, j=j, icc=icc, b=bank: t.matmul(
                    b[:, j * 128:(j + 1) * 128],
                    q8d[j][:, icc * 128:(icc + 1) * 128], ident8[:],
                    start=True, stop=True),
                  wait=w, sig=(t_ptr, 1) if j == 3 else None)
            if icc % 11 == 0 and gi >= 1:
                wsem, wval = dsd_led[(gi - 1, icc // 11)]
                R.wait("vector", wsem, wval)
            V(lambda v, icc=icc, b=bank: v.tensor_copy(
                pstgd[:, icc * 512:(icc + 1) * 512], b[:]),
              wait=t_ptr.cur(), sig=(v_pev, 1))
            if icc % 11 == 10:
                p = icc // 11
                DMA("sync", lambda e, gi=gi, p=p: e.dma_start(
                    out=qwd_sc[gi][:, p * 5632:(p + 1) * 5632],
                    in_=pstgd[:, p * 5632:(p + 1) * 5632]),
                    sig=(d_scrd4[p], 16), wait=v_pev.cur())
                dsd_led[(gi, p)] = (d_scrd4[p], d_scrd4[p].n)
        dtr_led[gi] = t_ptr.n

    for _ds in d_scr2:
        R.wait("sync", _ds, _ds.n)
        R.wait("scalar", _ds, _ds.n)
        R.wait("vector", _ds, _ds.n)
        R.wait("gpsimd", _ds, _ds.n)
    R.wait("sync", d_wsr, d_wsr.n)
    for _ds in d_scrd4:
        R.wait("sync", _ds, _ds.n)
    R.wait("sync", v_epi, KC)
    R.wait("gpsimd", d_wsr, d_wsr.n)
    R.wait("gpsimd", v_epi, KC)
    oproj_sst_led = {}
    oproj_top_led = {}

    def emit_oproj_od(od):
        wq = woq2[od % 2]
        dwo = d_wo2[od % 2]
        for k in range(KC):
            wdma = ((v_epi.h, KC) if od < 2
                    else (t_op.h, oproj_top_led[od - 2]))
            DMA("gpsimd", lambda e, k=k, od=od, wq=wq: e.dma_start(
                out=wq[k][:],
                in_=d_in["woT_sw"][k][:, od * 512:(od + 1) * 512]),
                sig=(dwo, 16), wait=wdma)
        need = (dwo.h, dwo.n)
        sst_led = oproj_sst_led
        for tj in range(8):
            bank = po_ps[tj % 2]
            for cc in range(KC):
                w = None
                if cc == 0:
                    R.wait("tensor", need[0], need[1])
                    if od == 0 and tj == 0:
                        R.wait("tensor", v_epi, v_epi.n)
                        R.wait("tensor", v_rb, 2)
                    w = v_sev.lag(1)
                T(lambda t, cc=cc, b=bank, tj=tj: t.matmul(
                    b[:], x8T3[:, cc, tj * 128:(tj + 1) * 128], wq[cc][:],
                    start=(cc == 0), stop=(cc == KC - 1)),
                  wait=w if cc == 0 else None,
                  sig=(t_op, 1) if cc == KC - 1 else None)
            top_now = t_op.n
            R.wait("vector", t_op, top_now)
            gsst = od * 8 + tj
            if gsst >= 2:
                R.wait("vector", d_so[gsst % 2], sst_led[gsst - 2][1])
            V(lambda v, b=bank: v.tensor_scalar(
                smg[:], b[:], 256.0, MAGIC, ALU.mult, ALU.add))
            V(lambda v, tj=tj: v.tensor_scalar(
                sst[tj % 2][:], smg[:], -MAGIC, 1.0 / 256.0, ALU.add, ALU.mult),
              sig=(v_sev, 1))
            dso = d_so[gsst % 2]
            DMA("gpsimd", lambda e, tj=tj, od=od: e.dma_start(
                out=s_sc[tj * 128:(tj + 1) * 128, od * 512:(od + 1) * 512],
                in_=sst[tj % 2][:]), sig=(dso, 16), wait=v_sev.cur())
            sst_led[gsst] = (dso.h, dso.n)
        oproj_top_led[od] = t_op.n

    for od in range(4):
        emit_oproj_od(od)
        if od >= 1:
            emit_down_tr(od - 1)
        emit_down_quant(od)
    emit_down_tr(3)
    emit_ws_row(wsd_cols, KC, 2)

    esB.close()
    esX.close()

    esC = ExitStack()
    sbC, psC = mk(esC)
    Tt = [sbC(f"Tt{i_}", [128, I], F32) for i_ in range(4)]
    qt8 = sbC("qt8", [128, I], FP8)
    qtT = [sbC(f"qtT{i_}", [128, I], FP8) for i_ in range(4)]
    wgq = [sbC(f"wgq{i_}", [128, 2048], FP8) for i_ in range(3)]
    wuq = [sbC(f"wuq{i_}", [128, 2048], FP8) for i_ in range(3)]
    wdq = [sbC(f"wdq{i_}", [128, 2048], FP8) for i_ in range(4)]
    wsb = [sbC(f"wsb{i_}", [128, 512], F32) for i_ in range(2)]
    ayb = sbC("ayb", [128, 512], F32)
    y2b = sbC("y2b", [128, 512], F32)
    sgm8 = sbC("sgm8", [128, 512], F32)
    xrl = [sbC(f"xrl{i_}", [128, 512], F32) for i_ in range(2)]
    srl = [sbC(f"srl{i_}", [128, 512], F32) for i_ in range(2)]
    outb = [sbC(f"outb{i_}", [128, 512], F32) for i_ in range(2)]
    tm1 = sbC("tm1", [128, 1], F32)
    tm2 = sbC("tm2", [128, 1], F32)
    pgu = pbank

    for _ds in d_scr2 + d_scrd4:
        R.wait("sync", _ds, _ds.n)
    R.wait("sync", d_wsr, d_wsr.n)
    R.wait("sync", d_so[0], d_so[0].n)
    R.wait("sync", d_so[1], d_so[1].n)
    R.wait("sync", t_op, t_op.n)
    R.wait("sync", v_sev, v_sev.n)
    R.wait("vector", d_so[0], d_so[0].n)
    R.wait("vector", d_so[1], d_so[1].n)
    R.wait("vector", d_wsr, d_wsr.n)
    for _ds in d_scrd4:
        R.wait("vector", _ds, _ds.n)
    R.wait("vector", t_op, t_op.n)
    R.wait("vector", v_pev, v_pev.n)
    R.wait("vector", s_pev, s_pev.n)
    R.wait("scalar", t_op, t_op.n)
    R.wait("scalar", v_pev, v_pev.n)
    R.wait("tensor", v_sev, v_sev.n)
    R.wait("tensor", v_pev, v_pev.n)
    R.wait("tensor", s_pev, s_pev.n)
    R.wait("tensor", v_qxT, v_qxT.n)
    R.wait("tensor", s_qxT, s_qxT.n)
    for _ds in d_scrd4:
        R.wait("gpsimd", _ds, _ds.n)

    V(lambda v: v.tensor_scalar_mul(sgf[:], rmax[:], FQ))
    V(lambda v: v.tensor_tensor(sg2[:], sgf[:], sgf[:], ALU.mult),
      sig=(v_misc, 1))
    vm_sgf = v_misc.n

    bc_state = {"n": 0}
    bc_use_led = {}

    def bcast_ws(dr_idx, base):
        n = bc_state["n"]
        bc_state["n"] = n + 1
        p = n % 2
        drs = d_row[p]
        DMA("sync", lambda e, di=dr_idx, b=base, p=p: e.dma_start(
            out=wsb[p][:],
            in_=ws_dr[di:di + 1, :, :].rearrange("a b c -> a (b c)")
                [0:1, b:b + 512].partition_broadcast(128)),
            sig=(drs, 16), wait=bc_use_led.get(n - 2))
        return p, (drs.h, drs.n)

    def bc_mark_used(n_offset=0):
        bc_use_led[bc_state["n"] - 1] = (ch_v.h, ch_v.n)

    mm_led = {}
    dq_led = {}
    dn_led = {}
    dqd_led = {}
    y_led = {}
    ag_led = {}
    up_led = {}
    v_ag = sem("v_ag")

    def load_quarter(wsel, gq):
        sc = qwg_sc if wsel == 0 else qwu_sc
        ring = wgq if wsel == 0 else wuq
        ic_ = (gq % (IB * 4)) // 4
        q = gq % 4
        wq = mm_led.get((wsel, gq - 3))
        dqs = d_q8[wsel][gq % 3]
        DMA("sync", lambda e, sc=sc, ic_=ic_, q=q, b=ring[gq % 3]: e.dma_start(
            out=b[:], in_=sc[ic_][:, q * 2048:(q + 1) * 2048]),
            sig=(dqs, 16), wait=wq)
        dq_led[(wsel, gq)] = (dqs.h, dqs.n)

    for wsel in range(2):
        for gq0 in range(3):
            load_quarter(wsel, gq0)

    GB, UB = pgu[0:4], pgu[4:8]
    for TS in range(2):
        R.wait("vector", s_qt, s_qt.n)
        R.wait("vector", t_dn, t_dn.n)
        for ic in range(IB):
            bc_g = bcast_ws(0, ic * 512)
            bc_u = bcast_ws(1, ic * 512)
            for wsel, bufs, sem4, banks in ((0, wgq, t_g4, GB),
                                            (1, wuq, t_u4, UB)):
                glb = TS * IB + ic
                guard = (ag_led if wsel == 0 else up_led).get(glb - 1)
                if guard is not None:
                    R.wait("tensor", guard[0], guard[1])
                for k in range(KC):
                    gq = TS * IB * 4 + ic * 4 + k // 4
                    for tj in range(4):
                        if k % 4 == 0 and tj == 0:
                            R.wait("tensor", dq_led[(wsel, gq)][0],
                                   dq_led[(wsel, gq)][1])
                        last_q = (k % 4 == 3 and tj == 3)
                        T(lambda t, k=k, tj=tj, bu=bufs, sl=gq % 3, TS=TS,
                          bks=banks: t.matmul(
                            bks[tj][:],
                            qxT3[:, k, TS * 512 + tj * 128:TS * 512 + (tj + 1) * 128],
                            bu[sl][:, (k % 4) * 512:(k % 4 + 1) * 512],
                            start=(k == 0), stop=(k == KC - 1)),
                          sig=(sem4, 1) if last_q else None)
                        if last_q:
                            mm_led[(wsel, gq)] = (sem4.h, sem4.n)
                            if gq + 3 < 2 * IB * 4:
                                load_quarter(wsel, gq + 3)
            glb = TS * IB + ic
            R.wait("vector", t_g4, t_g4.n)
            R.wait("vector", bc_g[1][0], bc_g[1][1])
            for tj in range(4):
                V(lambda v, tj=tj, ic=ic, p=bc_g[0]: v.tensor_tensor(
                    Tt[tj][:, ic * 512:(ic + 1) * 512], GB[tj][:],
                    wsb[p][:], ALU.mult),
                  wait=(v_misc.h, vm_sgf) if (TS == 0 and ic == 0
                                              and tj == 0) else None,
                  sig=(v_ag, 1))
            ag_led[glb] = (v_ag.h, v_ag.n)
            bc_use_led[bc_state["n"] - 2] = (v_ag.h, v_ag.n)
            R.wait("scalar", t_u4, t_u4.n)
            R.wait("scalar", v_ffn, v_ffn.n)
            R.wait("vector", t_u4, t_u4.n)
            R.wait("vector", bc_u[1][0], bc_u[1][1])
            for tj in range(4):
                S(lambda s, tj=tj, ic=ic, TS=TS: s.activation(
                    sgm8[:], Tt[tj][:, ic * 512:(ic + 1) * 512],
                    AF.Sigmoid,
                    scale=sgf[:, TS * 4 + tj:TS * 4 + tj + 1]),
                  wait=v_ag.cur() if tj == 0 else v_ffn.cur(), sig=(s_ffn, 1))
                V(lambda v, tj=tj, p=bc_u[0]: v.tensor_tensor(
                    ayb[:], UB[tj][:], wsb[p][:], ALU.mult))
                V(lambda v, tj=tj, ic=ic, TS=TS: v.scalar_tensor_tensor(
                    y2b[:], Tt[tj][:, ic * 512:(ic + 1) * 512],
                    sg2[:, TS * 4 + tj:TS * 4 + tj + 1], ayb[:],
                    ALU.mult, ALU.mult))
                V(lambda v, tj=tj, ic=ic: v.tensor_tensor(
                    Tt[tj][:, ic * 512:(ic + 1) * 512], y2b[:], sgm8[:],
                    ALU.mult), wait=s_ffn.cur(), sig=(v_ffn, 1))
            up_led[glb] = (v_ffn.h, v_ffn.n)
            bc_use_led[bc_state["n"] - 1] = (v_ffn.h, v_ffn.n)
        for tj in range(4):
            V(lambda v, tj=tj: v.reduce_max(tm1[:], Tt[tj][:], axis=AX.X,
                                            apply_absolute_value=True))
            V(lambda v, tj=tj, TS=TS: v.tensor_scalar_mul(
                tyf[:, TS * 4 + tj:TS * 4 + tj + 1], tm1[:], FQ))
            R.wait("vector", s_qt, s_qt.n)
            recip_nr(tm2[:], tm1[:], wn1[:])
            V(lambda v: v.tensor_scalar_mul(tm2[:], tm2[:], 224.0),
              sig=(v_tmax, 1))
            R.wait("scalar", t_trt, t_trt.n)
            S(lambda s, tj=tj: s.activation(qt8[:], Tt[tj][:], AF.Copy,
                                            scale=tm2[:]),
              wait=v_tmax.cur(), sig=(s_qt, 1))
            sqt_now = s_qt.n
            if TS == 0 and tj == 0:
                R.wait("tensor", v_ffn, v_ffn.n)
            for icc in range(IC):
                bank = pgu[icc % 2]
                w = (s_qt.h, sqt_now) if icc == 0 else (
                    v_qtT.lag(1) if icc >= 2 else None)
                T(lambda t, icc=icc, b=bank: t.matmul(
                    b[:, 0:128], qt8[:, icc * 128:(icc + 1) * 128], ident8[:],
                    start=True, stop=True), wait=w, sig=(t_trt, 1))
                V(lambda v, icc=icc, b=bank, tj=tj: v.tensor_copy(
                    qtT[tj][:, icc * 128:(icc + 1) * 128], b[:, 0:128]),
                  wait=t_trt.cur(), sig=(v_qtT, 1))
        for hc in range(4):
            bc_d = bcast_ws(2, hc * 512)
            for ib_ in range(IB):
                gb = (TS * 4 + hc) * IB + ib_
                dqd = d_q8d[gb % 4]
                DMA("sync", lambda e, ib_=ib_, hc=hc, b=wdq[gb % 4]:
                    e.dma_start(out=b[:],
                                in_=qwd_sc[hc][:, ib_ * 2048:(ib_ + 1) * 2048]),
                    sig=(dqd, 16), wait=dn_led.get(gb - 4))
                dqd_led[gb] = (dqd.h, dqd.n)
                R.wait("tensor", dqd_led[gb][0], dqd_led[gb][1])
                if ib_ == 0:
                    R.wait("tensor", v_qtT, v_qtT.n)
                    R.wait("tensor", v_fin, v_fin.n)
                for sl4 in range(4):
                    icc = ib_ * 4 + sl4
                    for tj in range(4):
                        T(lambda t, icc=icc, tj=tj, gb=gb, sl4=sl4: t.matmul(
                            UB[tj][:], qtT[tj][:, icc * 128:(icc + 1) * 128],
                            wdq[gb % 4][:, sl4 * 512:(sl4 + 1) * 512],
                            start=(icc == 0), stop=(icc == IC - 1)),
                          sig=(t_dn, 1))
                dn_led[gb] = (t_dn.h, t_dn.n)
            tdn_now = t_dn.n
            for tj in range(4):
                rowb = TS * 512 + tj * 128
                gfin = (TS * 4 + hc) * 4 + tj
                dsxs = d_sx[tj % 2]
                dsss = d_ss[tj % 2]
                DMA("sync", lambda e, rowb=rowb, hc=hc, b=xrl[tj % 2]:
                    e.dma_start(out=b[:],
                                in_=d_in["xown"][rowb:rowb + 128,
                                                 hc * 512:(hc + 1) * 512]),
                    sig=(dsxs, 16), wait=v_fin.lag(1))
                DMA("sync", lambda e, rowb=rowb, hc=hc, b=srl[tj % 2]:
                    e.dma_start(out=b[:],
                                in_=s_sc[rowb:rowb + 128,
                                         hc * 512:(hc + 1) * 512]),
                    sig=(dsss, 16), wait=v_fin.lag(1))
                dss_now = (dsss, dsss.n)
                dsx_now = (dsxs, dsxs.n)
                R.wait("vector", t_dn, tdn_now)
                R.wait("vector", bc_d[1][0], bc_d[1][1])
                V(lambda v, tj=tj, p=bc_d[0]: v.tensor_tensor(
                    ayb[:], UB[tj][:], wsb[p][:], ALU.mult))
                S(lambda s, tj=tj, TS=TS: s.activation(
                    sgm8[:], ayb[:], AF.Sigmoid,
                    scale=tyf[:, TS * 4 + tj:TS * 4 + tj + 1]),
                  wait=(ch_v.h, ch_v.n), sig=(s_ffn, 1))
                V(lambda v, tj=tj, TS=TS: v.tensor_scalar_mul(
                    y2b[:], ayb[:], tyf[:, TS * 4 + tj:TS * 4 + tj + 1]))
                V(lambda v: v.tensor_tensor(y2b[:], y2b[:], sgm8[:], ALU.mult),
                  wait=s_ffn.cur())
                R.wait("vector", dss_now[0], dss_now[1])
                R.wait("vector", dsx_now[0], dsx_now[1])
                V(lambda v, tj=tj: v.tensor_tensor(y2b[:], y2b[:],
                                                   srl[tj % 2][:], ALU.mult))
                V(lambda v, tj=tj: v.tensor_tensor(outb[tj % 2][:], y2b[:],
                                                   xrl[tj % 2][:], ALU.add),
                  wait=y_led.get(gfin - 2), sig=(v_fin, 1))
                dys = d_y[gfin % 2]
                DMA("gpsimd", lambda e, rowb=rowb, hc=hc, b=outb[tj % 2]:
                    e.dma_start(out=y_out[rowb:rowb + 128,
                                          hc * 512:(hc + 1) * 512], in_=b[:]),
                    sig=(dys, 16), wait=v_fin.cur())
                y_led[gfin] = (dys.h, dys.n)
            bc_use_led[bc_state["n"] - 1] = (ch_v.h, ch_v.n)

    R.wait("gpsimd", d_y[0], d_y[0].n)
    R.wait("gpsimd", d_y[1], d_y[1].n)
    R.wait("sync", d_y[0], d_y[0].n)
    R.wait("sync", d_y[1], d_y[1].n)

    with nc.Block() as block:
        @block.sync
        def _(e):
            R.replay("sync", e)

        @block.vector
        def _(e):
            R.replay("vector", e)

        @block.scalar
        def _(e):
            R.replay("scalar", e)

        @block.gpsimd
        def _(e):
            R.replay("gpsimd", e)

        @block.tensor
        def _(e):
            R.replay("tensor", e)

    esC.close()
    es0.close()
    return nc


def _host_prepare(ins):
    x = np.ascontiguousarray(ins["L0_attn_hidden_states"], dtype=np.float32)
    w_i, w_f, w_g, w_o = (np.asarray(ins[f"L0_attn_w_{k}"], dtype=np.float32)
                          for k in "ifgo")
    down_idx = np.asarray(ins["L0_ffn_down_reorder_index"]).astype(np.int64)

    def strip_sw(w):
        t = w.reshape(KC, 128, KC, 128)
        t = np.transpose(t, (2, 1, 0, 3))
        return np.ascontiguousarray(t.reshape(KC, 128, D), dtype=np.float32)

    def vec_img(v):
        return np.ascontiguousarray(
            np.asarray(v, dtype=np.float32).reshape(KC, 128).T)

    shared = {
        "wi_sw": strip_sw(w_i), "wf_sw": strip_sw(w_f), "wg_sw": strip_sw(w_g),
        "woT_sw": np.ascontiguousarray(w_o.T.reshape(KC, 128, D),
                                       dtype=np.float32),
        "si_img": vec_img(ins["L0_attn_s_i"]),
        "sf_img": vec_img(ins["L0_attn_s_f"]),
        "sg_img": vec_img(ins["L0_attn_s_g"]),
        "so_img": vec_img(ins["L0_attn_s_o"]),
        "gnw_img": vec_img(ins["L0_attn_g_norm_w"]),
        "wgat_nat": np.ascontiguousarray(
            np.asarray(ins["L0_ffn_w_gate"], dtype=np.float32)[down_idx]
            .reshape(IC, 128, D)),
        "wup_nat": np.ascontiguousarray(
            np.asarray(ins["L0_ffn_w_up"], dtype=np.float32)[down_idx]
            .reshape(IC, 128, D)),
        "wdn_nat": np.ascontiguousarray(
            np.asarray(ins["L0_ffn_w_down"], dtype=np.float32)
            .reshape(KC, 128, I)),
    }
    zeros_pre = np.zeros((PRE, D), dtype=np.float32)
    in_maps = []
    for core in range(8):
        b, half = core // 2, core % 2
        m = dict(shared)
        m["xown"] = np.ascontiguousarray(x[b, half * TOK:(half + 1) * TOK, :])
        m["xpre"] = (np.ascontiguousarray(x[b, 0:PRE, :]) if half == 1
                     else zeros_pre)
        in_maps.append(m)
    return in_maps


def kernel(**inputs):
    in_maps = _host_prepare(inputs)
    nc = build_program()
    res = run_bass_kernel_spmd(nc, in_maps, core_ids=list(range(8)))
    out = np.empty((4, 2048, D), dtype=np.float32)
    for core in range(8):
        b, half = core // 2, core % 2
        out[b, half * TOK:(half + 1) * TOK, :] = np.asarray(res.results[core]["y"])
    return out


# version 21
# speedup vs baseline: 1.0886x; 1.0553x over previous
import numpy as np
import concourse.bass as bass
import concourse.mybir as mybir
from concourse.bass_utils import run_bass_kernel_spmd

F32 = mybir.dt.float32
BF16 = mybir.dt.bfloat16
FP8 = mybir.dt.float8e4
I32 = mybir.dt.int32
AF = mybir.ActivationFunctionType
ALU = mybir.AluOpType
AX = mybir.AxisListType

D = 2048
KC = D // 128
TOK = 1024
PRE = 1024
I = 5632
IC = I // 128
IB = I // 512
MAGIC = float(np.float32(1.5 * 2 ** 23))
EPS = 1e-5
RSQRT_C = 0x5F3759DF
FQ = 4.0 / (448.0 * 448.0)


class SemC:
    def __init__(self, h):
        self.h = h
        self.n = 0

    def inc(self, k=1):
        self.n += k
        return (self.h, self.n)

    def cur(self):
        return (self.h, self.n)

    def lag(self, k):
        return (self.h, max(0, self.n - k))


class Rec:
    CHAINED = ("vector", "scalar", "gpsimd")

    def __init__(self):
        self.st = {e: [] for e in ("sync", "vector", "scalar", "gpsimd", "tensor")}
        self.last = {e: None for e in self.st}

    def emit(self, eng, fn, wait=None, sig=None, chain_sem=None, nochain=False):
        if eng in self.CHAINED and wait is not None:
            if wait[1] > 0:
                self.st[eng].append(("W", wait[0], wait[1]))
            wait = None
        last = self.last[eng] if (eng in self.CHAINED and not nochain) else None
        if sig is not None:
            sem, amount = sig
            tgt = sem.inc(amount)
            upd = (tgt[0], amount)
        elif eng in self.CHAINED and chain_sem is not None:
            tgt = chain_sem.inc(1)
            upd = (tgt[0], 1)
        else:
            tgt, upd = None, None
        self.st[eng].append(("I", fn, wait, last, upd))
        if eng in self.CHAINED and tgt is not None and not nochain:
            self.last[eng] = tgt
        return tgt

    def wait(self, eng, sem, val=None):
        v = sem.n if val is None else val
        h = sem.h if hasattr(sem, "h") else sem
        if v > 0:
            self.st[eng].append(("W", h, v))

    def replay(self, eng, engine):
        for item in self.st[eng]:
            if item[0] == "W":
                engine.wait_ge(item[1], item[2])
                continue
            _, fn, wait, last, upd = item
            ins = fn(engine)
            if wait is not None and wait[1] > 0:
                ins._wait_ge(wait[0], wait[1])
            elif last is not None:
                ins._wait_ge(last[0], last[1])
            if upd is not None:
                ins.then_inc(upd[0], upd[1])


def build_program():
    nc = bass.Bass("TRN2", target_bir_lowering=False, debug=False)

    d_in = {}
    for name, shape in [
        ("xpre", [PRE, D]), ("xown", [TOK, D]),
        ("wi_sw", [KC, 128, D]), ("wf_sw", [KC, 128, D]),
        ("wg_sw", [KC, 128, D]), ("woT_sw", [KC, 128, D]),
        ("si_img", [128, KC]), ("sf_img", [128, KC]), ("sg_img", [128, KC]),
        ("so_img", [128, KC]), ("gnw_img", [128, KC]),
        ("wgat_nat", [IC, 128, D]), ("wup_nat", [IC, 128, D]),
        ("wdn_nat", [KC, 128, I]),
    ]:
        d_in[name] = nc.dram_tensor(name, shape, F32, kind="ExternalInput").ap()
    y_out = nc.dram_tensor("y", [TOK, D], F32, kind="ExternalOutput").ap()

    qwg_sc = nc.dram_tensor("qwg_sc", [IB, 128, KC * 512], FP8, kind="Internal").ap()
    qwu_sc = nc.dram_tensor("qwu_sc", [IB, 128, KC * 512], FP8, kind="Internal").ap()
    qwd_sc = nc.dram_tensor("qwd_sc", [4, 128, IC * 512], FP8, kind="Internal").ap()
    woT_bf = nc.dram_tensor("woT_bf", [KC, 128, D], BF16, kind="Internal").ap()
    s_sc = nc.dram_tensor("s_sc", [TOK, D], F32, kind="Internal").ap()
    h_sc = nc.dram_tensor("h_sc", [KC, 128, TOK], F32, kind="Internal").ap()
    ws_dr = nc.dram_tensor("ws_dr", [3, IC, 128], F32, kind="Internal").ap()

    from contextlib import ExitStack
    es0 = ExitStack()

    def mk(stack):
        def sb(name, shape, dtype):
            return stack.enter_context(nc.sbuf_tensor(name, shape, dtype))

        def ps(name, shape, dtype=F32):
            return stack.enter_context(nc.psum_tensor(name, shape, dtype))
        return sb, ps

    sb0, ps0 = mk(es0)

    def sem(name):
        return SemC(es0.enter_context(nc.semaphore(name)))

    si = sb0("si", [128, KC], F32)
    sf = sb0("sf", [128, KC], F32)
    nsf = sb0("nsf", [128, KC], F32)
    sgv = sb0("sgv", [128, KC], F32)
    sov = sb0("sov", [128, KC], F32)
    gnw = sb0("gnw", [128, KC], F32)
    gso = sb0("gso", [128, KC], F32)
    rmax = sb0("rmax", [128, 8], F32)
    sgf = sb0("sgf", [128, 8], F32)
    sg2 = sb0("sg2", [128, 8], F32)
    tyf = sb0("tyf", [128, 8], F32)
    identb = sb0("identb", [128, 128], BF16)
    ident8 = sb0("ident8", [128, 128], FP8)
    identf = sb0("identf", [128, 128], F32)
    ones_col = sb0("ones_col", [128, 1], F32)
    ones_row = sb0("ones_row", [1, 128], F32)
    row_t = sb0("row_t", [1, 512], F32)
    wsg_cols = sb0("wsg_cols", [128, IC], F32)
    wsu_cols = sb0("wsu_cols", [128, IC], F32)
    wsd_cols = sb0("wsd_cols", [128, KC], F32)
    qxT3 = sb0("qxT3", [128, KC, TOK], FP8)
    ssum = sb0("ssum", [1, TOK], F32)
    rstd_r = sb0("rstd_r", [1, TOK], F32)
    ir1 = sb0("ir1", [1, TOK], I32)
    ir2 = sb0("ir2", [1, TOK], I32)
    wmaxc = sb0("wmaxc", [128, 1], F32)
    wrec = sb0("wrec", [128, 1], F32)
    wn1 = sb0("wn1", [128, 1], F32)
    wn2 = sb0("wn2", [128, 1], F32)
    li1 = sb0("li1", [128, 1], I32)
    li2 = sb0("li2", [128, 1], I32)
    gm1 = sb0("gm1", [128, 1], F32)
    gm2 = sb0("gm2", [128, 1], F32)
    wst_sb = sb0("wst_sb", [IC, 128], F32)
    wtmp = sb0("wtmp", [128, IC], F32)
    wtmp2 = sb0("wtmp2", [128, IC], F32)
    lnm_c = sb0("lnm_c", [128, 16], F32)
    lsq_c = sb0("lsq_c", [128, 16], F32)
    lrs_c = sb0("lrs_c", [128, 16], F32)
    lv1_c = sb0("lv1_c", [128, 16], F32)
    lv2_c = sb0("lv2_c", [128, 16], F32)
    lvi_c = sb0("lvi_c", [128, 16], I32)
    lvi2_c = sb0("lvi2_c", [128, 16], I32)
    pbank = [ps0(f"pbank{i_}", [128, 512], F32) for i_ in range(8)]

    ch_v, ch_s, ch_g = sem("ch_v"), sem("ch_s"), sem("ch_g")
    d_small = sem("d_small")
    d_x = [sem(f"d_x{i_}") for i_ in range(4)]
    d_wif = [sem(f"d_wif{i_}") for i_ in range(4)]
    d_wg = [sem(f"d_wg{i_}") for i_ in range(2)]
    d_wo2 = [sem(f"d_wo{i_}") for i_ in range(2)]
    d_nat = [sem(f"d_nat{i_}") for i_ in range(3)]
    d_natd = [sem(f"d_natd{i_}") for i_ in range(2)]
    d_natd2 = sem("d_natd2")
    d_scr2 = [sem(f"d_scr{i_}") for i_ in range(2)]
    d_scrd4 = [sem(f"d_scrd{i_}") for i_ in range(4)]
    d_q8 = [[sem(f"d_q8_{w_}{i_}") for i_ in range(4)] for w_ in range(2)]
    d_q8d = [sem(f"d_q8d{i_}") for i_ in range(4)]
    d_wc = sem("d_wc")
    d_sx = [sem(f"d_sx{i_}") for i_ in range(2)]
    d_ss = [sem(f"d_ss{i_}") for i_ in range(2)]
    d_so = [sem(f"d_so{i_}") for i_ in range(2)]
    d_hsp = [sem(f"d_hsp{i_}") for i_ in range(3)]
    d_hrl = [sem(f"d_hrl{i_}") for i_ in range(3)]
    d_y = [sem(f"d_y{i_}") for i_ in range(2)]
    d_wsr = sem("d_wsr")
    d_row = [sem(f"d_row{i_}") for i_ in range(2)]
    g_id = sem("g_id")
    g_pmax = sem("g_pmax")
    v_misc = sem("v_misc")
    v_ln = sem("v_ln")
    s_ln = sem("s_ln")
    v_x8T = sem("v_x8T")
    v_qxT = sem("v_qxT")
    s_x8T = sem("s_x8T")
    s_qxT = sem("s_qxT")
    t_trx = sem("t_trx")
    s_pq = sem("s_pq")
    t_ptr = sem("t_ptr")
    v_pev = sem("v_pev")
    s_pev = sem("s_pev")
    t_if = sem("t_if")
    s_att = sem("s_att")
    v_att = sem("v_att")
    v_scan = sem("v_scan")
    s_hsq = sem("s_hsq")
    t_h2 = sem("t_h2")
    t_bc = sem("t_bc")
    v_rb = sem("v_rb")
    t_g = sem("t_g")
    s_epi = sem("s_epi")
    v_epi = sem("v_epi")
    t_op = sem("t_op")
    v_sev = sem("v_sev")
    t_gu = sem("t_gu")
    v_ffn = sem("v_ffn")
    s_ffn = sem("s_ffn")
    v_tmax = sem("v_tmax")
    s_qt = sem("s_qt")
    t_trt = sem("t_trt")
    v_qtT = sem("v_qtT")
    t_dn = sem("t_dn")
    v_fin = sem("v_fin")
    t_wsb = sem("t_wsb")
    t_g4 = sem("t_g4")
    t_u4 = sem("t_u4")
    v_wsb = sem("v_wsb")

    R = Rec()

    def V(fn, wait=None, sig=None):
        return R.emit("vector", fn, wait=wait, sig=sig, chain_sem=ch_v)

    def S(fn, wait=None, sig=None):
        return R.emit("scalar", fn, wait=wait, sig=sig, chain_sem=ch_s)

    def G(fn, wait=None, sig=None):
        return R.emit("gpsimd", fn, wait=wait, sig=sig, chain_sem=ch_g)

    def T(fn, wait=None, sig=None):
        return R.emit("tensor", fn, wait=wait, sig=sig)

    def DMA(eng, fn, sig, wait=None):
        return R.emit(eng, fn, wait=wait, sig=sig, nochain=True)


    def recip_nr(dst, src, f1):
        V(lambda v: v.reciprocal(dst, src))
        V(lambda v: v.tensor_tensor(f1, dst, src, ALU.mult))
        V(lambda v: v.tensor_scalar(f1, f1, -1.0, 2.0, ALU.mult, ALU.add))
        V(lambda v: v.tensor_tensor(dst, dst, f1, ALU.mult))

    def rsqrt_nr(dst, src, it1, it2, f1, f2):
        V(lambda v: v.tensor_scalar(it1, src.bitcast(I32), 1, None,
                                    ALU.logical_shift_right))
        V(lambda v: v.tensor_scalar(it2, it1, -1, RSQRT_C, ALU.mult, ALU.add))
        V(lambda v: v.tensor_copy(dst, it2.bitcast(F32)))
        for _ in range(3):
            V(lambda v: v.tensor_tensor(f1, dst, dst, ALU.mult))
            V(lambda v: v.tensor_tensor(f2, f1, src, ALU.mult))
            V(lambda v: v.tensor_scalar(f2, f2, -0.5, 1.5, ALU.mult, ALU.add))
            V(lambda v: v.tensor_tensor(dst, dst, f2, ALU.mult))

    for tile, name in [(si, "si_img"), (sf, "sf_img"), (sgv, "sg_img"),
                       (sov, "so_img"), (gnw, "gnw_img")]:
        DMA("sync", lambda e, t=tile, n=name: e.dma_start(out=t[:], in_=d_in[n]),
            sig=(d_small, 16))
    for ident in (identb, ident8, identf):
        G(lambda g, t=ident: g.memset(t[:], 0.0))
        G(lambda g, t=ident: g.affine_select(
            out=t[:], in_=t[:], compare_op=ALU.not_equal, fill=1.0, base=0,
            pattern=[[-1, 128]], channel_multiplier=1))
    G(lambda g: g.engine_nop(), sig=(g_id, 1))
    V(lambda v: v.memset(ones_col[:], 1.0))
    V(lambda v: v.memset(ones_row[:], 1.0))
    R.wait("vector", d_small, 16 * 5)
    V(lambda v: v.tensor_mul(gso[:], gnw[:], sov[:]))
    V(lambda v: v.tensor_scalar_mul(nsf[:], sf[:], -1.0), sig=(v_misc, 1))

    esX = ExitStack()
    sbX, _ = mk(esX)
    x8T3 = sbX("x8T3", [128, KC, PRE + TOK], BF16)

    esP = ExitStack()
    sbP, psP = mk(esP)
    natb = [sbP(f"natb{i_}", [128, D], F32) for i_ in range(3)]
    wmax2 = [sbP(f"wmax2{i_}", [128, 1], F32) for i_ in range(2)]
    wrec2 = [sbP(f"wrec2{i_}", [128, 1], F32) for i_ in range(2)]
    wifb = [sbP(f"wifb{i_}", [128, D], BF16) for i_ in range(6)]
    q8n = [sbP(f"q8n{i_}", [128, D], FP8) for i_ in range(8)]
    pstg = [sbP(f"pstg{i_}", [128, 2048], FP8) for i_ in range(2)]
    pp_tr = [pbank[2], pbank[3]]

    esL = ExitStack()
    sbL, psL = mk(esL)
    xint = [sbL(f"xint{i_}", [128, D], F32) for i_ in range(2)]
    xnt = [sbL("xnt0", [128, D], F32)]
    x8s = [sbL(f"x8s{i_}", [128, D], BF16) for i_ in range(2)]
    q8s = [sbL(f"q8s{i_}", [128, D], FP8) for i_ in range(2)]
    lsum = sbL("lsum", [128, 1], F32)
    wrecl = [sbL(f"wrecl{i_}", [128, 1], F32) for i_ in range(2)]
    x4 = [xint[0], xint[1], xnt[0]]
    lssq = sbL("lssq", [128, 1], F32)
    lvar = sbL("lvar", [128, 1], F32)
    lrstd = sbL("lrstd", [128, 1], F32)
    lnm = sbL("lnm", [128, 1], F32)
    lrm = sbL("lrm", [128, 1], F32)
    pb_bf = [pbank[0], pbank[1]]
    pq_f = [pbank[2], pbank[3]]

    gu_groups = [("g", gi) for gi in range(IB)] + [("u", gi) for gi in range(IB)]

    prep_c = {"n": 0}
    tpr_led = {}
    prep_scr_led = {}

    def emit_prep_quant(kind, gi, gidx):
        nat = d_in["wgat_nat"] if kind == "g" else d_in["wup_nat"]
        cols = wsg_cols if kind == "g" else wsu_cols
        for j in range(4):
            c = gi * 4 + j
            gc = prep_c["n"]
            prep_c["n"] = gc + 1
            buf = natb[gc % 3]
            dns = d_nat[gc % 3]
            wmx = wmax2[gc % 2]
            wrc = wrec2[gc % 2]
            DMA("scalar", lambda e, b=buf, n=nat, c=c: e.dma_start(
                out=b[:], in_=n[c]), sig=(dns, 16), wait=s_pq.lag(2))
            R.wait("vector", dns, dns.n)
            R.wait("vector", s_pq, s_pq.lag(1)[1])
            V(lambda v, b=buf, m=wmx: v.reduce_max(m[:], b[:], axis=AX.X,
                                                   apply_absolute_value=True))
            V(lambda v, c=c, cl=cols, m=wmx: v.tensor_copy(cl[:, c:c + 1], m[:]),
              sig=(g_pmax, 1))
            recip_nr(wrc[:], wmx[:], wn1[:])
            V(lambda v, r=wrc: v.tensor_scalar_mul(r[:], r[:], 224.0),
              sig=(v_misc, 1))
            prior_tr = tpr_led.get(gidx - 2)
            if prior_tr is not None:
                R.wait("vector", t_ptr, prior_tr)
            V(lambda v, b=buf, sl=(gidx % 2) * 4 + j, r=wrc: v.tensor_scalar(
                q8n[sl][:], b[:], r[:], None, ALU.mult), sig=(s_pq, 1))

    def emit_prep_tr(kind, gi, gidx, spq_tgt):
        sc = qwg_sc if kind == "g" else qwu_sc
        s_start = s_pev.n
        evac_led = {}
        scr_led = prep_scr_led
        for quarter in range(4):
            for kk in range(4):
                k = quarter * 4 + kk
                bank = pp_tr[k % 2]
                for j in range(4):
                    w = None
                    if j == 0:
                        if k == 0:
                            R.wait("tensor", v_qxT, v_qxT.n)
                            R.wait("tensor", s_qxT, s_qxT.n)
                            w = (s_pq.h, spq_tgt)
                        elif k == 1:
                            w = (s_pev.h, s_start)
                        else:
                            w = evac_led[k - 2]
                    T(lambda t, j=j, k=k, b=bank, sl=(gidx % 2) * 4 + j:
                      t.matmul(
                        b[:, j * 128:(j + 1) * 128],
                        q8n[sl][:, k * 128:(k + 1) * 128],
                        ident8[:], start=True, stop=True),
                      wait=w, sig=(t_ptr, 1) if j == 3 else None)
                qg = gidx * 4 + quarter
                pbuf = pstg[qg % 2]
                if kk == 0 and qg >= 2:
                    wsem, wval = scr_led[qg - 2]
                    R.wait("vector", wsem, wval)
                    R.wait("scalar", wsem, wval)
                if k % 2 == 0:
                    V(lambda v, kk=kk, b=bank, pb=pbuf: v.tensor_copy(
                        pb[:, kk * 512:(kk + 1) * 512], b[:]),
                      wait=t_ptr.cur(), sig=(v_pev, 1))
                    evac_led[k] = (v_pev.h, v_pev.n)
                else:
                    S(lambda s_, kk=kk, b=bank, pb=pbuf: s_.copy(
                        pb[:, kk * 512:(kk + 1) * 512], b[:]),
                      wait=t_ptr.cur(), sig=(s_pev, 1))
                    evac_led[k] = (s_pev.h, s_pev.n)
            R.wait("sync", s_pev, s_pev.n)
            R.wait("sync", v_pev, v_pev.n)
            DMA("sync", lambda e, sc=sc, gi=gi, q=quarter,
                pb=pstg[(gidx * 4 + quarter) % 2]: e.dma_start(
                out=sc[gi][:, q * 2048:(q + 1) * 2048], in_=pb[:]),
                sig=(d_scr2[(gidx * 4 + quarter) % 2], 16))
            scr_led[gidx * 4 + quarter] = (
                d_scr2[(gidx * 4 + quarter) % 2], d_scr2[(gidx * 4 + quarter) % 2].n)
        tpr_led[gidx] = t_ptr.n

    prep_pending = []

    def emit_prep_group(kind, gi):
        gidx = len(prep_pending)
        if gidx >= 1:
            emit_prep_tr(*prep_pending[gidx - 1])
        emit_prep_quant(kind, gi, gidx)
        prep_pending.append((kind, gi, gidx, s_pq.n))

    def flush_prep():
        for gidx in range(len(prep_pending)):
            if gidx not in tpr_led:
                emit_prep_tr(*prep_pending[gidx])

    ln_end = []
    ln_x8_led = []
    x8_bank_led = {}
    qx_bank_led = {}
    trx_end = []

    def emit_ln_pass1(j):
        buf = x4[j % 3]
        src = d_in["xpre"] if j < 8 else d_in["xown"]
        row = (j % 8) * 128
        dxs = d_x[j % 3]
        DMA("sync", lambda e, b=buf, s=src, r=row: e.dma_start(
            out=b[:], in_=s[r:r + 128, :]), sig=(dxs, 16),
            wait=ln_end[j - 3] if j >= 3 else None)
        R.wait("vector", dxs, dxs.n)
        R.wait("scalar", dxs, dxs.n)
        V(lambda v, b=buf: v.reduce_sum(lsum[:], b[:], axis=AX.X))
        V(lambda v, j=j: v.tensor_scalar_mul(lnm_c[:, j:j + 1], lsum[:],
                                             -1.0 / D), sig=(v_ln, 1))
        S(lambda s, b=buf: s.activation(b[:], b[:], AF.Square,
                                        accum_out=lssq[:]), wait=v_ln.cur())
        S(lambda s, j=j: s.copy(lsq_c[:, j:j + 1], lssq[:]),
          sig=(s_ln, 1))
        ln_end.append((s_ln.h, s_ln.n))

    def emit_ln_stats():
        R.wait("vector", v_ln, v_ln.n)
        R.wait("vector", s_ln, s_ln.n)
        V(lambda v: v.tensor_tensor(lv1_c[:], lnm_c[:], lnm_c[:], ALU.mult))
        V(lambda v: v.tensor_scalar_mul(lv1_c[:], lv1_c[:], -1.0))
        V(lambda v: v.scalar_tensor_tensor(lv1_c[:], lsq_c[:], 1.0 / D,
                                           lv1_c[:], ALU.mult, ALU.add))
        V(lambda v: v.tensor_scalar_add(lv1_c[:], lv1_c[:], EPS))
        rsqrt_nr(lrs_c[:], lv1_c[:], lvi_c[:], lvi2_c[:], lv2_c[:], wtmp[:, 0:16])
        V(lambda v: v.engine_nop(), sig=(v_misc, 1))

    q8s_led = {}

    def emit_ln_pass2A(j):
        buf = x4[j % 3]
        src = d_in["xpre"] if j < 8 else d_in["xown"]
        row = (j % 8) * 128
        own = j >= 8
        dxs = d_x[j % 3]
        DMA("sync", lambda e, b=buf, s=src, r=row: e.dma_start(
            out=b[:], in_=s[r:r + 128, :]), sig=(dxs, 16),
            wait=ln_end[16 + j - 3] if j >= 3 else
            ln_end[{0: 15, 1: 13, 2: 14}[j % 3]])
        R.wait("vector", dxs, dxs.n)
        R.wait("scalar", dxs, dxs.n)
        V(lambda v, b=buf, j=j: v.tensor_scalar(
            b[:], b[:], lnm_c[:, j:j + 1], lrs_c[:, j:j + 1],
            ALU.add, ALU.mult), sig=(v_ln, 1))
        if own:
            wl = wrecl[j % 2]
            V(lambda v, b=buf: v.reduce_max(lrm[:], b[:], axis=AX.X,
                                            apply_absolute_value=True))
            V(lambda v, j=j: v.tensor_copy(rmax[:, j - 8:j - 7], lrm[:]))
            if j >= 10:
                R.wait("vector", s_ln, q8s_led[j - 2])
            recip_nr(wl[:], lrm[:], wn1[:])
            V(lambda v, w=wl: v.tensor_scalar_mul(w[:], w[:], 224.0),
              sig=(v_misc, 1))
            if j >= 10:
                R.wait("scalar", t_trx, trx_end[j - 2])
            S(lambda s, b=buf, bq=j % 2, w=wl: s.activation(
                q8s[bq][:], b[:], AF.Copy, scale=w[:]),
              wait=v_misc.cur(), sig=(s_ln, 1))
        q8s_led[j] = s_ln.n

    def emit_ln_pass2B(j):
        buf = x4[j % 3]
        tcol = j * 128
        own = j >= 8
        V(lambda v, b=buf: v.tensor_scalar(b[:], b[:], 256.0, MAGIC,
                                           ALU.mult, ALU.add),
          wait=(s_ln.h, q8s_led[j]) if own else None, sig=(v_ln, 1))
        if j >= 2:
            R.wait("scalar", t_trx, trx_end[j - 2])
        S(lambda s, b=buf, bx=j % 2: s.activation(
            x8s[bx][:], b[:], AF.Copy, scale=1.0 / 256.0, bias=-MAGIC / 256.0),
          wait=v_ln.cur(), sig=(s_ln, 1))
        sln_x8 = s_ln.n
        gb8 = j * 4
        gbq = (j - 8) * 4
        for kg in range(4):
            bank = pb_bf[kg % 2]
            g = gb8 + kg
            for dk in range(4):
                k = kg * 4 + dk
                w = None
                if dk == 0:
                    w = x8_bank_led.get(g - 2)
                    if kg == 0:
                        w = (s_ln.h, sln_x8)
                        if g >= 2:
                            bw = x8_bank_led[g - 2]
                            R.wait("tensor", bw[0], bw[1])
                T(lambda t, k=k, b=bank, bf=j % 2, dk=dk: t.matmul(
                    b[:, dk * 128:(dk + 1) * 128],
                    x8s[bf][:, k * 128:(k + 1) * 128], identb[:],
                    start=True, stop=True),
                  wait=w, sig=(t_trx, 1) if dk == 3 else None)
            if kg % 2 == 0:
                V(lambda v, kg=kg, b=bank, tc=tcol: v.tensor_copy(
                    x8T3[:, kg * 4:(kg + 1) * 4, tc:tc + 128], b[:]),
                  wait=t_trx.cur(), sig=(v_x8T, 1))
                x8_bank_led[g] = (v_x8T.h, v_x8T.n)
            else:
                S(lambda s_, kg=kg, b=bank, tc=tcol: s_.copy(
                    x8T3[:, kg * 4:(kg + 1) * 4, tc:tc + 128], b[:]),
                  wait=t_trx.cur(), sig=(s_x8T, 1))
                x8_bank_led[g] = (s_x8T.h, s_x8T.n)
        if own:
            for kg in range(4):
                qbank = pq_f[kg % 2]
                g = gbq + kg
                for dk in range(4):
                    k = kg * 4 + dk
                    w = None
                    if dk == 0:
                        w = qx_bank_led.get(g - 2)
                        if kg == 0:
                            w = (s_ln.h, q8s_led[j])
                            R.wait("tensor", v_pev, v_pev.n)
                            R.wait("tensor", s_pev, s_pev.n)
                            if g >= 2:
                                bw = qx_bank_led[g - 2]
                                R.wait("tensor", bw[0], bw[1])
                    T(lambda t, k=k, b=qbank, bq=j % 2, dk=dk: t.matmul(
                        b[:, dk * 128:(dk + 1) * 128],
                        q8s[bq][:, k * 128:(k + 1) * 128], ident8[:],
                        start=True, stop=True),
                      wait=w, sig=(t_trx, 1) if dk == 3 else None)
                if kg % 2 == 0:
                    V(lambda v, kg=kg, b=qbank, j=j: v.tensor_copy(
                        qxT3[:, kg * 4:(kg + 1) * 4, (j - 8) * 128:(j - 7) * 128],
                        b[:]), wait=t_trx.cur(), sig=(v_qxT, 1))
                    qx_bank_led[g] = (v_qxT.h, v_qxT.n)
                else:
                    S(lambda s_, kg=kg, b=qbank, j=j: s_.copy(
                        qxT3[:, kg * 4:(kg + 1) * 4, (j - 8) * 128:(j - 7) * 128],
                        b[:]), wait=t_trx.cur(), sig=(s_qxT, 1))
                    qx_bank_led[g] = (s_qxT.h, s_qxT.n)
        ln_end.append((s_ln.h, s_ln.n))
        trx_end.append(t_trx.n)
        ln_x8_led.append(((v_x8T.h, v_x8T.n), (s_x8T.h, s_x8T.n)))

    R.wait("tensor", g_id, 1)
    gu_i = 0
    for j in range(16):
        emit_ln_pass1(j)
    emit_ln_stats()
    for j in range(16):
        emit_ln_pass2A(j)
        if j >= 1:
            emit_ln_pass2B(j - 1)
    emit_ln_pass2B(15)
    esL.close()

    esA = ExitStack()
    sbA, psA = mk(esA)
    fbuf = sbA("fbuf", [128, 1024], F32)
    f2buf = sbA("f2buf", [128, 1024], F32)
    hpre = sbA("hpre", [128, PRE], F32)
    hsqb = [sbA(f"hsqb{i_}", [128, 1024], F32) for i_ in range(3)]
    Hrot = [sbA(f"Hrot{i_}", [128, TOK], F32) for i_ in range(3)]
    rstd_b = hsqb[0]
    pi_ps = [pbank[4], pbank[5]]
    pf_ps = [pbank[6], pbank[7]]
    ph2 = [pbank[0], pbank[1]]

    hsp_led = {}
    hrl_led = {}

    def emit_h2_sum(cc):
        R.wait("tensor", s_hsq, cc + 1)
        if cc == 0:
            R.wait("tensor", v_x8T, v_x8T.n)
            R.wait("tensor", s_x8T, s_x8T.n)
        for hh in range(2):
            T(lambda t, cc=cc, hh=hh: t.matmul(
                ph2[hh][0:1, :], ones_col[:],
                hsqb[cc % 3][:, hh * 512:(hh + 1) * 512],
                start=(cc == 0), stop=(cc == KC - 1), skip_group_check=True),
              sig=(t_h2, 1))

    def emit_attn_chunk(cc):
        bi, bf_ = wifb[(cc % 3) * 2], wifb[(cc % 3) * 2 + 1]
        dws = d_wif[cc % 3]
        wdma = (None if cc < 3 else t_if.lag(16))
        DMA("gpsimd", lambda e, cc=cc, b=bi: e.dma_start(
            out=b[:], in_=d_in["wi_sw"][cc]), sig=(dws, 16), wait=wdma)
        DMA("gpsimd", lambda e, cc=cc, b=bf_: e.dma_start(
            out=b[:], in_=d_in["wf_sw"][cc]), sig=(dws, 16))
        need_w = (dws, dws.n)
        for seg in range(2):
            fmm_led = None
            for wsel, wbuf, pss in ((1, bf_, pf_ps), (0, bi, pi_ps)):
                for hh in range(2):
                    colbase = seg * 1024 + hh * 512
                    bank = pss[hh]
                    for k in range(KC):
                        w = None
                        if k == 0:
                            R.wait("tensor", need_w[0], need_w[1])
                            blk = seg * 8 + hh * 4 + 3
                            if cc == 0:
                                R.wait("tensor", ln_x8_led[blk][0][0],
                                       ln_x8_led[blk][0][1])
                            R.wait("tensor", ln_x8_led[blk][1][0],
                                       ln_x8_led[blk][1][1])
                            w = v_att.cur() if wsel == 0 else s_att.cur()
                        T(lambda t, k=k, b=bank, wb=wbuf, cb=colbase: t.matmul(
                            b[:], wb[:, k * 128:(k + 1) * 128],
                            x8T3[:, k, cb:cb + 512],
                            start=(k == 0), stop=(k == KC - 1)),
                          wait=w if k == 0 else None,
                          sig=(t_if, 1) if k == KC - 1 else None)
                if wsel == 1:
                    fmm_led = (t_if.h, t_if.n)
            tif_now = t_if.n
            R.wait("scalar", t_if, fmm_led[1])
            R.wait("scalar", v_scan, v_scan.n)
            for hh in range(2):
                S(lambda s, cc=cc, hh=hh: s.activation(
                    fbuf[:, hh * 512:(hh + 1) * 512], pf_ps[hh][:], AF.Sigmoid,
                    scale=sf[:, cc:cc + 1]))
                S(lambda s, cc=cc, hh=hh: s.activation(
                    f2buf[:, hh * 512:(hh + 1) * 512], pf_ps[hh][:], AF.Sigmoid,
                    scale=nsf[:, cc:cc + 1]), sig=(s_att, 1))
            satt_now = s_att.n
            R.wait("vector", t_if, tif_now)
            R.wait("vector", s_att, satt_now)
            for hh in range(2):
                V(lambda v, cc=cc, hh=hh: v.scalar_tensor_tensor(
                    f2buf[:, hh * 512:(hh + 1) * 512], pi_ps[hh][:],
                    si[:, cc:cc + 1], f2buf[:, hh * 512:(hh + 1) * 512],
                    ALU.mult, ALU.mult), sig=(v_att, 1))
            if seg == 0:
                V(lambda v: v.tensor_tensor_scan(
                    hpre[:], fbuf[:], f2buf[:], 0.0, ALU.mult, ALU.add),
                  sig=(v_scan, 1))
            else:
                if cc >= 3:
                    R.wait("vector", d_hsp[cc % 3], hsp_led[cc - 3][1])
                V(lambda v, cc=cc: v.tensor_tensor_scan(
                    Hrot[cc % 3][:], fbuf[:], f2buf[:], hpre[:, PRE - 1:PRE],
                    ALU.mult, ALU.add), sig=(v_scan, 1))
        R.wait("scalar", t_h2, t_h2.n)
        S(lambda s, cc=cc: s.activation(hsqb[cc % 3][:], Hrot[cc % 3][:],
                                        AF.Square),
          wait=v_scan.cur(), sig=(s_hsq, 1))
        dhs = d_hsp[cc % 3]
        DMA("sync", lambda e, cc=cc: e.dma_start(
            out=h_sc[cc], in_=Hrot[cc % 3][:]), sig=(dhs, 16),
            wait=s_hsq.cur())
        hsp_led[cc] = (dhs.h, dhs.n)
        if cc >= 2:
            emit_h2_sum(cc - 2)

    for cc in range(KC):
        emit_attn_chunk(cc)
        if gu_i < len(gu_groups):
            emit_prep_group(*gu_groups[gu_i])
            gu_i += 1
        if cc < 6 and gu_i < len(gu_groups):
            emit_prep_group(*gu_groups[gu_i])
            gu_i += 1
    emit_h2_sum(KC - 2)
    emit_h2_sum(KC - 1)
    while gu_i < len(gu_groups):
        emit_prep_group(*gu_groups[gu_i])
        gu_i += 1
    flush_prep()

    def emit_ws_row(cols, nchunks, dr_idx):
        n = nchunks
        R.wait("vector", g_pmax, g_pmax.n)
        R.wait("vector", t_ptr, t_ptr.n)
        V(lambda v, n=n, cl=cols: v.tensor_copy(wtmp[:, 0:n], cl[:, 0:n]),
          sig=(v_misc, 1))
        T(lambda t, n=n: t.matmul(pp_tr[0][0:n, 0:128], wtmp[:, 0:n],
                                  identf[:], start=True, stop=True),
          wait=v_misc.cur(), sig=(t_ptr, 1))
        R.wait("vector", d_wsr, d_wsr.n)
        V(lambda v, n=n: v.tensor_copy(wst_sb[0:n, :], pp_tr[0][0:n, 0:128]),
          wait=t_ptr.cur(), sig=(v_pev, 1))
        DMA("sync", lambda e, n=n, di=dr_idx: e.dma_start(
            out=ws_dr[di][0:n, :], in_=wst_sb[0:n, :]), sig=(d_wsr, 16),
            wait=v_pev.cur())

    emit_ws_row(wsg_cols, IC, 0)
    emit_ws_row(wsu_cols, IC, 1)

    R.wait("vector", t_h2, t_h2.n)
    V(lambda v: v.tensor_copy(ssum[0:1, 0:512], ph2[0][0:1, :]))
    V(lambda v: v.tensor_copy(ssum[0:1, 512:1024], ph2[1][0:1, :]))
    V(lambda v: v.tensor_scalar(ssum[:], ssum[:], 1.0 / D, EPS, ALU.mult,
                                ALU.add))
    rsqrt_nr(rstd_r[:], ssum[:], ir1[:], ir2[:], hpre[0:1, :], rstd_b[0:1, :])
    V(lambda v: v.engine_nop(), sig=(v_misc, 1))
    vm_rstd = v_misc.n
    for hh in range(2):
        T(lambda t, hh=hh: t.matmul(
            ph2[hh][:, :], ones_row[:], rstd_r[0:1, hh * 512:(hh + 1) * 512],
            start=True, stop=True, skip_group_check=True),
          wait=(v_misc.h, vm_rstd), sig=(t_bc, 1))
        V(lambda v, hh=hh: v.tensor_copy(
            rstd_b[:, hh * 512:(hh + 1) * 512], ph2[hh][:, :]),
          wait=t_bc.cur(), sig=(v_rb, 1))

    sgbuf = fbuf
    g2buf = f2buf
    o1buf = hpre
    R.wait("scalar", v_scan, v_scan.n)

    def emit_epi_chunk(cc):
        b = wifb[cc % 2]
        dgs = d_wg[cc % 2]
        wdma = (t_if.cur() if cc < 2 else t_g.lag(2))
        DMA("gpsimd", lambda e, cc=cc, b=b: e.dma_start(
            out=b[:], in_=d_in["wg_sw"][cc]), sig=(dgs, 16), wait=wdma)
        need = (dgs, dgs.n)
        hb = Hrot[cc % 3]
        dhr = d_hrl[cc % 3]
        last_spill = {0: 15, 1: 13, 2: 14}[cc % 3]
        DMA("sync", lambda e, cc=cc, h=hb: e.dma_start(
            out=h[:], in_=h_sc[cc]), sig=(dhr, 16),
            wait=(hsp_led[last_spill] if cc < 3 else v_epi.lag(2)))
        hrl_led[cc] = (dhr.h, dhr.n)
        bankset = pi_ps if cc % 2 == 0 else pf_ps
        for hh in range(2):
            bank = bankset[hh]
            for k in range(KC):
                w = None
                if k == 0:
                    R.wait("tensor", need[0], need[1])
                    w = (v_epi.lag(1) if cc >= 2 else
                         ((v_att.h, v_att.n) if cc == 0 else (s_att.h, s_att.n)))
                T(lambda t, k=k, b2=bank, wb=b, hh=hh: t.matmul(
                    b2[:], wb[:, k * 128:(k + 1) * 128],
                    x8T3[:, k, 1024 + hh * 512:1024 + (hh + 1) * 512],
                    start=(k == 0), stop=(k == KC - 1)),
                  wait=w if k == 0 else None,
                  sig=(t_g, 1) if k == KC - 1 else None)
        tg_now = t_g.n
        R.wait("scalar", t_g, tg_now)
        R.wait("scalar", v_epi, v_epi.n)
        for hh in range(2):
            S(lambda s, cc=cc, hh=hh, bk=bankset[hh]: s.activation(
                sgbuf[:, hh * 512:(hh + 1) * 512], bk[:], AF.Sigmoid,
                scale=sgv[:, cc:cc + 1]), sig=(s_epi, 1))
        sep_now = s_epi.n
        R.wait("vector", t_g, tg_now)
        R.wait("vector", s_epi, sep_now)
        R.wait("vector", d_hrl[cc % 3], hrl_led[cc][1])
        for hh in range(2):
            V(lambda v, cc=cc, hh=hh, bk=bankset[hh]: v.tensor_scalar_mul(
                g2buf[:, hh * 512:(hh + 1) * 512], bk[:],
                sgv[:, cc:cc + 1]))
        V(lambda v: v.tensor_tensor(g2buf[:], g2buf[:], sgbuf[:], ALU.mult))
        V(lambda v, cc=cc, h=hb: v.scalar_tensor_tensor(
            o1buf[:], h[:], gso[:, cc:cc + 1], rstd_b[:], ALU.mult, ALU.mult),
          wait=(v_rb.h, 2) if cc == 0 else None)
        V(lambda v, cc=cc: v.tensor_tensor(
            x8T3[:, cc, 0:1024], o1buf[:], g2buf[:], ALU.mult), sig=(v_epi, 1))

    for cc in range(KC):
        emit_epi_chunk(cc)
    esA.close()
    esP.close()

    esB = ExitStack()
    sbB, psB = mk(esB)
    woq2 = [[sbB(f"woq_{p_}_{k}", [128, 512], BF16) for k in range(KC)]
            for p_ in range(2)]
    sst = [sbB(f"sst{i_}", [128, 512], F32) for i_ in range(2)]
    smg = sbB("smg", [128, 512], F32)
    po_ps = [pbank[0], pbank[1]]
    natd = [sbB("natd0", [128, D], F32), sbB("natd1", [128, D], F32),
            sbB("natd2", [128, 1536], F32)]
    q8d = [sbB(f"q8d{i_}", [128, I], FP8) for i_ in range(4)]
    pstgd = sbB("pstgd", [128, IC * 512], FP8)
    ppd = [pbank[2], pbank[3]]

    dq_spq = {}
    dtr_led = {}
    dsd_led = {}

    def emit_down_quant(gi):
        widths = [(0, 2048), (2048, 2048), (4096, 1536)]
        if gi >= 1:
            R.wait("vector", t_ptr, dtr_led[gi - 1])
        for j in range(4):
            c = gi * 4 + j
            led = []
            for pi_, (off, w) in enumerate(widths):
                buf = natd[pi_]
                dnd = d_natd[pi_ % 2] if pi_ < 2 else d_natd2
                if gi == 0 and j == 0:
                    wdma = v_epi.cur() if pi_ == 0 else None
                else:
                    wdma = s_pq.lag(2 - pi_) if pi_ < 2 else s_pq.cur()
                DMA("scalar", lambda e, b=buf, c=c, o=off, w=w: e.dma_start(
                    out=b[:, 0:w], in_=d_in["wdn_nat"][c][:, o:o + w]),
                    sig=(dnd, 16), wait=wdma)
                led.append((dnd.h, dnd.n))
            for pi_, (off, w) in enumerate(widths):
                buf = natd[pi_]
                R.wait("vector", led[pi_][0], led[pi_][1])
                if pi_ == 0:
                    R.wait("vector", s_pq, s_pq.n)
                    V(lambda v, b=buf, w=w: v.reduce_max(
                        wmaxc[:], b[:, 0:w], axis=AX.X,
                        apply_absolute_value=True), sig=(g_pmax, 1))
                else:
                    V(lambda v, b=buf, w=w: v.reduce_max(
                        wn2[:], b[:, 0:w], axis=AX.X,
                        apply_absolute_value=True))
                    V(lambda v: v.tensor_max(wmaxc[:], wmaxc[:], wn2[:]),
                      sig=(g_pmax, 1))
            V(lambda v, c=c: v.tensor_copy(wsd_cols[:, c:c + 1], wmaxc[:]),
              sig=(g_pmax, 1))
            recip_nr(wrec[:], wmaxc[:], wn1[:])
            V(lambda v: v.tensor_scalar_mul(wrec[:], wrec[:], 224.0),
              sig=(v_misc, 1))
            for pi_, (off, w) in enumerate(widths):
                buf = natd[pi_]
                V(lambda v, b=buf, j=j, o=off, w=w: v.tensor_scalar(
                    q8d[j][:, o:o + w], b[:, 0:w], wrec[:], None, ALU.mult),
                  sig=(s_pq, 1))
        dq_spq[gi] = s_pq.n

    def emit_down_tr(gi):
        for icc in range(IC):
            bank = ppd[icc % 2]
            for j in range(4):
                w = None
                if j == 0:
                    w = (s_pq.h, dq_spq[gi]) if icc == 0 else v_pev.lag(1)
                T(lambda t, j=j, icc=icc, b=bank: t.matmul(
                    b[:, j * 128:(j + 1) * 128],
                    q8d[j][:, icc * 128:(icc + 1) * 128], ident8[:],
                    start=True, stop=True),
                  wait=w, sig=(t_ptr, 1) if j == 3 else None)
            if icc % 11 == 0 and gi >= 1:
                wsem, wval = dsd_led[(gi - 1, icc // 11)]
                R.wait("vector", wsem, wval)
            V(lambda v, icc=icc, b=bank: v.tensor_copy(
                pstgd[:, icc * 512:(icc + 1) * 512], b[:]),
              wait=t_ptr.cur(), sig=(v_pev, 1))
            if icc % 11 == 10:
                p = icc // 11
                DMA("sync", lambda e, gi=gi, p=p: e.dma_start(
                    out=qwd_sc[gi][:, p * 5632:(p + 1) * 5632],
                    in_=pstgd[:, p * 5632:(p + 1) * 5632]),
                    sig=(d_scrd4[p], 16), wait=v_pev.cur())
                dsd_led[(gi, p)] = (d_scrd4[p], d_scrd4[p].n)
        dtr_led[gi] = t_ptr.n

    for _ds in d_scr2:
        R.wait("sync", _ds, _ds.n)
        R.wait("scalar", _ds, _ds.n)
        R.wait("vector", _ds, _ds.n)
        R.wait("gpsimd", _ds, _ds.n)
    R.wait("sync", d_wsr, d_wsr.n)
    for _ds in d_scrd4:
        R.wait("sync", _ds, _ds.n)
    R.wait("sync", v_epi, KC)
    R.wait("gpsimd", d_wsr, d_wsr.n)
    R.wait("gpsimd", v_epi, KC)
    oproj_sst_led = {}
    oproj_top_led = {}

    def emit_oproj_od(od):
        wq = woq2[od % 2]
        dwo = d_wo2[od % 2]
        for k in range(KC):
            wdma = ((v_epi.h, KC) if od < 2
                    else (t_op.h, oproj_top_led[od - 2]))
            DMA("gpsimd", lambda e, k=k, od=od, wq=wq: e.dma_start(
                out=wq[k][:],
                in_=d_in["woT_sw"][k][:, od * 512:(od + 1) * 512]),
                sig=(dwo, 16), wait=wdma)
        need = (dwo.h, dwo.n)
        sst_led = oproj_sst_led
        for tj in range(8):
            bank = po_ps[tj % 2]
            for cc in range(KC):
                w = None
                if cc == 0:
                    R.wait("tensor", need[0], need[1])
                    if od == 0 and tj == 0:
                        R.wait("tensor", v_epi, v_epi.n)
                        R.wait("tensor", v_rb, 2)
                    w = v_sev.lag(1)
                T(lambda t, cc=cc, b=bank, tj=tj: t.matmul(
                    b[:], x8T3[:, cc, tj * 128:(tj + 1) * 128], wq[cc][:],
                    start=(cc == 0), stop=(cc == KC - 1)),
                  wait=w if cc == 0 else None,
                  sig=(t_op, 1) if cc == KC - 1 else None)
            top_now = t_op.n
            R.wait("vector", t_op, top_now)
            gsst = od * 8 + tj
            if gsst >= 2:
                R.wait("vector", d_so[gsst % 2], sst_led[gsst - 2][1])
            V(lambda v, b=bank: v.tensor_scalar(
                smg[:], b[:], 256.0, MAGIC, ALU.mult, ALU.add))
            V(lambda v, tj=tj: v.tensor_scalar(
                sst[tj % 2][:], smg[:], -MAGIC, 1.0 / 256.0, ALU.add, ALU.mult),
              sig=(v_sev, 1))
            dso = d_so[gsst % 2]
            DMA("gpsimd", lambda e, tj=tj, od=od: e.dma_start(
                out=s_sc[tj * 128:(tj + 1) * 128, od * 512:(od + 1) * 512],
                in_=sst[tj % 2][:]), sig=(dso, 16), wait=v_sev.cur())
            sst_led[gsst] = (dso.h, dso.n)
        oproj_top_led[od] = t_op.n

    for od in range(4):
        emit_oproj_od(od)
        if od >= 1:
            emit_down_tr(od - 1)
        emit_down_quant(od)
    emit_down_tr(3)
    emit_ws_row(wsd_cols, KC, 2)

    esB.close()
    esX.close()

    esC = ExitStack()
    sbC, psC = mk(esC)
    Tt = [sbC(f"Tt{i_}", [128, I], F32) for i_ in range(4)]
    qt8r = [sbC(f"qt8_{i_}", [128, I], FP8) for i_ in range(2)]
    qtT = [sbC(f"qtT{i_}", [128, I], FP8) for i_ in range(4)]
    wgq = [sbC(f"wgq{i_}", [128, 2048], FP8) for i_ in range(4)]
    wuq = [sbC(f"wuq{i_}", [128, 2048], FP8) for i_ in range(4)]
    wdq = [sbC(f"wdq{i_}", [128, 2048], FP8) for i_ in range(4)]
    wsb = [sbC(f"wsb{i_}", [128, 512], F32) for i_ in range(2)]
    ayb = sbC("ayb", [128, 512], F32)
    y2b = sbC("y2b", [128, 512], F32)
    sgm8 = sbC("sgm8", [128, 512], F32)
    xrl = [sbC(f"xrl{i_}", [128, 512], F32) for i_ in range(2)]
    srl = [sbC(f"srl{i_}", [128, 512], F32) for i_ in range(2)]
    outb = [sbC(f"outb{i_}", [128, 512], F32) for i_ in range(2)]
    tm1 = sbC("tm1", [128, 1], F32)
    tm2l = [sbC(f"tm2_{i_}", [128, 1], F32) for i_ in range(2)]
    pgu = pbank

    for _ds in d_scr2 + d_scrd4:
        R.wait("sync", _ds, _ds.n)
    R.wait("sync", d_wsr, d_wsr.n)
    R.wait("sync", d_so[0], d_so[0].n)
    R.wait("sync", d_so[1], d_so[1].n)
    R.wait("sync", t_op, t_op.n)
    R.wait("sync", v_sev, v_sev.n)
    R.wait("vector", d_so[0], d_so[0].n)
    R.wait("vector", d_so[1], d_so[1].n)
    R.wait("vector", d_wsr, d_wsr.n)
    for _ds in d_scrd4:
        R.wait("vector", _ds, _ds.n)
    R.wait("vector", t_op, t_op.n)
    R.wait("vector", v_pev, v_pev.n)
    R.wait("vector", s_pev, s_pev.n)
    R.wait("scalar", t_op, t_op.n)
    R.wait("scalar", v_pev, v_pev.n)
    R.wait("tensor", v_sev, v_sev.n)
    R.wait("tensor", v_pev, v_pev.n)
    R.wait("tensor", s_pev, s_pev.n)
    R.wait("tensor", v_qxT, v_qxT.n)
    R.wait("tensor", s_qxT, s_qxT.n)
    for _ds in d_scrd4:
        R.wait("gpsimd", _ds, _ds.n)

    V(lambda v: v.tensor_scalar_mul(sgf[:], rmax[:], FQ))
    V(lambda v: v.tensor_tensor(sg2[:], sgf[:], sgf[:], ALU.mult),
      sig=(v_misc, 1))
    vm_sgf = v_misc.n

    bc_state = {"n": 0}
    bc_use_led = {}

    def bcast_ws(dr_idx, base):
        n = bc_state["n"]
        bc_state["n"] = n + 1
        p = n % 2
        drs = d_row[p]
        DMA("sync", lambda e, di=dr_idx, b=base, p=p: e.dma_start(
            out=wsb[p][:],
            in_=ws_dr[di:di + 1, :, :].rearrange("a b c -> a (b c)")
                [0:1, b:b + 512].partition_broadcast(128)),
            sig=(drs, 16), wait=bc_use_led.get(n - 2))
        return p, (drs.h, drs.n)

    def bc_mark_used(n_offset=0):
        bc_use_led[bc_state["n"] - 1] = (ch_v.h, ch_v.n)

    mm_led = {}
    dq_led = {}
    dn_led = {}
    dqd_led = {}
    qt_sqt_led = {}
    qt_trt_led = {}
    y_led = {}
    ag_led = {}
    up_led = {}
    v_ag = sem("v_ag")

    def load_quarter(wsel, gq):
        sc = qwg_sc if wsel == 0 else qwu_sc
        ring = wgq if wsel == 0 else wuq
        ic_ = (gq % (IB * 4)) // 4
        q = gq % 4
        wq = mm_led.get((wsel, gq - 4))
        dqs = d_q8[wsel][gq % 4]
        DMA("sync", lambda e, sc=sc, ic_=ic_, q=q, b=ring[gq % 4]: e.dma_start(
            out=b[:], in_=sc[ic_][:, q * 2048:(q + 1) * 2048]),
            sig=(dqs, 16), wait=wq)
        dq_led[(wsel, gq)] = (dqs.h, dqs.n)

    for wsel in range(2):
        for gq0 in range(4):
            load_quarter(wsel, gq0)

    GB, UB = pgu[0:4], pgu[4:8]
    for TS in range(2):
        R.wait("vector", s_qt, s_qt.n)
        R.wait("vector", t_dn, t_dn.n)
        R.wait("tensor", v_fin, v_fin.n)
        for ic in range(IB):
            bc_g = bcast_ws(0, ic * 512)
            bc_u = bcast_ws(1, ic * 512)
            for wsel, bufs, sem4, banks in ((0, wgq, t_g4, GB),
                                            (1, wuq, t_u4, UB)):
                glb = TS * IB + ic
                guard = (ag_led if wsel == 0 else up_led).get(glb - 1)
                if guard is not None:
                    R.wait("tensor", guard[0], guard[1])
                for k in range(KC):
                    gq = TS * IB * 4 + ic * 4 + k // 4
                    for tj in range(4):
                        if k % 4 == 0 and tj == 0:
                            R.wait("tensor", dq_led[(wsel, gq)][0],
                                   dq_led[(wsel, gq)][1])
                        last_q = (k % 4 == 3 and tj == 3)
                        T(lambda t, k=k, tj=tj, bu=bufs, sl=gq % 4, TS=TS,
                          bks=banks: t.matmul(
                            bks[tj][:],
                            qxT3[:, k, TS * 512 + tj * 128:TS * 512 + (tj + 1) * 128],
                            bu[sl][:, (k % 4) * 512:(k % 4 + 1) * 512],
                            start=(k == 0), stop=(k == KC - 1)),
                          sig=(sem4, 1) if last_q else None)
                        if last_q:
                            mm_led[(wsel, gq)] = (sem4.h, sem4.n)
                            if gq + 4 < 2 * IB * 4:
                                load_quarter(wsel, gq + 4)
            glb = TS * IB + ic
            R.wait("vector", t_g4, t_g4.n)
            R.wait("vector", bc_g[1][0], bc_g[1][1])
            for tj in range(4):
                V(lambda v, tj=tj, ic=ic, p=bc_g[0]: v.tensor_tensor(
                    Tt[tj][:, ic * 512:(ic + 1) * 512], GB[tj][:],
                    wsb[p][:], ALU.mult),
                  wait=(v_misc.h, vm_sgf) if (TS == 0 and ic == 0
                                              and tj == 0) else None,
                  sig=(v_ag, 1))
            ag_led[glb] = (v_ag.h, v_ag.n)
            bc_use_led[bc_state["n"] - 2] = (v_ag.h, v_ag.n)
            R.wait("scalar", t_u4, t_u4.n)
            R.wait("scalar", v_ffn, v_ffn.n)
            R.wait("vector", t_u4, t_u4.n)
            R.wait("vector", bc_u[1][0], bc_u[1][1])
            for tj in range(4):
                S(lambda s, tj=tj, ic=ic, TS=TS: s.activation(
                    sgm8[:], Tt[tj][:, ic * 512:(ic + 1) * 512],
                    AF.Sigmoid,
                    scale=sgf[:, TS * 4 + tj:TS * 4 + tj + 1]),
                  wait=v_ag.cur() if tj == 0 else v_ffn.cur(), sig=(s_ffn, 1))
                V(lambda v, tj=tj, p=bc_u[0]: v.tensor_tensor(
                    ayb[:], UB[tj][:], wsb[p][:], ALU.mult))
                V(lambda v, tj=tj, ic=ic, TS=TS: v.scalar_tensor_tensor(
                    y2b[:], Tt[tj][:, ic * 512:(ic + 1) * 512],
                    sg2[:, TS * 4 + tj:TS * 4 + tj + 1], ayb[:],
                    ALU.mult, ALU.mult))
                V(lambda v, tj=tj, ic=ic: v.tensor_tensor(
                    Tt[tj][:, ic * 512:(ic + 1) * 512], y2b[:], sgm8[:],
                    ALU.mult), wait=s_ffn.cur(), sig=(v_ffn, 1))
            up_led[glb] = (v_ffn.h, v_ffn.n)
            bc_use_led[bc_state["n"] - 1] = (v_ffn.h, v_ffn.n)
        def emit_t_quant(tj):
            tm2b = tm2l[tj % 2]
            V(lambda v, tj=tj: v.reduce_max(tm1[:], Tt[tj][:], axis=AX.X,
                                            apply_absolute_value=True))
            V(lambda v, tj=tj, TS=TS: v.tensor_scalar_mul(
                tyf[:, TS * 4 + tj:TS * 4 + tj + 1], tm1[:], FQ))
            pw = qt_sqt_led.get((TS, tj - 2))
            if pw is not None:
                R.wait("vector", s_qt, pw)
            recip_nr(tm2b[:], tm1[:], wn1[:])
            V(lambda v, b=tm2b: v.tensor_scalar_mul(b[:], b[:], 224.0),
              sig=(v_tmax, 1))
            ptr = qt_trt_led.get((TS, tj - 2), qt_trt_led.get((TS - 1, tj + 2)))
            if ptr is not None:
                R.wait("scalar", t_trt, ptr)
            S(lambda s, tj=tj, b=tm2b: s.activation(
                qt8r[tj % 2][:], Tt[tj][:], AF.Copy, scale=b[:]),
              wait=v_tmax.cur(), sig=(s_qt, 1))
            qt_sqt_led[(TS, tj)] = s_qt.n

        def emit_t_tr(tj):
            if TS == 0 and tj == 0:
                R.wait("tensor", v_ffn, v_ffn.n)
            for g4 in range(11):
                bank = pgu[g4 % 2]
                for dk in range(4):
                    icc = g4 * 4 + dk
                    w = None
                    if dk == 0:
                        bg = qtT_bank_led.get(g4 % 2)
                        if g4 == 0:
                            w = (s_qt.h, qt_sqt_led[(TS, tj)])
                            if bg is not None:
                                R.wait("tensor", bg[0], bg[1])
                        else:
                            w = bg
                    T(lambda t, icc=icc, b=bank, dk=dk, tj=tj: t.matmul(
                        b[:, dk * 128:(dk + 1) * 128],
                        qt8r[tj % 2][:, icc * 128:(icc + 1) * 128], ident8[:],
                        start=True, stop=True),
                      wait=w, sig=(t_trt, 1) if dk == 3 else None)
                V(lambda v, g4=g4, b=bank, tj=tj: v.tensor_copy(
                    qtT[tj][:, g4 * 512:(g4 + 1) * 512], b[:]),
                  wait=t_trt.cur(), sig=(v_qtT, 1))
                qtT_bank_led[g4 % 2] = (v_qtT.h, v_qtT.n)
            qt_trt_led[(TS, tj)] = t_trt.n

        qtT_bank_led = {}
        for tj in range(4):
            emit_t_quant(tj)
            if tj >= 1:
                emit_t_tr(tj - 1)
        emit_t_tr(3)
        for hc in range(4):
            bc_d = bcast_ws(2, hc * 512)
            for ib_ in range(IB):
                gb = (TS * 4 + hc) * IB + ib_
                dqd = d_q8d[gb % 4]
                DMA("sync", lambda e, ib_=ib_, hc=hc, b=wdq[gb % 4]:
                    e.dma_start(out=b[:],
                                in_=qwd_sc[hc][:, ib_ * 2048:(ib_ + 1) * 2048]),
                    sig=(dqd, 16), wait=dn_led.get(gb - 4))
                dqd_led[gb] = (dqd.h, dqd.n)
                R.wait("tensor", dqd_led[gb][0], dqd_led[gb][1])
                if ib_ == 0:
                    R.wait("tensor", v_qtT, v_qtT.n)
                    R.wait("tensor", v_fin, v_fin.n)
                for sl4 in range(4):
                    icc = ib_ * 4 + sl4
                    for tj in range(4):
                        T(lambda t, icc=icc, tj=tj, gb=gb, sl4=sl4: t.matmul(
                            UB[tj][:], qtT[tj][:, icc * 128:(icc + 1) * 128],
                            wdq[gb % 4][:, sl4 * 512:(sl4 + 1) * 512],
                            start=(icc == 0), stop=(icc == IC - 1)),
                          sig=(t_dn, 1))
                dn_led[gb] = (t_dn.h, t_dn.n)
            tdn_now = t_dn.n
            for tj in range(4):
                rowb = TS * 512 + tj * 128
                gfin = (TS * 4 + hc) * 4 + tj
                dsxs = d_sx[tj % 2]
                dsss = d_ss[tj % 2]
                DMA("sync", lambda e, rowb=rowb, hc=hc, b=xrl[tj % 2]:
                    e.dma_start(out=b[:],
                                in_=d_in["xown"][rowb:rowb + 128,
                                                 hc * 512:(hc + 1) * 512]),
                    sig=(dsxs, 16), wait=v_fin.lag(1))
                DMA("sync", lambda e, rowb=rowb, hc=hc, b=srl[tj % 2]:
                    e.dma_start(out=b[:],
                                in_=s_sc[rowb:rowb + 128,
                                         hc * 512:(hc + 1) * 512]),
                    sig=(dsss, 16), wait=v_fin.lag(1))
                dss_now = (dsss, dsss.n)
                dsx_now = (dsxs, dsxs.n)
                R.wait("vector", t_dn, tdn_now)
                R.wait("vector", bc_d[1][0], bc_d[1][1])
                V(lambda v, tj=tj, p=bc_d[0]: v.tensor_tensor(
                    ayb[:], UB[tj][:], wsb[p][:], ALU.mult))
                S(lambda s, tj=tj, TS=TS: s.activation(
                    sgm8[:], ayb[:], AF.Sigmoid,
                    scale=tyf[:, TS * 4 + tj:TS * 4 + tj + 1]),
                  wait=(ch_v.h, ch_v.n), sig=(s_ffn, 1))
                V(lambda v, tj=tj, TS=TS: v.tensor_scalar_mul(
                    y2b[:], ayb[:], tyf[:, TS * 4 + tj:TS * 4 + tj + 1]))
                V(lambda v: v.tensor_tensor(y2b[:], y2b[:], sgm8[:], ALU.mult),
                  wait=s_ffn.cur())
                R.wait("vector", dss_now[0], dss_now[1])
                R.wait("vector", dsx_now[0], dsx_now[1])
                V(lambda v, tj=tj: v.tensor_tensor(y2b[:], y2b[:],
                                                   srl[tj % 2][:], ALU.mult))
                V(lambda v, tj=tj: v.tensor_tensor(outb[tj % 2][:], y2b[:],
                                                   xrl[tj % 2][:], ALU.add),
                  wait=y_led.get(gfin - 2), sig=(v_fin, 1))
                dys = d_y[gfin % 2]
                DMA("gpsimd", lambda e, rowb=rowb, hc=hc, b=outb[tj % 2]:
                    e.dma_start(out=y_out[rowb:rowb + 128,
                                          hc * 512:(hc + 1) * 512], in_=b[:]),
                    sig=(dys, 16), wait=v_fin.cur())
                y_led[gfin] = (dys.h, dys.n)
            bc_use_led[bc_state["n"] - 1] = (ch_v.h, ch_v.n)

    R.wait("gpsimd", d_y[0], d_y[0].n)
    R.wait("gpsimd", d_y[1], d_y[1].n)
    R.wait("sync", d_y[0], d_y[0].n)
    R.wait("sync", d_y[1], d_y[1].n)

    with nc.Block() as block:
        @block.sync
        def _(e):
            R.replay("sync", e)

        @block.vector
        def _(e):
            R.replay("vector", e)

        @block.scalar
        def _(e):
            R.replay("scalar", e)

        @block.gpsimd
        def _(e):
            R.replay("gpsimd", e)

        @block.tensor
        def _(e):
            R.replay("tensor", e)

    esC.close()
    es0.close()
    return nc


def _host_prepare(ins):
    x = np.ascontiguousarray(ins["L0_attn_hidden_states"], dtype=np.float32)
    w_i, w_f, w_g, w_o = (np.asarray(ins[f"L0_attn_w_{k}"], dtype=np.float32)
                          for k in "ifgo")
    down_idx = np.asarray(ins["L0_ffn_down_reorder_index"]).astype(np.int64)

    def strip_sw(w):
        t = w.reshape(KC, 128, KC, 128)
        t = np.transpose(t, (2, 1, 0, 3))
        return np.ascontiguousarray(t.reshape(KC, 128, D), dtype=np.float32)

    def vec_img(v):
        return np.ascontiguousarray(
            np.asarray(v, dtype=np.float32).reshape(KC, 128).T)

    shared = {
        "wi_sw": strip_sw(w_i), "wf_sw": strip_sw(w_f), "wg_sw": strip_sw(w_g),
        "woT_sw": np.ascontiguousarray(w_o.T.reshape(KC, 128, D),
                                       dtype=np.float32),
        "si_img": vec_img(ins["L0_attn_s_i"]),
        "sf_img": vec_img(ins["L0_attn_s_f"]),
        "sg_img": vec_img(ins["L0_attn_s_g"]),
        "so_img": vec_img(ins["L0_attn_s_o"]),
        "gnw_img": vec_img(ins["L0_attn_g_norm_w"]),
        "wgat_nat": np.ascontiguousarray(
            np.asarray(ins["L0_ffn_w_gate"], dtype=np.float32)[down_idx]
            .reshape(IC, 128, D)),
        "wup_nat": np.ascontiguousarray(
            np.asarray(ins["L0_ffn_w_up"], dtype=np.float32)[down_idx]
            .reshape(IC, 128, D)),
        "wdn_nat": np.ascontiguousarray(
            np.asarray(ins["L0_ffn_w_down"], dtype=np.float32)
            .reshape(KC, 128, I)),
    }
    zeros_pre = np.zeros((PRE, D), dtype=np.float32)
    in_maps = []
    for core in range(8):
        b, half = core // 2, core % 2
        m = dict(shared)
        m["xown"] = np.ascontiguousarray(x[b, half * TOK:(half + 1) * TOK, :])
        m["xpre"] = (np.ascontiguousarray(x[b, 0:PRE, :]) if half == 1
                     else zeros_pre)
        in_maps.append(m)
    return in_maps


def kernel(**inputs):
    in_maps = _host_prepare(inputs)
    nc = build_program()
    res = run_bass_kernel_spmd(nc, in_maps, core_ids=list(range(8)))
    out = np.empty((4, 2048, D), dtype=np.float32)
    for core in range(8):
        b, half = core // 2, core % 2
        out[b, half * TOK:(half + 1) * TOK, :] = np.asarray(res.results[core]["y"])
    return out


# version 22
# speedup vs baseline: 1.0929x; 1.0040x over previous
import numpy as np
import concourse.bass as bass
import concourse.mybir as mybir
from concourse.bass_utils import run_bass_kernel_spmd

F32 = mybir.dt.float32
BF16 = mybir.dt.bfloat16
FP8 = mybir.dt.float8e4
I32 = mybir.dt.int32
AF = mybir.ActivationFunctionType
ALU = mybir.AluOpType
AX = mybir.AxisListType

D = 2048
KC = D // 128
TOK = 1024
PRE = 1024
I = 5632
IC = I // 128
IB = I // 512
MAGIC = float(np.float32(1.5 * 2 ** 23))
EPS = 1e-5
RSQRT_C = 0x5F3759DF
FQ = 4.0 / (448.0 * 448.0)


class SemC:
    def __init__(self, h):
        self.h = h
        self.n = 0

    def inc(self, k=1):
        self.n += k
        return (self.h, self.n)

    def cur(self):
        return (self.h, self.n)

    def lag(self, k):
        return (self.h, max(0, self.n - k))


class Rec:
    CHAINED = ("vector", "scalar", "gpsimd")

    def __init__(self):
        self.st = {e: [] for e in ("sync", "vector", "scalar", "gpsimd", "tensor")}
        self.last = {e: None for e in self.st}

    def emit(self, eng, fn, wait=None, sig=None, chain_sem=None, nochain=False):
        if eng in self.CHAINED and wait is not None:
            if wait[1] > 0:
                self.st[eng].append(("W", wait[0], wait[1]))
            wait = None
        last = self.last[eng] if (eng in self.CHAINED and not nochain) else None
        if sig is not None:
            sem, amount = sig
            tgt = sem.inc(amount)
            upd = (tgt[0], amount)
        elif eng in self.CHAINED and chain_sem is not None:
            tgt = chain_sem.inc(1)
            upd = (tgt[0], 1)
        else:
            tgt, upd = None, None
        self.st[eng].append(("I", fn, wait, last, upd))
        if eng in self.CHAINED and tgt is not None and not nochain:
            self.last[eng] = tgt
        return tgt

    def wait(self, eng, sem, val=None):
        v = sem.n if val is None else val
        h = sem.h if hasattr(sem, "h") else sem
        if v > 0:
            self.st[eng].append(("W", h, v))

    def replay(self, eng, engine):
        for item in self.st[eng]:
            if item[0] == "W":
                engine.wait_ge(item[1], item[2])
                continue
            _, fn, wait, last, upd = item
            ins = fn(engine)
            if wait is not None and wait[1] > 0:
                ins._wait_ge(wait[0], wait[1])
            elif last is not None:
                ins._wait_ge(last[0], last[1])
            if upd is not None:
                ins.then_inc(upd[0], upd[1])


def build_program():
    nc = bass.Bass("TRN2", target_bir_lowering=False, debug=False)

    d_in = {}
    for name, shape in [
        ("xpre", [PRE, D]), ("xown", [TOK, D]),
        ("wi_sw", [KC, 128, D]), ("wf_sw", [KC, 128, D]),
        ("wg_sw", [KC, 128, D]), ("woT_sw", [KC, 128, D]),
        ("si_img", [128, KC]), ("sf_img", [128, KC]), ("sg_img", [128, KC]),
        ("so_img", [128, KC]), ("gnw_img", [128, KC]),
        ("wgat_nat", [IC, 128, D]), ("wup_nat", [IC, 128, D]),
        ("wdn_nat", [KC, 128, I]),
    ]:
        d_in[name] = nc.dram_tensor(name, shape, F32, kind="ExternalInput").ap()
    y_out = nc.dram_tensor("y", [TOK, D], F32, kind="ExternalOutput").ap()

    qwg_sc = nc.dram_tensor("qwg_sc", [IB, 128, KC * 512], FP8, kind="Internal").ap()
    qwu_sc = nc.dram_tensor("qwu_sc", [IB, 128, KC * 512], FP8, kind="Internal").ap()
    qwd_sc = nc.dram_tensor("qwd_sc", [4, 128, IC * 512], FP8, kind="Internal").ap()
    woT_bf = nc.dram_tensor("woT_bf", [KC, 128, D], BF16, kind="Internal").ap()
    s_sc = nc.dram_tensor("s_sc", [TOK, D], F32, kind="Internal").ap()
    h_sc = nc.dram_tensor("h_sc", [KC, 128, TOK], F32, kind="Internal").ap()
    ws_dr = nc.dram_tensor("ws_dr", [3, IC, 128], F32, kind="Internal").ap()

    from contextlib import ExitStack
    es0 = ExitStack()

    def mk(stack):
        def sb(name, shape, dtype):
            return stack.enter_context(nc.sbuf_tensor(name, shape, dtype))

        def ps(name, shape, dtype=F32):
            return stack.enter_context(nc.psum_tensor(name, shape, dtype))
        return sb, ps

    sb0, ps0 = mk(es0)

    def sem(name):
        return SemC(es0.enter_context(nc.semaphore(name)))

    si = sb0("si", [128, KC], F32)
    sf = sb0("sf", [128, KC], F32)
    nsf = sb0("nsf", [128, KC], F32)
    sgv = sb0("sgv", [128, KC], F32)
    sov = sb0("sov", [128, KC], F32)
    gnw = sb0("gnw", [128, KC], F32)
    gso = sb0("gso", [128, KC], F32)
    rmax = sb0("rmax", [128, 8], F32)
    sgf = sb0("sgf", [128, 8], F32)
    sg2 = sb0("sg2", [128, 8], F32)
    tyf = sb0("tyf", [128, 8], F32)
    identb = sb0("identb", [128, 128], BF16)
    ident8 = sb0("ident8", [128, 128], FP8)
    identf = sb0("identf", [128, 128], F32)
    ones_col = sb0("ones_col", [128, 1], F32)
    ones_row = sb0("ones_row", [1, 128], F32)
    row_t = sb0("row_t", [1, 512], F32)
    wsg_cols = sb0("wsg_cols", [128, IC], F32)
    wsu_cols = sb0("wsu_cols", [128, IC], F32)
    wsd_cols = sb0("wsd_cols", [128, KC], F32)
    qxT3 = sb0("qxT3", [128, KC, TOK], FP8)
    ssum = sb0("ssum", [1, TOK], F32)
    rstd_r = sb0("rstd_r", [1, TOK], F32)
    ir1 = sb0("ir1", [1, TOK], I32)
    ir2 = sb0("ir2", [1, TOK], I32)
    wmaxc = sb0("wmaxc", [128, 1], F32)
    wrec = sb0("wrec", [128, 1], F32)
    wn1 = sb0("wn1", [128, 1], F32)
    wn2 = sb0("wn2", [128, 1], F32)
    li1 = sb0("li1", [128, 1], I32)
    li2 = sb0("li2", [128, 1], I32)
    gm1 = sb0("gm1", [128, 1], F32)
    gm2 = sb0("gm2", [128, 1], F32)
    wst_sb = sb0("wst_sb", [IC, 128], F32)
    wtmp = sb0("wtmp", [128, IC], F32)
    wtmp2 = sb0("wtmp2", [128, IC], F32)
    lnm_c = sb0("lnm_c", [128, 16], F32)
    lsq_c = sb0("lsq_c", [128, 16], F32)
    lrs_c = sb0("lrs_c", [128, 16], F32)
    lv1_c = sb0("lv1_c", [128, 16], F32)
    lv2_c = sb0("lv2_c", [128, 16], F32)
    lvi_c = sb0("lvi_c", [128, 16], I32)
    lvi2_c = sb0("lvi2_c", [128, 16], I32)
    pbank = [ps0(f"pbank{i_}", [128, 512], F32) for i_ in range(8)]

    ch_v, ch_s, ch_g = sem("ch_v"), sem("ch_s"), sem("ch_g")
    d_small = sem("d_small")
    d_x = [sem(f"d_x{i_}") for i_ in range(4)]
    d_wif = [sem(f"d_wif{i_}") for i_ in range(4)]
    d_wg = [sem(f"d_wg{i_}") for i_ in range(2)]
    d_wo2 = [sem(f"d_wo{i_}") for i_ in range(2)]
    d_nat = [sem(f"d_nat{i_}") for i_ in range(3)]
    d_natd = [sem(f"d_natd{i_}") for i_ in range(2)]
    d_natd2 = sem("d_natd2")
    d_scr2 = [sem(f"d_scr{i_}") for i_ in range(2)]
    d_scrd4 = [sem(f"d_scrd{i_}") for i_ in range(4)]
    d_q8 = [[sem(f"d_q8_{w_}{i_}") for i_ in range(4)] for w_ in range(2)]
    d_q8d = [sem(f"d_q8d{i_}") for i_ in range(4)]
    d_wc = sem("d_wc")
    d_sx = [sem(f"d_sx{i_}") for i_ in range(2)]
    d_ss = [sem(f"d_ss{i_}") for i_ in range(2)]
    d_so = [sem(f"d_so{i_}") for i_ in range(2)]
    d_hsp = [sem(f"d_hsp{i_}") for i_ in range(3)]
    d_hrl = [sem(f"d_hrl{i_}") for i_ in range(3)]
    d_y = [sem(f"d_y{i_}") for i_ in range(2)]
    d_wsr = sem("d_wsr")
    d_row = [sem(f"d_row{i_}") for i_ in range(2)]
    g_id = sem("g_id")
    g_pmax = sem("g_pmax")
    v_misc = sem("v_misc")
    v_ln = sem("v_ln")
    s_ln = sem("s_ln")
    v_x8T = sem("v_x8T")
    v_qxT = sem("v_qxT")
    s_x8T = sem("s_x8T")
    s_qxT = sem("s_qxT")
    t_trx = sem("t_trx")
    s_pq = sem("s_pq")
    t_ptr = sem("t_ptr")
    v_pev = sem("v_pev")
    s_pev = sem("s_pev")
    t_if = sem("t_if")
    s_att = sem("s_att")
    v_att = sem("v_att")
    v_scan = sem("v_scan")
    s_hsq = sem("s_hsq")
    t_h2 = sem("t_h2")
    t_bc = sem("t_bc")
    v_rb = sem("v_rb")
    t_g = sem("t_g")
    s_epi = sem("s_epi")
    v_epi = sem("v_epi")
    t_op = sem("t_op")
    v_sev = sem("v_sev")
    t_gu = sem("t_gu")
    v_ffn = sem("v_ffn")
    s_ffn = sem("s_ffn")
    v_tmax = sem("v_tmax")
    s_qt = sem("s_qt")
    t_trt = sem("t_trt")
    v_qtT = sem("v_qtT")
    t_dn = sem("t_dn")
    v_fin = sem("v_fin")
    t_wsb = sem("t_wsb")
    t_g4 = sem("t_g4")
    t_u4 = sem("t_u4")
    v_wsb = sem("v_wsb")

    R = Rec()

    def V(fn, wait=None, sig=None):
        return R.emit("vector", fn, wait=wait, sig=sig, chain_sem=ch_v)

    def S(fn, wait=None, sig=None):
        return R.emit("scalar", fn, wait=wait, sig=sig, chain_sem=ch_s)

    def G(fn, wait=None, sig=None):
        return R.emit("gpsimd", fn, wait=wait, sig=sig, chain_sem=ch_g)

    def T(fn, wait=None, sig=None):
        return R.emit("tensor", fn, wait=wait, sig=sig)

    def DMA(eng, fn, sig, wait=None):
        return R.emit(eng, fn, wait=wait, sig=sig, nochain=True)


    def recip_nr(dst, src, f1):
        V(lambda v: v.reciprocal(dst, src))
        V(lambda v: v.tensor_tensor(f1, dst, src, ALU.mult))
        V(lambda v: v.tensor_scalar(f1, f1, -1.0, 2.0, ALU.mult, ALU.add))
        V(lambda v: v.tensor_tensor(dst, dst, f1, ALU.mult))

    def rsqrt_nr(dst, src, it1, it2, f1, f2):
        V(lambda v: v.tensor_scalar(it1, src.bitcast(I32), 1, None,
                                    ALU.logical_shift_right))
        V(lambda v: v.tensor_scalar(it2, it1, -1, RSQRT_C, ALU.mult, ALU.add))
        V(lambda v: v.tensor_copy(dst, it2.bitcast(F32)))
        for _ in range(3):
            V(lambda v: v.tensor_tensor(f1, dst, dst, ALU.mult))
            V(lambda v: v.tensor_tensor(f2, f1, src, ALU.mult))
            V(lambda v: v.tensor_scalar(f2, f2, -0.5, 1.5, ALU.mult, ALU.add))
            V(lambda v: v.tensor_tensor(dst, dst, f2, ALU.mult))

    for tile, name in [(si, "si_img"), (sf, "sf_img"), (sgv, "sg_img"),
                       (sov, "so_img"), (gnw, "gnw_img")]:
        DMA("sync", lambda e, t=tile, n=name: e.dma_start(out=t[:], in_=d_in[n]),
            sig=(d_small, 16))
    for ident in (identb, ident8, identf):
        G(lambda g, t=ident: g.memset(t[:], 0.0))
        G(lambda g, t=ident: g.affine_select(
            out=t[:], in_=t[:], compare_op=ALU.not_equal, fill=1.0, base=0,
            pattern=[[-1, 128]], channel_multiplier=1))
    G(lambda g: g.engine_nop(), sig=(g_id, 1))
    V(lambda v: v.memset(ones_col[:], 1.0))
    V(lambda v: v.memset(ones_row[:], 1.0))
    R.wait("vector", d_small, 16 * 5)
    V(lambda v: v.tensor_mul(gso[:], gnw[:], sov[:]))
    V(lambda v: v.tensor_scalar_mul(nsf[:], sf[:], -1.0), sig=(v_misc, 1))

    esX = ExitStack()
    sbX, _ = mk(esX)
    x8T3 = sbX("x8T3", [128, KC, PRE + TOK], BF16)

    esP = ExitStack()
    sbP, psP = mk(esP)
    natb = [sbP(f"natb{i_}", [128, D], F32) for i_ in range(3)]
    wmax2 = [sbP(f"wmax2{i_}", [128, 1], F32) for i_ in range(2)]
    wrec2 = [sbP(f"wrec2{i_}", [128, 1], F32) for i_ in range(2)]
    wifb = [sbP(f"wifb{i_}", [128, D], BF16) for i_ in range(6)]
    q8n = [sbP(f"q8n{i_}", [128, D], FP8) for i_ in range(8)]
    pstg = [sbP(f"pstg{i_}", [128, 2048], FP8) for i_ in range(2)]
    pp_tr = [pbank[2], pbank[3]]

    esL = ExitStack()
    sbL, psL = mk(esL)
    xint = [sbL(f"xint{i_}", [128, D], F32) for i_ in range(2)]
    xnt = [sbL("xnt0", [128, D], F32)]
    x8s = [sbL(f"x8s{i_}", [128, D], BF16) for i_ in range(2)]
    q8s = [sbL(f"q8s{i_}", [128, D], FP8) for i_ in range(2)]
    lsum = sbL("lsum", [128, 1], F32)
    wrecl = [sbL(f"wrecl{i_}", [128, 1], F32) for i_ in range(2)]
    x4 = [xint[0], xint[1], xnt[0]]
    lssq = sbL("lssq", [128, 1], F32)
    lvar = sbL("lvar", [128, 1], F32)
    lrstd = sbL("lrstd", [128, 1], F32)
    lnm = sbL("lnm", [128, 1], F32)
    lrm = sbL("lrm", [128, 1], F32)
    pb_bf = [pbank[0], pbank[1]]
    pq_f = [pbank[2], pbank[3]]

    gu_groups = [("g", gi) for gi in range(IB)] + [("u", gi) for gi in range(IB)]

    prep_c = {"n": 0}
    tpr_led = {}
    prep_scr_led = {}

    def emit_prep_quant(kind, gi, gidx):
        nat = d_in["wgat_nat"] if kind == "g" else d_in["wup_nat"]
        cols = wsg_cols if kind == "g" else wsu_cols
        for j in range(4):
            c = gi * 4 + j
            gc = prep_c["n"]
            prep_c["n"] = gc + 1
            buf = natb[gc % 3]
            dns = d_nat[gc % 3]
            wmx = wmax2[gc % 2]
            wrc = wrec2[gc % 2]
            DMA("scalar", lambda e, b=buf, n=nat, c=c: e.dma_start(
                out=b[:], in_=n[c]), sig=(dns, 16), wait=s_pq.lag(2))
            R.wait("vector", dns, dns.n)
            R.wait("vector", s_pq, s_pq.lag(1)[1])
            V(lambda v, b=buf, m=wmx: v.reduce_max(m[:], b[:], axis=AX.X,
                                                   apply_absolute_value=True))
            V(lambda v, c=c, cl=cols, m=wmx: v.tensor_copy(cl[:, c:c + 1], m[:]),
              sig=(g_pmax, 1))
            recip_nr(wrc[:], wmx[:], wn1[:])
            V(lambda v, r=wrc: v.tensor_scalar_mul(r[:], r[:], 224.0),
              sig=(v_misc, 1))
            prior_tr = tpr_led.get(gidx - 2)
            if prior_tr is not None:
                R.wait("vector", t_ptr, prior_tr)
            V(lambda v, b=buf, sl=(gidx % 2) * 4 + j, r=wrc: v.tensor_scalar(
                q8n[sl][:], b[:], r[:], None, ALU.mult), sig=(s_pq, 1))

    def emit_prep_tr(kind, gi, gidx, spq_tgt):
        sc = qwg_sc if kind == "g" else qwu_sc
        s_start = s_pev.n
        evac_led = {}
        scr_led = prep_scr_led
        for quarter in range(4):
            for kk in range(4):
                k = quarter * 4 + kk
                bank = pp_tr[k % 2]
                for j in range(4):
                    w = None
                    if j == 0:
                        if k == 0:
                            R.wait("tensor", v_qxT, v_qxT.n)
                            R.wait("tensor", s_qxT, s_qxT.n)
                            w = (s_pq.h, spq_tgt)
                        elif k == 1:
                            w = (s_pev.h, s_start)
                        else:
                            w = evac_led[k - 2]
                    T(lambda t, j=j, k=k, b=bank, sl=(gidx % 2) * 4 + j:
                      t.matmul(
                        b[:, j * 128:(j + 1) * 128],
                        q8n[sl][:, k * 128:(k + 1) * 128],
                        ident8[:], start=True, stop=True),
                      wait=w, sig=(t_ptr, 1) if j == 3 else None)
                qg = gidx * 4 + quarter
                pbuf = pstg[qg % 2]
                if kk == 0 and qg >= 2:
                    wsem, wval = scr_led[qg - 2]
                    R.wait("vector", wsem, wval)
                    R.wait("scalar", wsem, wval)
                if k % 2 == 0:
                    V(lambda v, kk=kk, b=bank, pb=pbuf: v.tensor_copy(
                        pb[:, kk * 512:(kk + 1) * 512], b[:]),
                      wait=t_ptr.cur(), sig=(v_pev, 1))
                    evac_led[k] = (v_pev.h, v_pev.n)
                else:
                    S(lambda s_, kk=kk, b=bank, pb=pbuf: s_.copy(
                        pb[:, kk * 512:(kk + 1) * 512], b[:]),
                      wait=t_ptr.cur(), sig=(s_pev, 1))
                    evac_led[k] = (s_pev.h, s_pev.n)
            R.wait("sync", s_pev, s_pev.n)
            R.wait("sync", v_pev, v_pev.n)
            DMA("sync", lambda e, sc=sc, gi=gi, q=quarter,
                pb=pstg[(gidx * 4 + quarter) % 2]: e.dma_start(
                out=sc[gi][:, q * 2048:(q + 1) * 2048], in_=pb[:]),
                sig=(d_scr2[(gidx * 4 + quarter) % 2], 16))
            scr_led[gidx * 4 + quarter] = (
                d_scr2[(gidx * 4 + quarter) % 2], d_scr2[(gidx * 4 + quarter) % 2].n)
        tpr_led[gidx] = t_ptr.n

    prep_pending = []

    def emit_prep_group(kind, gi):
        gidx = len(prep_pending)
        if gidx >= 1:
            emit_prep_tr(*prep_pending[gidx - 1])
        emit_prep_quant(kind, gi, gidx)
        prep_pending.append((kind, gi, gidx, s_pq.n))

    def flush_prep():
        for gidx in range(len(prep_pending)):
            if gidx not in tpr_led:
                emit_prep_tr(*prep_pending[gidx])

    ln_end = []
    ln_x8_led = []
    x8_bank_led = {}
    qx_bank_led = {}
    trx_end = []

    def emit_ln_pass1(j):
        buf = x4[j % 3]
        src = d_in["xpre"] if j < 8 else d_in["xown"]
        row = (j % 8) * 128
        dxs = d_x[j % 3]
        DMA("sync", lambda e, b=buf, s=src, r=row: e.dma_start(
            out=b[:], in_=s[r:r + 128, :]), sig=(dxs, 16),
            wait=ln_end[j - 3] if j >= 3 else None)
        R.wait("vector", dxs, dxs.n)
        R.wait("scalar", dxs, dxs.n)
        V(lambda v, b=buf: v.reduce_sum(lsum[:], b[:], axis=AX.X))
        V(lambda v, j=j: v.tensor_scalar_mul(lnm_c[:, j:j + 1], lsum[:],
                                             -1.0 / D), sig=(v_ln, 1))
        S(lambda s, b=buf: s.activation(b[:], b[:], AF.Square,
                                        accum_out=lssq[:]), wait=v_ln.cur())
        S(lambda s, j=j: s.copy(lsq_c[:, j:j + 1], lssq[:]),
          sig=(s_ln, 1))
        ln_end.append((s_ln.h, s_ln.n))

    def emit_ln_stats():
        R.wait("vector", v_ln, v_ln.n)
        R.wait("vector", s_ln, s_ln.n)
        V(lambda v: v.tensor_tensor(lv1_c[:], lnm_c[:], lnm_c[:], ALU.mult))
        V(lambda v: v.tensor_scalar_mul(lv1_c[:], lv1_c[:], -1.0))
        V(lambda v: v.scalar_tensor_tensor(lv1_c[:], lsq_c[:], 1.0 / D,
                                           lv1_c[:], ALU.mult, ALU.add))
        V(lambda v: v.tensor_scalar_add(lv1_c[:], lv1_c[:], EPS))
        rsqrt_nr(lrs_c[:], lv1_c[:], lvi_c[:], lvi2_c[:], lv2_c[:], wtmp[:, 0:16])
        V(lambda v: v.engine_nop(), sig=(v_misc, 1))

    q8s_led = {}

    def emit_ln_pass2A(j):
        buf = x4[j % 3]
        src = d_in["xpre"] if j < 8 else d_in["xown"]
        row = (j % 8) * 128
        own = j >= 8
        dxs = d_x[j % 3]
        DMA("sync", lambda e, b=buf, s=src, r=row: e.dma_start(
            out=b[:], in_=s[r:r + 128, :]), sig=(dxs, 16),
            wait=ln_end[16 + j - 3] if j >= 3 else
            ln_end[{0: 15, 1: 13, 2: 14}[j % 3]])
        R.wait("vector", dxs, dxs.n)
        R.wait("scalar", dxs, dxs.n)
        V(lambda v, b=buf, j=j: v.tensor_scalar(
            b[:], b[:], lnm_c[:, j:j + 1], lrs_c[:, j:j + 1],
            ALU.add, ALU.mult), sig=(v_ln, 1))
        if own:
            wl = wrecl[j % 2]
            V(lambda v, b=buf: v.reduce_max(lrm[:], b[:], axis=AX.X,
                                            apply_absolute_value=True))
            V(lambda v, j=j: v.tensor_copy(rmax[:, j - 8:j - 7], lrm[:]))
            if j >= 10:
                R.wait("vector", s_ln, q8s_led[j - 2])
            recip_nr(wl[:], lrm[:], wn1[:])
            V(lambda v, w=wl: v.tensor_scalar_mul(w[:], w[:], 224.0),
              sig=(v_misc, 1))
            if j >= 10:
                R.wait("scalar", t_trx, trx_end[j - 2])
            S(lambda s, b=buf, bq=j % 2, w=wl: s.activation(
                q8s[bq][:], b[:], AF.Copy, scale=w[:]),
              wait=v_misc.cur(), sig=(s_ln, 1))
        q8s_led[j] = s_ln.n

    def emit_ln_pass2B(j):
        buf = x4[j % 3]
        tcol = j * 128
        own = j >= 8
        V(lambda v, b=buf: v.tensor_scalar(b[:], b[:], 256.0, MAGIC,
                                           ALU.mult, ALU.add),
          wait=(s_ln.h, q8s_led[j]) if own else None, sig=(v_ln, 1))
        if j >= 2:
            R.wait("scalar", t_trx, trx_end[j - 2])
        S(lambda s, b=buf, bx=j % 2: s.activation(
            x8s[bx][:], b[:], AF.Copy, scale=1.0 / 256.0, bias=-MAGIC / 256.0),
          wait=v_ln.cur(), sig=(s_ln, 1))
        sln_x8 = s_ln.n
        gb8 = j * 4
        gbq = (j - 8) * 4
        for kg in range(4):
            bank = pb_bf[kg % 2]
            g = gb8 + kg
            for dk in range(4):
                k = kg * 4 + dk
                w = None
                if dk == 0:
                    w = x8_bank_led.get(g - 2)
                    if kg == 0:
                        w = (s_ln.h, sln_x8)
                        if g >= 2:
                            bw = x8_bank_led[g - 2]
                            R.wait("tensor", bw[0], bw[1])
                T(lambda t, k=k, b=bank, bf=j % 2, dk=dk: t.matmul(
                    b[:, dk * 128:(dk + 1) * 128],
                    x8s[bf][:, k * 128:(k + 1) * 128], identb[:],
                    start=True, stop=True),
                  wait=w, sig=(t_trx, 1) if dk == 3 else None)
            if kg % 2 == 0:
                V(lambda v, kg=kg, b=bank, tc=tcol: v.tensor_copy(
                    x8T3[:, kg * 4:(kg + 1) * 4, tc:tc + 128], b[:]),
                  wait=t_trx.cur(), sig=(v_x8T, 1))
                x8_bank_led[g] = (v_x8T.h, v_x8T.n)
            else:
                S(lambda s_, kg=kg, b=bank, tc=tcol: s_.copy(
                    x8T3[:, kg * 4:(kg + 1) * 4, tc:tc + 128], b[:]),
                  wait=t_trx.cur(), sig=(s_x8T, 1))
                x8_bank_led[g] = (s_x8T.h, s_x8T.n)
        if own:
            for kg in range(4):
                qbank = pq_f[kg % 2]
                g = gbq + kg
                for dk in range(4):
                    k = kg * 4 + dk
                    w = None
                    if dk == 0:
                        w = qx_bank_led.get(g - 2)
                        if kg == 0:
                            w = (s_ln.h, q8s_led[j])
                            R.wait("tensor", v_pev, v_pev.n)
                            R.wait("tensor", s_pev, s_pev.n)
                            if g >= 2:
                                bw = qx_bank_led[g - 2]
                                R.wait("tensor", bw[0], bw[1])
                    T(lambda t, k=k, b=qbank, bq=j % 2, dk=dk: t.matmul(
                        b[:, dk * 128:(dk + 1) * 128],
                        q8s[bq][:, k * 128:(k + 1) * 128], ident8[:],
                        start=True, stop=True),
                      wait=w, sig=(t_trx, 1) if dk == 3 else None)
                if kg % 2 == 0:
                    V(lambda v, kg=kg, b=qbank, j=j: v.tensor_copy(
                        qxT3[:, kg * 4:(kg + 1) * 4, (j - 8) * 128:(j - 7) * 128],
                        b[:]), wait=t_trx.cur(), sig=(v_qxT, 1))
                    qx_bank_led[g] = (v_qxT.h, v_qxT.n)
                else:
                    S(lambda s_, kg=kg, b=qbank, j=j: s_.copy(
                        qxT3[:, kg * 4:(kg + 1) * 4, (j - 8) * 128:(j - 7) * 128],
                        b[:]), wait=t_trx.cur(), sig=(s_qxT, 1))
                    qx_bank_led[g] = (s_qxT.h, s_qxT.n)
        ln_end.append((s_ln.h, s_ln.n))
        trx_end.append(t_trx.n)
        ln_x8_led.append(((v_x8T.h, v_x8T.n), (s_x8T.h, s_x8T.n)))

    R.wait("tensor", g_id, 1)
    gu_i = 0
    for j in range(16):
        emit_ln_pass1(j)
    emit_ln_stats()
    for j in range(16):
        if j >= 2:
            emit_ln_pass2B(j - 2)
        emit_ln_pass2A(j)
    emit_ln_pass2B(14)
    emit_ln_pass2B(15)
    esL.close()

    esA = ExitStack()
    sbA, psA = mk(esA)
    fbuf = sbA("fbuf", [128, 1024], F32)
    f2buf = sbA("f2buf", [128, 1024], F32)
    hpre = sbA("hpre", [128, PRE], F32)
    hsqb = [sbA(f"hsqb{i_}", [128, 1024], F32) for i_ in range(3)]
    Hrot = [sbA(f"Hrot{i_}", [128, TOK], F32) for i_ in range(3)]
    rstd_b = hsqb[0]
    pi_ps = [pbank[4], pbank[5]]
    pf_ps = [pbank[6], pbank[7]]
    ph2 = [pbank[0], pbank[1]]

    hsp_led = {}
    hrl_led = {}

    def emit_h2_sum(cc):
        R.wait("tensor", s_hsq, cc + 1)
        if cc == 0:
            R.wait("tensor", v_x8T, v_x8T.n)
            R.wait("tensor", s_x8T, s_x8T.n)
        for hh in range(2):
            T(lambda t, cc=cc, hh=hh: t.matmul(
                ph2[hh][0:1, :], ones_col[:],
                hsqb[cc % 3][:, hh * 512:(hh + 1) * 512],
                start=(cc == 0), stop=(cc == KC - 1), skip_group_check=True),
              sig=(t_h2, 1))

    def emit_attn_chunk(cc):
        bi, bf_ = wifb[(cc % 3) * 2], wifb[(cc % 3) * 2 + 1]
        dws = d_wif[cc % 3]
        wdma = (None if cc < 3 else t_if.lag(16))
        DMA("gpsimd", lambda e, cc=cc, b=bi: e.dma_start(
            out=b[:], in_=d_in["wi_sw"][cc]), sig=(dws, 16), wait=wdma)
        DMA("gpsimd", lambda e, cc=cc, b=bf_: e.dma_start(
            out=b[:], in_=d_in["wf_sw"][cc]), sig=(dws, 16))
        need_w = (dws, dws.n)
        for seg in range(2):
            fmm_led = None
            for wsel, wbuf, pss in ((1, bf_, pf_ps), (0, bi, pi_ps)):
                for hh in range(2):
                    colbase = seg * 1024 + hh * 512
                    bank = pss[hh]
                    for k in range(KC):
                        w = None
                        if k == 0:
                            R.wait("tensor", need_w[0], need_w[1])
                            blk = seg * 8 + hh * 4 + 3
                            if cc == 0:
                                R.wait("tensor", ln_x8_led[blk][0][0],
                                       ln_x8_led[blk][0][1])
                            R.wait("tensor", ln_x8_led[blk][1][0],
                                       ln_x8_led[blk][1][1])
                            w = v_att.cur() if wsel == 0 else s_att.cur()
                        T(lambda t, k=k, b=bank, wb=wbuf, cb=colbase: t.matmul(
                            b[:], wb[:, k * 128:(k + 1) * 128],
                            x8T3[:, k, cb:cb + 512],
                            start=(k == 0), stop=(k == KC - 1)),
                          wait=w if k == 0 else None,
                          sig=(t_if, 1) if k == KC - 1 else None)
                if wsel == 1:
                    fmm_led = (t_if.h, t_if.n)
            tif_now = t_if.n
            R.wait("scalar", t_if, fmm_led[1])
            R.wait("scalar", v_scan, v_scan.n)
            for hh in range(2):
                S(lambda s, cc=cc, hh=hh: s.activation(
                    fbuf[:, hh * 512:(hh + 1) * 512], pf_ps[hh][:], AF.Sigmoid,
                    scale=sf[:, cc:cc + 1]))
                S(lambda s, cc=cc, hh=hh: s.activation(
                    f2buf[:, hh * 512:(hh + 1) * 512], pf_ps[hh][:], AF.Sigmoid,
                    scale=nsf[:, cc:cc + 1]), sig=(s_att, 1))
            satt_now = s_att.n
            R.wait("vector", t_if, tif_now)
            R.wait("vector", s_att, satt_now)
            for hh in range(2):
                V(lambda v, cc=cc, hh=hh: v.scalar_tensor_tensor(
                    f2buf[:, hh * 512:(hh + 1) * 512], pi_ps[hh][:],
                    si[:, cc:cc + 1], f2buf[:, hh * 512:(hh + 1) * 512],
                    ALU.mult, ALU.mult), sig=(v_att, 1))
            if seg == 0:
                V(lambda v: v.tensor_tensor_scan(
                    hpre[:], fbuf[:], f2buf[:], 0.0, ALU.mult, ALU.add),
                  sig=(v_scan, 1))
            else:
                if cc >= 3:
                    R.wait("vector", d_hsp[cc % 3], hsp_led[cc - 3][1])
                V(lambda v, cc=cc: v.tensor_tensor_scan(
                    Hrot[cc % 3][:], fbuf[:], f2buf[:], hpre[:, PRE - 1:PRE],
                    ALU.mult, ALU.add), sig=(v_scan, 1))
        R.wait("scalar", t_h2, t_h2.n)
        S(lambda s, cc=cc: s.activation(hsqb[cc % 3][:], Hrot[cc % 3][:],
                                        AF.Square),
          wait=v_scan.cur(), sig=(s_hsq, 1))
        dhs = d_hsp[cc % 3]
        DMA("sync", lambda e, cc=cc: e.dma_start(
            out=h_sc[cc], in_=Hrot[cc % 3][:]), sig=(dhs, 16),
            wait=s_hsq.cur())
        hsp_led[cc] = (dhs.h, dhs.n)
        if cc >= 2:
            emit_h2_sum(cc - 2)

    for cc in range(KC):
        emit_attn_chunk(cc)
        if gu_i < len(gu_groups):
            emit_prep_group(*gu_groups[gu_i])
            gu_i += 1
        if cc < 6 and gu_i < len(gu_groups):
            emit_prep_group(*gu_groups[gu_i])
            gu_i += 1
    emit_h2_sum(KC - 2)
    emit_h2_sum(KC - 1)
    while gu_i < len(gu_groups):
        emit_prep_group(*gu_groups[gu_i])
        gu_i += 1
    flush_prep()

    def emit_ws_row(cols, nchunks, dr_idx):
        n = nchunks
        R.wait("vector", g_pmax, g_pmax.n)
        R.wait("vector", t_ptr, t_ptr.n)
        V(lambda v, n=n, cl=cols: v.tensor_copy(wtmp[:, 0:n], cl[:, 0:n]),
          sig=(v_misc, 1))
        T(lambda t, n=n: t.matmul(pp_tr[0][0:n, 0:128], wtmp[:, 0:n],
                                  identf[:], start=True, stop=True),
          wait=v_misc.cur(), sig=(t_ptr, 1))
        R.wait("vector", d_wsr, d_wsr.n)
        V(lambda v, n=n: v.tensor_copy(wst_sb[0:n, :], pp_tr[0][0:n, 0:128]),
          wait=t_ptr.cur(), sig=(v_pev, 1))
        DMA("sync", lambda e, n=n, di=dr_idx: e.dma_start(
            out=ws_dr[di][0:n, :], in_=wst_sb[0:n, :]), sig=(d_wsr, 16),
            wait=v_pev.cur())

    emit_ws_row(wsg_cols, IC, 0)
    emit_ws_row(wsu_cols, IC, 1)

    R.wait("vector", t_h2, t_h2.n)
    V(lambda v: v.tensor_copy(ssum[0:1, 0:512], ph2[0][0:1, :]))
    V(lambda v: v.tensor_copy(ssum[0:1, 512:1024], ph2[1][0:1, :]))
    V(lambda v: v.tensor_scalar(ssum[:], ssum[:], 1.0 / D, EPS, ALU.mult,
                                ALU.add))
    rsqrt_nr(rstd_r[:], ssum[:], ir1[:], ir2[:], hpre[0:1, :], rstd_b[0:1, :])
    V(lambda v: v.engine_nop(), sig=(v_misc, 1))
    vm_rstd = v_misc.n
    for hh in range(2):
        T(lambda t, hh=hh: t.matmul(
            ph2[hh][:, :], ones_row[:], rstd_r[0:1, hh * 512:(hh + 1) * 512],
            start=True, stop=True, skip_group_check=True),
          wait=(v_misc.h, vm_rstd), sig=(t_bc, 1))
        V(lambda v, hh=hh: v.tensor_copy(
            rstd_b[:, hh * 512:(hh + 1) * 512], ph2[hh][:, :]),
          wait=t_bc.cur(), sig=(v_rb, 1))

    sgbuf = fbuf
    g2buf = f2buf
    o1buf = hpre
    R.wait("scalar", v_scan, v_scan.n)

    def emit_epi_chunk(cc):
        b = wifb[cc % 2]
        dgs = d_wg[cc % 2]
        wdma = (t_if.cur() if cc < 2 else t_g.lag(2))
        DMA("gpsimd", lambda e, cc=cc, b=b: e.dma_start(
            out=b[:], in_=d_in["wg_sw"][cc]), sig=(dgs, 16), wait=wdma)
        need = (dgs, dgs.n)
        hb = Hrot[cc % 3]
        dhr = d_hrl[cc % 3]
        last_spill = {0: 15, 1: 13, 2: 14}[cc % 3]
        DMA("sync", lambda e, cc=cc, h=hb: e.dma_start(
            out=h[:], in_=h_sc[cc]), sig=(dhr, 16),
            wait=(hsp_led[last_spill] if cc < 3 else v_epi.lag(2)))
        hrl_led[cc] = (dhr.h, dhr.n)
        bankset = pi_ps if cc % 2 == 0 else pf_ps
        for hh in range(2):
            bank = bankset[hh]
            for k in range(KC):
                w = None
                if k == 0:
                    R.wait("tensor", need[0], need[1])
                    w = (v_epi.lag(1) if cc >= 2 else
                         ((v_att.h, v_att.n) if cc == 0 else (s_att.h, s_att.n)))
                T(lambda t, k=k, b2=bank, wb=b, hh=hh: t.matmul(
                    b2[:], wb[:, k * 128:(k + 1) * 128],
                    x8T3[:, k, 1024 + hh * 512:1024 + (hh + 1) * 512],
                    start=(k == 0), stop=(k == KC - 1)),
                  wait=w if k == 0 else None,
                  sig=(t_g, 1) if k == KC - 1 else None)
        tg_now = t_g.n
        R.wait("scalar", t_g, tg_now)
        R.wait("scalar", v_epi, v_epi.n)
        for hh in range(2):
            S(lambda s, cc=cc, hh=hh, bk=bankset[hh]: s.activation(
                sgbuf[:, hh * 512:(hh + 1) * 512], bk[:], AF.Sigmoid,
                scale=sgv[:, cc:cc + 1]), sig=(s_epi, 1))
        sep_now = s_epi.n
        R.wait("vector", t_g, tg_now)
        R.wait("vector", s_epi, sep_now)
        R.wait("vector", d_hrl[cc % 3], hrl_led[cc][1])
        for hh in range(2):
            V(lambda v, cc=cc, hh=hh, bk=bankset[hh]: v.tensor_scalar_mul(
                g2buf[:, hh * 512:(hh + 1) * 512], bk[:],
                sgv[:, cc:cc + 1]))
        V(lambda v: v.tensor_tensor(g2buf[:], g2buf[:], sgbuf[:], ALU.mult))
        V(lambda v, cc=cc, h=hb: v.scalar_tensor_tensor(
            o1buf[:], h[:], gso[:, cc:cc + 1], rstd_b[:], ALU.mult, ALU.mult),
          wait=(v_rb.h, 2) if cc == 0 else None)
        V(lambda v, cc=cc: v.tensor_tensor(
            x8T3[:, cc, 0:1024], o1buf[:], g2buf[:], ALU.mult), sig=(v_epi, 1))

    for cc in range(KC):
        emit_epi_chunk(cc)
    esA.close()
    esP.close()

    esB = ExitStack()
    sbB, psB = mk(esB)
    woq2 = [[sbB(f"woq_{p_}_{k}", [128, 512], BF16) for k in range(KC)]
            for p_ in range(2)]
    sst = [sbB(f"sst{i_}", [128, 512], F32) for i_ in range(2)]
    smg = sbB("smg", [128, 512], F32)
    po_ps = [pbank[0], pbank[1]]
    natd = [sbB("natd0", [128, D], F32), sbB("natd1", [128, D], F32),
            sbB("natd2", [128, 1536], F32)]
    q8d = [sbB(f"q8d{i_}", [128, I], FP8) for i_ in range(4)]
    pstgd = sbB("pstgd", [128, IC * 512], FP8)
    ppd = [pbank[2], pbank[3]]

    dq_spq = {}
    dtr_led = {}
    dsd_led = {}

    def emit_down_quant(gi):
        widths = [(0, 2048), (2048, 2048), (4096, 1536)]
        if gi >= 1:
            R.wait("vector", t_ptr, dtr_led[gi - 1])
        for j in range(4):
            c = gi * 4 + j
            led = []
            for pi_, (off, w) in enumerate(widths):
                buf = natd[pi_]
                dnd = d_natd[pi_ % 2] if pi_ < 2 else d_natd2
                if gi == 0 and j == 0:
                    wdma = v_epi.cur() if pi_ == 0 else None
                else:
                    wdma = s_pq.lag(2 - pi_) if pi_ < 2 else s_pq.cur()
                DMA("scalar", lambda e, b=buf, c=c, o=off, w=w: e.dma_start(
                    out=b[:, 0:w], in_=d_in["wdn_nat"][c][:, o:o + w]),
                    sig=(dnd, 16), wait=wdma)
                led.append((dnd.h, dnd.n))
            for pi_, (off, w) in enumerate(widths):
                buf = natd[pi_]
                R.wait("vector", led[pi_][0], led[pi_][1])
                if pi_ == 0:
                    R.wait("vector", s_pq, s_pq.n)
                    V(lambda v, b=buf, w=w: v.reduce_max(
                        wmaxc[:], b[:, 0:w], axis=AX.X,
                        apply_absolute_value=True), sig=(g_pmax, 1))
                else:
                    V(lambda v, b=buf, w=w: v.reduce_max(
                        wn2[:], b[:, 0:w], axis=AX.X,
                        apply_absolute_value=True))
                    V(lambda v: v.tensor_max(wmaxc[:], wmaxc[:], wn2[:]),
                      sig=(g_pmax, 1))
            V(lambda v, c=c: v.tensor_copy(wsd_cols[:, c:c + 1], wmaxc[:]),
              sig=(g_pmax, 1))
            recip_nr(wrec[:], wmaxc[:], wn1[:])
            V(lambda v: v.tensor_scalar_mul(wrec[:], wrec[:], 224.0),
              sig=(v_misc, 1))
            for pi_, (off, w) in enumerate(widths):
                buf = natd[pi_]
                V(lambda v, b=buf, j=j, o=off, w=w: v.tensor_scalar(
                    q8d[j][:, o:o + w], b[:, 0:w], wrec[:], None, ALU.mult),
                  sig=(s_pq, 1))
        dq_spq[gi] = s_pq.n

    def emit_down_tr(gi):
        for icc in range(IC):
            bank = ppd[icc % 2]
            for j in range(4):
                w = None
                if j == 0:
                    w = (s_pq.h, dq_spq[gi]) if icc == 0 else v_pev.lag(1)
                T(lambda t, j=j, icc=icc, b=bank: t.matmul(
                    b[:, j * 128:(j + 1) * 128],
                    q8d[j][:, icc * 128:(icc + 1) * 128], ident8[:],
                    start=True, stop=True),
                  wait=w, sig=(t_ptr, 1) if j == 3 else None)
            if icc % 11 == 0 and gi >= 1:
                wsem, wval = dsd_led[(gi - 1, icc // 11)]
                R.wait("vector", wsem, wval)
            V(lambda v, icc=icc, b=bank: v.tensor_copy(
                pstgd[:, icc * 512:(icc + 1) * 512], b[:]),
              wait=t_ptr.cur(), sig=(v_pev, 1))
            if icc % 11 == 10:
                p = icc // 11
                DMA("sync", lambda e, gi=gi, p=p: e.dma_start(
                    out=qwd_sc[gi][:, p * 5632:(p + 1) * 5632],
                    in_=pstgd[:, p * 5632:(p + 1) * 5632]),
                    sig=(d_scrd4[p], 16), wait=v_pev.cur())
                dsd_led[(gi, p)] = (d_scrd4[p], d_scrd4[p].n)
        dtr_led[gi] = t_ptr.n

    for _ds in d_scr2:
        R.wait("sync", _ds, _ds.n)
        R.wait("scalar", _ds, _ds.n)
        R.wait("vector", _ds, _ds.n)
        R.wait("gpsimd", _ds, _ds.n)
    R.wait("sync", d_wsr, d_wsr.n)
    for _ds in d_scrd4:
        R.wait("sync", _ds, _ds.n)
    R.wait("sync", v_epi, KC)
    R.wait("gpsimd", d_wsr, d_wsr.n)
    R.wait("gpsimd", v_epi, KC)
    oproj_sst_led = {}
    oproj_top_led = {}

    def emit_oproj_od(od):
        wq = woq2[od % 2]
        dwo = d_wo2[od % 2]
        for k in range(KC):
            wdma = ((v_epi.h, KC) if od < 2
                    else (t_op.h, oproj_top_led[od - 2]))
            DMA("gpsimd", lambda e, k=k, od=od, wq=wq: e.dma_start(
                out=wq[k][:],
                in_=d_in["woT_sw"][k][:, od * 512:(od + 1) * 512]),
                sig=(dwo, 16), wait=wdma)
        need = (dwo.h, dwo.n)
        sst_led = oproj_sst_led
        for tj in range(8):
            bank = po_ps[tj % 2]
            for cc in range(KC):
                w = None
                if cc == 0:
                    R.wait("tensor", need[0], need[1])
                    if od == 0 and tj == 0:
                        R.wait("tensor", v_epi, v_epi.n)
                        R.wait("tensor", v_rb, 2)
                    w = v_sev.lag(1)
                T(lambda t, cc=cc, b=bank, tj=tj: t.matmul(
                    b[:], x8T3[:, cc, tj * 128:(tj + 1) * 128], wq[cc][:],
                    start=(cc == 0), stop=(cc == KC - 1)),
                  wait=w if cc == 0 else None,
                  sig=(t_op, 1) if cc == KC - 1 else None)
            top_now = t_op.n
            R.wait("vector", t_op, top_now)
            gsst = od * 8 + tj
            if gsst >= 2:
                R.wait("vector", d_so[gsst % 2], sst_led[gsst - 2][1])
            V(lambda v, b=bank: v.tensor_scalar(
                smg[:], b[:], 256.0, MAGIC, ALU.mult, ALU.add))
            V(lambda v, tj=tj: v.tensor_scalar(
                sst[tj % 2][:], smg[:], -MAGIC, 1.0 / 256.0, ALU.add, ALU.mult),
              sig=(v_sev, 1))
            dso = d_so[gsst % 2]
            DMA("gpsimd", lambda e, tj=tj, od=od: e.dma_start(
                out=s_sc[tj * 128:(tj + 1) * 128, od * 512:(od + 1) * 512],
                in_=sst[tj % 2][:]), sig=(dso, 16), wait=v_sev.cur())
            sst_led[gsst] = (dso.h, dso.n)
        oproj_top_led[od] = t_op.n

    for od in range(4):
        emit_oproj_od(od)
        if od >= 1:
            emit_down_tr(od - 1)
        emit_down_quant(od)
    emit_down_tr(3)
    emit_ws_row(wsd_cols, KC, 2)

    esB.close()
    esX.close()

    esC = ExitStack()
    sbC, psC = mk(esC)
    Tt = [sbC(f"Tt{i_}", [128, I], F32) for i_ in range(4)]
    qt8r = [sbC(f"qt8_{i_}", [128, I], FP8) for i_ in range(2)]
    qtT = [sbC(f"qtT{i_}", [128, I], FP8) for i_ in range(4)]
    wgq = [sbC(f"wgq{i_}", [128, 2048], FP8) for i_ in range(4)]
    wuq = [sbC(f"wuq{i_}", [128, 2048], FP8) for i_ in range(4)]
    wdq = [sbC(f"wdq{i_}", [128, 2048], FP8) for i_ in range(4)]
    wsb = [sbC(f"wsb{i_}", [128, 512], F32) for i_ in range(2)]
    ayb = sbC("ayb", [128, 512], F32)
    y2b = sbC("y2b", [128, 512], F32)
    sgm8 = sbC("sgm8", [128, 512], F32)
    xrl = [sbC(f"xrl{i_}", [128, 512], F32) for i_ in range(2)]
    srl = [sbC(f"srl{i_}", [128, 512], F32) for i_ in range(2)]
    outb = [sbC(f"outb{i_}", [128, 512], F32) for i_ in range(2)]
    tm1 = sbC("tm1", [128, 1], F32)
    tm2l = [sbC(f"tm2_{i_}", [128, 1], F32) for i_ in range(2)]
    pgu = pbank

    for _ds in d_scr2 + d_scrd4:
        R.wait("sync", _ds, _ds.n)
    R.wait("sync", d_wsr, d_wsr.n)
    R.wait("sync", d_so[0], d_so[0].n)
    R.wait("sync", d_so[1], d_so[1].n)
    R.wait("sync", t_op, t_op.n)
    R.wait("sync", v_sev, v_sev.n)
    R.wait("vector", d_so[0], d_so[0].n)
    R.wait("vector", d_so[1], d_so[1].n)
    R.wait("vector", d_wsr, d_wsr.n)
    for _ds in d_scrd4:
        R.wait("vector", _ds, _ds.n)
    R.wait("vector", t_op, t_op.n)
    R.wait("vector", v_pev, v_pev.n)
    R.wait("vector", s_pev, s_pev.n)
    R.wait("scalar", t_op, t_op.n)
    R.wait("scalar", v_pev, v_pev.n)
    R.wait("tensor", v_sev, v_sev.n)
    R.wait("tensor", v_pev, v_pev.n)
    R.wait("tensor", s_pev, s_pev.n)
    R.wait("tensor", v_qxT, v_qxT.n)
    R.wait("tensor", s_qxT, s_qxT.n)
    for _ds in d_scrd4:
        R.wait("gpsimd", _ds, _ds.n)

    V(lambda v: v.tensor_scalar_mul(sgf[:], rmax[:], FQ))
    V(lambda v: v.tensor_tensor(sg2[:], sgf[:], sgf[:], ALU.mult),
      sig=(v_misc, 1))
    vm_sgf = v_misc.n

    bc_state = {"n": 0}
    bc_use_led = {}

    def bcast_ws(dr_idx, base):
        n = bc_state["n"]
        bc_state["n"] = n + 1
        p = n % 2
        drs = d_row[p]
        DMA("sync", lambda e, di=dr_idx, b=base, p=p: e.dma_start(
            out=wsb[p][:],
            in_=ws_dr[di:di + 1, :, :].rearrange("a b c -> a (b c)")
                [0:1, b:b + 512].partition_broadcast(128)),
            sig=(drs, 16), wait=bc_use_led.get(n - 2))
        return p, (drs.h, drs.n)

    def bc_mark_used(n_offset=0):
        bc_use_led[bc_state["n"] - 1] = (ch_v.h, ch_v.n)

    mm_led = {}
    dq_led = {}
    dn_led = {}
    dqd_led = {}
    qt_sqt_led = {}
    qt_trt_led = {}
    y_led = {}
    ag_led = {}
    up_led = {}
    v_ag = sem("v_ag")

    def load_quarter(wsel, gq):
        sc = qwg_sc if wsel == 0 else qwu_sc
        ring = wgq if wsel == 0 else wuq
        ic_ = (gq % (IB * 4)) // 4
        q = gq % 4
        wq = mm_led.get((wsel, gq - 4))
        dqs = d_q8[wsel][gq % 4]
        DMA("sync", lambda e, sc=sc, ic_=ic_, q=q, b=ring[gq % 4]: e.dma_start(
            out=b[:], in_=sc[ic_][:, q * 2048:(q + 1) * 2048]),
            sig=(dqs, 16), wait=wq)
        dq_led[(wsel, gq)] = (dqs.h, dqs.n)

    for wsel in range(2):
        for gq0 in range(4):
            load_quarter(wsel, gq0)

    GB, UB = pgu[0:4], pgu[4:8]
    for TS in range(2):
        R.wait("vector", s_qt, s_qt.n)
        R.wait("vector", t_dn, t_dn.n)
        R.wait("tensor", v_fin, v_fin.n)
        for ic in range(IB):
            bc_g = bcast_ws(0, ic * 512)
            bc_u = bcast_ws(1, ic * 512)
            for wsel, bufs, sem4, banks in ((0, wgq, t_g4, GB),
                                            (1, wuq, t_u4, UB)):
                glb = TS * IB + ic
                guard = (ag_led if wsel == 0 else up_led).get(glb - 1)
                if guard is not None:
                    R.wait("tensor", guard[0], guard[1])
                for k in range(KC):
                    gq = TS * IB * 4 + ic * 4 + k // 4
                    for tj in range(4):
                        if k % 4 == 0 and tj == 0:
                            R.wait("tensor", dq_led[(wsel, gq)][0],
                                   dq_led[(wsel, gq)][1])
                        last_q = (k % 4 == 3 and tj == 3)
                        T(lambda t, k=k, tj=tj, bu=bufs, sl=gq % 4, TS=TS,
                          bks=banks: t.matmul(
                            bks[tj][:],
                            qxT3[:, k, TS * 512 + tj * 128:TS * 512 + (tj + 1) * 128],
                            bu[sl][:, (k % 4) * 512:(k % 4 + 1) * 512],
                            start=(k == 0), stop=(k == KC - 1)),
                          sig=(sem4, 1) if last_q else None)
                        if last_q:
                            mm_led[(wsel, gq)] = (sem4.h, sem4.n)
                            if gq + 4 < 2 * IB * 4:
                                load_quarter(wsel, gq + 4)
            glb = TS * IB + ic
            R.wait("vector", t_g4, t_g4.n)
            R.wait("vector", bc_g[1][0], bc_g[1][1])
            for tj in range(4):
                V(lambda v, tj=tj, ic=ic, p=bc_g[0]: v.tensor_tensor(
                    Tt[tj][:, ic * 512:(ic + 1) * 512], GB[tj][:],
                    wsb[p][:], ALU.mult),
                  wait=(v_misc.h, vm_sgf) if (TS == 0 and ic == 0
                                              and tj == 0) else None,
                  sig=(v_ag, 1))
            ag_led[glb] = (v_ag.h, v_ag.n)
            bc_use_led[bc_state["n"] - 2] = (v_ag.h, v_ag.n)
            R.wait("scalar", t_u4, t_u4.n)
            R.wait("scalar", v_ffn, v_ffn.n)
            R.wait("vector", t_u4, t_u4.n)
            R.wait("vector", bc_u[1][0], bc_u[1][1])
            for tj in range(4):
                S(lambda s, tj=tj, ic=ic, TS=TS: s.activation(
                    sgm8[:], Tt[tj][:, ic * 512:(ic + 1) * 512],
                    AF.Sigmoid,
                    scale=sgf[:, TS * 4 + tj:TS * 4 + tj + 1]),
                  wait=v_ag.cur() if tj == 0 else v_ffn.cur(), sig=(s_ffn, 1))
                V(lambda v, tj=tj, p=bc_u[0]: v.tensor_tensor(
                    ayb[:], UB[tj][:], wsb[p][:], ALU.mult))
                V(lambda v, tj=tj, ic=ic, TS=TS: v.scalar_tensor_tensor(
                    y2b[:], Tt[tj][:, ic * 512:(ic + 1) * 512],
                    sg2[:, TS * 4 + tj:TS * 4 + tj + 1], ayb[:],
                    ALU.mult, ALU.mult))
                V(lambda v, tj=tj, ic=ic: v.tensor_tensor(
                    Tt[tj][:, ic * 512:(ic + 1) * 512], y2b[:], sgm8[:],
                    ALU.mult), wait=s_ffn.cur(), sig=(v_ffn, 1))
            up_led[glb] = (v_ffn.h, v_ffn.n)
            bc_use_led[bc_state["n"] - 1] = (v_ffn.h, v_ffn.n)
        def emit_t_quant(tj):
            tm2b = tm2l[tj % 2]
            V(lambda v, tj=tj: v.reduce_max(tm1[:], Tt[tj][:], axis=AX.X,
                                            apply_absolute_value=True))
            V(lambda v, tj=tj, TS=TS: v.tensor_scalar_mul(
                tyf[:, TS * 4 + tj:TS * 4 + tj + 1], tm1[:], FQ))
            pw = qt_sqt_led.get((TS, tj - 2))
            if pw is not None:
                R.wait("vector", s_qt, pw)
            recip_nr(tm2b[:], tm1[:], wn1[:])
            V(lambda v, b=tm2b: v.tensor_scalar_mul(b[:], b[:], 224.0),
              sig=(v_tmax, 1))
            ptr = qt_trt_led.get((TS, tj - 2), qt_trt_led.get((TS - 1, tj + 2)))
            if ptr is not None:
                R.wait("scalar", t_trt, ptr)
            S(lambda s, tj=tj, b=tm2b: s.activation(
                qt8r[tj % 2][:], Tt[tj][:], AF.Copy, scale=b[:]),
              wait=v_tmax.cur(), sig=(s_qt, 1))
            qt_sqt_led[(TS, tj)] = s_qt.n

        def emit_t_tr(tj):
            if TS == 0 and tj == 0:
                R.wait("tensor", v_ffn, v_ffn.n)
            for g4 in range(11):
                bank = pgu[g4 % 2]
                for dk in range(4):
                    icc = g4 * 4 + dk
                    w = None
                    if dk == 0:
                        bg = qtT_bank_led.get(g4 % 2)
                        if g4 == 0:
                            w = (s_qt.h, qt_sqt_led[(TS, tj)])
                            if bg is not None:
                                R.wait("tensor", bg[0], bg[1])
                        else:
                            w = bg
                    T(lambda t, icc=icc, b=bank, dk=dk, tj=tj: t.matmul(
                        b[:, dk * 128:(dk + 1) * 128],
                        qt8r[tj % 2][:, icc * 128:(icc + 1) * 128], ident8[:],
                        start=True, stop=True),
                      wait=w, sig=(t_trt, 1) if dk == 3 else None)
                V(lambda v, g4=g4, b=bank, tj=tj: v.tensor_copy(
                    qtT[tj][:, g4 * 512:(g4 + 1) * 512], b[:]),
                  wait=t_trt.cur(), sig=(v_qtT, 1))
                qtT_bank_led[g4 % 2] = (v_qtT.h, v_qtT.n)
            qt_trt_led[(TS, tj)] = t_trt.n

        qtT_bank_led = {}
        for tj in range(4):
            emit_t_quant(tj)
            if tj >= 1:
                emit_t_tr(tj - 1)
        emit_t_tr(3)
        for hc in range(4):
            bc_d = bcast_ws(2, hc * 512)
            for ib_ in range(IB):
                gb = (TS * 4 + hc) * IB + ib_
                dqd = d_q8d[gb % 4]
                DMA("sync", lambda e, ib_=ib_, hc=hc, b=wdq[gb % 4]:
                    e.dma_start(out=b[:],
                                in_=qwd_sc[hc][:, ib_ * 2048:(ib_ + 1) * 2048]),
                    sig=(dqd, 16), wait=dn_led.get(gb - 4))
                dqd_led[gb] = (dqd.h, dqd.n)
                R.wait("tensor", dqd_led[gb][0], dqd_led[gb][1])
                if ib_ == 0:
                    R.wait("tensor", v_qtT, v_qtT.n)
                    R.wait("tensor", v_fin, v_fin.n)
                for sl4 in range(4):
                    icc = ib_ * 4 + sl4
                    for tj in range(4):
                        T(lambda t, icc=icc, tj=tj, gb=gb, sl4=sl4: t.matmul(
                            UB[tj][:], qtT[tj][:, icc * 128:(icc + 1) * 128],
                            wdq[gb % 4][:, sl4 * 512:(sl4 + 1) * 512],
                            start=(icc == 0), stop=(icc == IC - 1)),
                          sig=(t_dn, 1))
                dn_led[gb] = (t_dn.h, t_dn.n)
            tdn_now = t_dn.n
            for tj in range(4):
                rowb = TS * 512 + tj * 128
                gfin = (TS * 4 + hc) * 4 + tj
                dsxs = d_sx[tj % 2]
                dsss = d_ss[tj % 2]
                DMA("scalar", lambda e, rowb=rowb, hc=hc, b=xrl[tj % 2]:
                    e.dma_start(out=b[:],
                                in_=d_in["xown"][rowb:rowb + 128,
                                                 hc * 512:(hc + 1) * 512]),
                    sig=(dsxs, 16), wait=v_fin.lag(1))
                DMA("scalar", lambda e, rowb=rowb, hc=hc, b=srl[tj % 2]:
                    e.dma_start(out=b[:],
                                in_=s_sc[rowb:rowb + 128,
                                         hc * 512:(hc + 1) * 512]),
                    sig=(dsss, 16), wait=v_fin.lag(1))
                dss_now = (dsss, dsss.n)
                dsx_now = (dsxs, dsxs.n)
                R.wait("vector", t_dn, tdn_now)
                R.wait("vector", bc_d[1][0], bc_d[1][1])
                V(lambda v, tj=tj, p=bc_d[0]: v.tensor_tensor(
                    ayb[:], UB[tj][:], wsb[p][:], ALU.mult))
                S(lambda s, tj=tj, TS=TS: s.activation(
                    sgm8[:], ayb[:], AF.Sigmoid,
                    scale=tyf[:, TS * 4 + tj:TS * 4 + tj + 1]),
                  wait=(ch_v.h, ch_v.n), sig=(s_ffn, 1))
                V(lambda v, tj=tj, TS=TS: v.tensor_scalar_mul(
                    y2b[:], ayb[:], tyf[:, TS * 4 + tj:TS * 4 + tj + 1]))
                V(lambda v: v.tensor_tensor(y2b[:], y2b[:], sgm8[:], ALU.mult),
                  wait=s_ffn.cur())
                R.wait("vector", dss_now[0], dss_now[1])
                R.wait("vector", dsx_now[0], dsx_now[1])
                V(lambda v, tj=tj: v.tensor_tensor(y2b[:], y2b[:],
                                                   srl[tj % 2][:], ALU.mult))
                V(lambda v, tj=tj: v.tensor_tensor(outb[tj % 2][:], y2b[:],
                                                   xrl[tj % 2][:], ALU.add),
                  wait=y_led.get(gfin - 2), sig=(v_fin, 1))
                dys = d_y[gfin % 2]
                DMA("gpsimd", lambda e, rowb=rowb, hc=hc, b=outb[tj % 2]:
                    e.dma_start(out=y_out[rowb:rowb + 128,
                                          hc * 512:(hc + 1) * 512], in_=b[:]),
                    sig=(dys, 16), wait=v_fin.cur())
                y_led[gfin] = (dys.h, dys.n)
            bc_use_led[bc_state["n"] - 1] = (ch_v.h, ch_v.n)

    R.wait("gpsimd", d_y[0], d_y[0].n)
    R.wait("gpsimd", d_y[1], d_y[1].n)
    R.wait("sync", d_y[0], d_y[0].n)
    R.wait("sync", d_y[1], d_y[1].n)

    with nc.Block() as block:
        @block.sync
        def _(e):
            R.replay("sync", e)

        @block.vector
        def _(e):
            R.replay("vector", e)

        @block.scalar
        def _(e):
            R.replay("scalar", e)

        @block.gpsimd
        def _(e):
            R.replay("gpsimd", e)

        @block.tensor
        def _(e):
            R.replay("tensor", e)

    esC.close()
    es0.close()
    return nc


def _host_prepare(ins):
    x = np.ascontiguousarray(ins["L0_attn_hidden_states"], dtype=np.float32)
    w_i, w_f, w_g, w_o = (np.asarray(ins[f"L0_attn_w_{k}"], dtype=np.float32)
                          for k in "ifgo")
    down_idx = np.asarray(ins["L0_ffn_down_reorder_index"]).astype(np.int64)

    def strip_sw(w):
        t = w.reshape(KC, 128, KC, 128)
        t = np.transpose(t, (2, 1, 0, 3))
        return np.ascontiguousarray(t.reshape(KC, 128, D), dtype=np.float32)

    def vec_img(v):
        return np.ascontiguousarray(
            np.asarray(v, dtype=np.float32).reshape(KC, 128).T)

    shared = {
        "wi_sw": strip_sw(w_i), "wf_sw": strip_sw(w_f), "wg_sw": strip_sw(w_g),
        "woT_sw": np.ascontiguousarray(w_o.T.reshape(KC, 128, D),
                                       dtype=np.float32),
        "si_img": vec_img(ins["L0_attn_s_i"]),
        "sf_img": vec_img(ins["L0_attn_s_f"]),
        "sg_img": vec_img(ins["L0_attn_s_g"]),
        "so_img": vec_img(ins["L0_attn_s_o"]),
        "gnw_img": vec_img(ins["L0_attn_g_norm_w"]),
        "wgat_nat": np.ascontiguousarray(
            np.asarray(ins["L0_ffn_w_gate"], dtype=np.float32)[down_idx]
            .reshape(IC, 128, D)),
        "wup_nat": np.ascontiguousarray(
            np.asarray(ins["L0_ffn_w_up"], dtype=np.float32)[down_idx]
            .reshape(IC, 128, D)),
        "wdn_nat": np.ascontiguousarray(
            np.asarray(ins["L0_ffn_w_down"], dtype=np.float32)
            .reshape(KC, 128, I)),
    }
    zeros_pre = np.zeros((PRE, D), dtype=np.float32)
    in_maps = []
    for core in range(8):
        b, half = core // 2, core % 2
        m = dict(shared)
        m["xown"] = np.ascontiguousarray(x[b, half * TOK:(half + 1) * TOK, :])
        m["xpre"] = (np.ascontiguousarray(x[b, 0:PRE, :]) if half == 1
                     else zeros_pre)
        in_maps.append(m)
    return in_maps


def kernel(**inputs):
    in_maps = _host_prepare(inputs)
    nc = build_program()
    res = run_bass_kernel_spmd(nc, in_maps, core_ids=list(range(8)))
    out = np.empty((4, 2048, D), dtype=np.float32)
    for core in range(8):
        b, half = core // 2, core % 2
        out[b, half * TOK:(half + 1) * TOK, :] = np.asarray(res.results[core]["y"])
    return out


# version 23
# speedup vs baseline: 1.1151x; 1.0203x over previous
import numpy as np
import concourse.bass as bass
import concourse.mybir as mybir
from concourse.bass_utils import run_bass_kernel_spmd

F32 = mybir.dt.float32
BF16 = mybir.dt.bfloat16
FP8 = mybir.dt.float8e4
I32 = mybir.dt.int32
AF = mybir.ActivationFunctionType
ALU = mybir.AluOpType
AX = mybir.AxisListType

D = 2048
KC = D // 128
TOK = 1024
PRE = 1024
I = 5632
IC = I // 128
IB = I // 512
MAGIC = float(np.float32(1.5 * 2 ** 23))
EPS = 1e-5
RSQRT_C = 0x5F3759DF
FQ = 4.0 / (448.0 * 448.0)


class SemC:
    def __init__(self, h):
        self.h = h
        self.n = 0

    def inc(self, k=1):
        self.n += k
        return (self.h, self.n)

    def cur(self):
        return (self.h, self.n)

    def lag(self, k):
        return (self.h, max(0, self.n - k))


class Rec:
    CHAINED = ("vector", "scalar", "gpsimd")

    def __init__(self):
        self.st = {e: [] for e in ("sync", "vector", "scalar", "gpsimd", "tensor")}
        self.last = {e: None for e in self.st}

    def emit(self, eng, fn, wait=None, sig=None, chain_sem=None, nochain=False):
        if eng in self.CHAINED and wait is not None:
            if wait[1] > 0:
                self.st[eng].append(("W", wait[0], wait[1]))
            wait = None
        last = self.last[eng] if (eng in self.CHAINED and not nochain) else None
        if sig is not None:
            sem, amount = sig
            tgt = sem.inc(amount)
            upd = (tgt[0], amount)
        elif eng in self.CHAINED and chain_sem is not None:
            tgt = chain_sem.inc(1)
            upd = (tgt[0], 1)
        else:
            tgt, upd = None, None
        self.st[eng].append(("I", fn, wait, last, upd))
        if eng in self.CHAINED and tgt is not None and not nochain:
            self.last[eng] = tgt
        return tgt

    def wait(self, eng, sem, val=None):
        v = sem.n if val is None else val
        h = sem.h if hasattr(sem, "h") else sem
        if v > 0:
            self.st[eng].append(("W", h, v))

    def replay(self, eng, engine):
        for item in self.st[eng]:
            if item[0] == "W":
                engine.wait_ge(item[1], item[2])
                continue
            _, fn, wait, last, upd = item
            ins = fn(engine)
            if wait is not None and wait[1] > 0:
                ins._wait_ge(wait[0], wait[1])
            elif last is not None:
                ins._wait_ge(last[0], last[1])
            if upd is not None:
                ins.then_inc(upd[0], upd[1])


def build_program():
    nc = bass.Bass("TRN2", target_bir_lowering=False, debug=False)

    d_in = {}
    for name, shape in [
        ("xpre", [PRE, D]), ("xown", [TOK, D]),
        ("wi_sw", [KC, 128, D]), ("wf_sw", [KC, 128, D]),
        ("wg_sw", [KC, 128, D]), ("woT_sw", [KC, 128, D]),
        ("si_img", [128, KC]), ("sf_img", [128, KC]), ("sg_img", [128, KC]),
        ("so_img", [128, KC]), ("gnw_img", [128, KC]),
        ("wgat_nat", [IC, 128, D]), ("wup_nat", [IC, 128, D]),
        ("wdn_nat", [KC, 128, I]),
    ]:
        d_in[name] = nc.dram_tensor(name, shape, F32, kind="ExternalInput").ap()
    y_out = nc.dram_tensor("y", [TOK, D], F32, kind="ExternalOutput").ap()

    qwg_sc = nc.dram_tensor("qwg_sc", [IB, 128, KC * 512], FP8, kind="Internal").ap()
    qwu_sc = nc.dram_tensor("qwu_sc", [IB, 128, KC * 512], FP8, kind="Internal").ap()
    qwd_sc = nc.dram_tensor("qwd_sc", [4, 128, IC * 512], FP8, kind="Internal").ap()
    woT_bf = nc.dram_tensor("woT_bf", [KC, 128, D], BF16, kind="Internal").ap()
    s_sc = nc.dram_tensor("s_sc", [TOK, D], F32, kind="Internal").ap()
    h_sc = nc.dram_tensor("h_sc", [KC, 128, TOK], F32, kind="Internal").ap()
    ws_dr = nc.dram_tensor("ws_dr", [3, IC, 128], F32, kind="Internal").ap()

    from contextlib import ExitStack
    es0 = ExitStack()

    def mk(stack):
        def sb(name, shape, dtype):
            return stack.enter_context(nc.sbuf_tensor(name, shape, dtype))

        def ps(name, shape, dtype=F32):
            return stack.enter_context(nc.psum_tensor(name, shape, dtype))
        return sb, ps

    sb0, ps0 = mk(es0)

    def sem(name):
        return SemC(es0.enter_context(nc.semaphore(name)))

    si = sb0("si", [128, KC], F32)
    sf = sb0("sf", [128, KC], F32)
    nsf = sb0("nsf", [128, KC], F32)
    sgv = sb0("sgv", [128, KC], F32)
    sov = sb0("sov", [128, KC], F32)
    gnw = sb0("gnw", [128, KC], F32)
    gso = sb0("gso", [128, KC], F32)
    rmax = sb0("rmax", [128, 8], F32)
    sgf = sb0("sgf", [128, 8], F32)
    sg2 = sb0("sg2", [128, 8], F32)
    tyf = sb0("tyf", [128, 8], F32)
    identb = sb0("identb", [128, 128], BF16)
    ident8 = sb0("ident8", [128, 128], FP8)
    identf = sb0("identf", [128, 128], F32)
    ones_col = sb0("ones_col", [128, 1], F32)
    ones_row = sb0("ones_row", [1, 128], F32)
    row_t = sb0("row_t", [1, 512], F32)
    wsg_cols = sb0("wsg_cols", [128, IC], F32)
    wsu_cols = sb0("wsu_cols", [128, IC], F32)
    wsd_cols = sb0("wsd_cols", [128, KC], F32)
    qxT3 = sb0("qxT3", [128, KC, TOK], FP8)
    ssum = sb0("ssum", [1, TOK], F32)
    rstd_r = sb0("rstd_r", [1, TOK], F32)
    ir1 = sb0("ir1", [1, TOK], I32)
    ir2 = sb0("ir2", [1, TOK], I32)
    wmaxc = sb0("wmaxc", [128, 1], F32)
    wrec = sb0("wrec", [128, 1], F32)
    wn1 = sb0("wn1", [128, 1], F32)
    wn2 = sb0("wn2", [128, 1], F32)
    li1 = sb0("li1", [128, 1], I32)
    li2 = sb0("li2", [128, 1], I32)
    gm1 = sb0("gm1", [128, 1], F32)
    gm2 = sb0("gm2", [128, 1], F32)
    wst_sb = sb0("wst_sb", [IC, 128], F32)
    wtmp = sb0("wtmp", [128, IC], F32)
    wtmp2 = sb0("wtmp2", [128, IC], F32)
    lnm_c = sb0("lnm_c", [128, 16], F32)
    lsq_c = sb0("lsq_c", [128, 16], F32)
    lrs_c = sb0("lrs_c", [128, 16], F32)
    lv1_c = sb0("lv1_c", [128, 16], F32)
    lv2_c = sb0("lv2_c", [128, 16], F32)
    lvi_c = sb0("lvi_c", [128, 16], I32)
    lvi2_c = sb0("lvi2_c", [128, 16], I32)
    pbank = [ps0(f"pbank{i_}", [128, 512], F32) for i_ in range(8)]

    ch_v, ch_s, ch_g = sem("ch_v"), sem("ch_s"), sem("ch_g")
    d_small = sem("d_small")
    d_x = [sem(f"d_x{i_}") for i_ in range(4)]
    d_wif = [sem(f"d_wif{i_}") for i_ in range(4)]
    d_wg = [sem(f"d_wg{i_}") for i_ in range(2)]
    d_wo2 = [sem(f"d_wo{i_}") for i_ in range(2)]
    d_nat = [sem(f"d_nat{i_}") for i_ in range(3)]
    d_natd = [sem(f"d_natd{i_}") for i_ in range(2)]
    d_natd2 = sem("d_natd2")
    d_scr2 = [sem(f"d_scr{i_}") for i_ in range(2)]
    d_scrd4 = [sem(f"d_scrd{i_}") for i_ in range(4)]
    d_q8 = [[sem(f"d_q8_{w_}{i_}") for i_ in range(4)] for w_ in range(2)]
    d_q8d = [sem(f"d_q8d{i_}") for i_ in range(4)]
    d_wc = sem("d_wc")
    d_sx = [sem(f"d_sx{i_}") for i_ in range(2)]
    d_ss = [sem(f"d_ss{i_}") for i_ in range(2)]
    d_so = [sem(f"d_so{i_}") for i_ in range(2)]
    d_hsp = [sem(f"d_hsp{i_}") for i_ in range(3)]
    d_hrl = [sem(f"d_hrl{i_}") for i_ in range(3)]
    d_y = [sem(f"d_y{i_}") for i_ in range(2)]
    d_wsr = sem("d_wsr")
    d_row = [sem(f"d_row{i_}") for i_ in range(2)]
    g_id = sem("g_id")
    g_pmax = sem("g_pmax")
    v_misc = sem("v_misc")
    v_ln = sem("v_ln")
    s_ln = sem("s_ln")
    v_x8T = sem("v_x8T")
    v_qxT = sem("v_qxT")
    s_x8T = sem("s_x8T")
    s_qxT = sem("s_qxT")
    t_trx = sem("t_trx")
    s_pq = sem("s_pq")
    t_ptr = sem("t_ptr")
    v_pev = sem("v_pev")
    s_pev = sem("s_pev")
    t_if = sem("t_if")
    s_att = sem("s_att")
    v_att = sem("v_att")
    v_scan = sem("v_scan")
    s_hsq = sem("s_hsq")
    t_h2 = sem("t_h2")
    t_bc = sem("t_bc")
    v_rb = sem("v_rb")
    t_g = sem("t_g")
    s_epi = sem("s_epi")
    v_epi = sem("v_epi")
    t_op = sem("t_op")
    v_sev = sem("v_sev")
    t_gu = sem("t_gu")
    v_ffn = sem("v_ffn")
    s_ffn = sem("s_ffn")
    v_tmax = sem("v_tmax")
    s_qt = sem("s_qt")
    t_trt = sem("t_trt")
    v_qtT = sem("v_qtT")
    t_dn = sem("t_dn")
    v_fin = sem("v_fin")
    t_wsb = sem("t_wsb")
    t_g4 = sem("t_g4")
    t_u4 = sem("t_u4")
    v_wsb = sem("v_wsb")

    R = Rec()

    def V(fn, wait=None, sig=None):
        return R.emit("vector", fn, wait=wait, sig=sig, chain_sem=ch_v)

    def S(fn, wait=None, sig=None):
        return R.emit("scalar", fn, wait=wait, sig=sig, chain_sem=ch_s)

    def G(fn, wait=None, sig=None):
        return R.emit("gpsimd", fn, wait=wait, sig=sig, chain_sem=ch_g)

    def T(fn, wait=None, sig=None):
        return R.emit("tensor", fn, wait=wait, sig=sig)

    def DMA(eng, fn, sig, wait=None):
        return R.emit(eng, fn, wait=wait, sig=sig, nochain=True)


    def recip_nr(dst, src, f1):
        V(lambda v: v.reciprocal(dst, src))
        V(lambda v: v.tensor_tensor(f1, dst, src, ALU.mult))
        V(lambda v: v.tensor_scalar(f1, f1, -1.0, 2.0, ALU.mult, ALU.add))
        V(lambda v: v.tensor_tensor(dst, dst, f1, ALU.mult))

    def rsqrt_nr(dst, src, it1, it2, f1, f2):
        V(lambda v: v.tensor_scalar(it1, src.bitcast(I32), 1, None,
                                    ALU.logical_shift_right))
        V(lambda v: v.tensor_scalar(it2, it1, -1, RSQRT_C, ALU.mult, ALU.add))
        V(lambda v: v.tensor_copy(dst, it2.bitcast(F32)))
        for _ in range(3):
            V(lambda v: v.tensor_tensor(f1, dst, dst, ALU.mult))
            V(lambda v: v.tensor_tensor(f2, f1, src, ALU.mult))
            V(lambda v: v.tensor_scalar(f2, f2, -0.5, 1.5, ALU.mult, ALU.add))
            V(lambda v: v.tensor_tensor(dst, dst, f2, ALU.mult))

    for tile, name in [(si, "si_img"), (sf, "sf_img"), (sgv, "sg_img"),
                       (sov, "so_img"), (gnw, "gnw_img")]:
        DMA("sync", lambda e, t=tile, n=name: e.dma_start(out=t[:], in_=d_in[n]),
            sig=(d_small, 16))
    for ident in (identb, ident8, identf):
        G(lambda g, t=ident: g.memset(t[:], 0.0))
        G(lambda g, t=ident: g.affine_select(
            out=t[:], in_=t[:], compare_op=ALU.not_equal, fill=1.0, base=0,
            pattern=[[-1, 128]], channel_multiplier=1))
    G(lambda g: g.engine_nop(), sig=(g_id, 1))
    V(lambda v: v.memset(ones_col[:], 1.0))
    V(lambda v: v.memset(ones_row[:], 1.0))
    R.wait("vector", d_small, 16 * 5)
    V(lambda v: v.tensor_mul(gso[:], gnw[:], sov[:]))
    V(lambda v: v.tensor_scalar_mul(nsf[:], sf[:], -1.0), sig=(v_misc, 1))

    esX = ExitStack()
    sbX, _ = mk(esX)
    x8T3 = sbX("x8T3", [128, KC, PRE + TOK], BF16)

    esP = ExitStack()
    sbP, psP = mk(esP)
    natb = [sbP(f"natb{i_}", [128, D], F32) for i_ in range(3)]
    wmax2 = [sbP(f"wmax2{i_}", [128, 1], F32) for i_ in range(2)]
    wrec2 = [sbP(f"wrec2{i_}", [128, 1], F32) for i_ in range(2)]
    wifb = [sbP(f"wifb{i_}", [128, D], BF16) for i_ in range(6)]
    q8n = [sbP(f"q8n{i_}", [128, D], FP8) for i_ in range(8)]
    pstg = [sbP(f"pstg{i_}", [128, 2048], FP8) for i_ in range(2)]
    pp_tr = [pbank[2], pbank[3]]

    esL = ExitStack()
    sbL, psL = mk(esL)
    xint = [sbL(f"xint{i_}", [128, D], F32) for i_ in range(2)]
    xnt = [sbL("xnt0", [128, D], F32)]
    x8s = [sbL(f"x8s{i_}", [128, D], BF16) for i_ in range(2)]
    q8s = [sbL(f"q8s{i_}", [128, D], FP8) for i_ in range(2)]
    lsum = sbL("lsum", [128, 1], F32)
    wrecl = [sbL(f"wrecl{i_}", [128, 1], F32) for i_ in range(2)]
    x4 = [xint[0], xint[1], xnt[0]]
    lssq = sbL("lssq", [128, 1], F32)
    lvar = sbL("lvar", [128, 1], F32)
    lrstd = sbL("lrstd", [128, 1], F32)
    lnm = sbL("lnm", [128, 1], F32)
    lrm = sbL("lrm", [128, 1], F32)
    pb_bf = [pbank[0], pbank[1]]
    pq_f = [pbank[2], pbank[3]]

    gu_groups = [("g", gi) for gi in range(IB)] + [("u", gi) for gi in range(IB)]

    prep_c = {"n": 0}
    tpr_led = {}
    prep_scr_led = {}

    def emit_prep_quant(kind, gi, gidx):
        nat = d_in["wgat_nat"] if kind == "g" else d_in["wup_nat"]
        cols = wsg_cols if kind == "g" else wsu_cols
        for j in range(4):
            c = gi * 4 + j
            gc = prep_c["n"]
            prep_c["n"] = gc + 1
            buf = natb[gc % 3]
            dns = d_nat[gc % 3]
            wmx = wmax2[gc % 2]
            wrc = wrec2[gc % 2]
            DMA("scalar", lambda e, b=buf, n=nat, c=c: e.dma_start(
                out=b[:], in_=n[c]), sig=(dns, 16), wait=s_pq.lag(2))
            R.wait("vector", dns, dns.n)
            R.wait("vector", s_pq, s_pq.lag(1)[1])
            V(lambda v, b=buf, m=wmx: v.reduce_max(m[:], b[:], axis=AX.X,
                                                   apply_absolute_value=True))
            V(lambda v, c=c, cl=cols, m=wmx: v.tensor_copy(cl[:, c:c + 1], m[:]),
              sig=(g_pmax, 1))
            recip_nr(wrc[:], wmx[:], wn1[:])
            V(lambda v, r=wrc: v.tensor_scalar_mul(r[:], r[:], 224.0),
              sig=(v_misc, 1))
            prior_tr = tpr_led.get(gidx - 2)
            if prior_tr is not None:
                R.wait("vector", t_ptr, prior_tr)
            V(lambda v, b=buf, sl=(gidx % 2) * 4 + j, r=wrc: v.tensor_scalar(
                q8n[sl][:], b[:], r[:], None, ALU.mult), sig=(s_pq, 1))

    def emit_prep_tr(kind, gi, gidx, spq_tgt):
        sc = qwg_sc if kind == "g" else qwu_sc
        s_start = s_pev.n
        evac_led = {}
        scr_led = prep_scr_led
        for quarter in range(4):
            for kk in range(4):
                k = quarter * 4 + kk
                bank = pp_tr[k % 2]
                for j in range(4):
                    w = None
                    if j == 0:
                        if k == 0:
                            R.wait("tensor", v_qxT, v_qxT.n)
                            R.wait("tensor", s_qxT, s_qxT.n)
                            w = (s_pq.h, spq_tgt)
                        elif k == 1:
                            w = (s_pev.h, s_start)
                        else:
                            w = evac_led[k - 2]
                    T(lambda t, j=j, k=k, b=bank, sl=(gidx % 2) * 4 + j:
                      t.matmul(
                        b[:, j * 128:(j + 1) * 128],
                        q8n[sl][:, k * 128:(k + 1) * 128],
                        ident8[:], start=True, stop=True),
                      wait=w, sig=(t_ptr, 1) if j == 3 else None)
                qg = gidx * 4 + quarter
                pbuf = pstg[qg % 2]
                if kk == 0 and qg >= 2:
                    wsem, wval = scr_led[qg - 2]
                    R.wait("vector", wsem, wval)
                    R.wait("scalar", wsem, wval)
                if k % 2 == 0:
                    V(lambda v, kk=kk, b=bank, pb=pbuf: v.tensor_copy(
                        pb[:, kk * 512:(kk + 1) * 512], b[:]),
                      wait=t_ptr.cur(), sig=(v_pev, 1))
                    evac_led[k] = (v_pev.h, v_pev.n)
                else:
                    S(lambda s_, kk=kk, b=bank, pb=pbuf: s_.copy(
                        pb[:, kk * 512:(kk + 1) * 512], b[:]),
                      wait=t_ptr.cur(), sig=(s_pev, 1))
                    evac_led[k] = (s_pev.h, s_pev.n)
            R.wait("sync", s_pev, s_pev.n)
            R.wait("sync", v_pev, v_pev.n)
            DMA("sync", lambda e, sc=sc, gi=gi, q=quarter,
                pb=pstg[(gidx * 4 + quarter) % 2]: e.dma_start(
                out=sc[gi][:, q * 2048:(q + 1) * 2048], in_=pb[:]),
                sig=(d_scr2[(gidx * 4 + quarter) % 2], 16))
            scr_led[gidx * 4 + quarter] = (
                d_scr2[(gidx * 4 + quarter) % 2], d_scr2[(gidx * 4 + quarter) % 2].n)
        tpr_led[gidx] = t_ptr.n

    prep_pending = []

    def emit_prep_group(kind, gi):
        gidx = len(prep_pending)
        if gidx >= 1:
            emit_prep_tr(*prep_pending[gidx - 1])
        emit_prep_quant(kind, gi, gidx)
        prep_pending.append((kind, gi, gidx, s_pq.n))

    def flush_prep():
        for gidx in range(len(prep_pending)):
            if gidx not in tpr_led:
                emit_prep_tr(*prep_pending[gidx])

    ln_end = []
    ln_x8_led = []
    x8_bank_led = {}
    qx_bank_led = {}
    trx_end = []

    def emit_ln_pass1(j):
        buf = x4[j % 3]
        src = d_in["xpre"] if j < 8 else d_in["xown"]
        row = (j % 8) * 128
        dxs = d_x[j % 3]
        DMA("sync", lambda e, b=buf, s=src, r=row: e.dma_start(
            out=b[:], in_=s[r:r + 128, :]), sig=(dxs, 16),
            wait=ln_end[j - 3] if j >= 3 else None)
        R.wait("vector", dxs, dxs.n)
        R.wait("scalar", dxs, dxs.n)
        V(lambda v, b=buf: v.reduce_sum(lsum[:], b[:], axis=AX.X))
        V(lambda v, j=j: v.tensor_scalar_mul(lnm_c[:, j:j + 1], lsum[:],
                                             -1.0 / D), sig=(v_ln, 1))
        S(lambda s, b=buf: s.activation(b[:], b[:], AF.Square,
                                        accum_out=lssq[:]), wait=v_ln.cur())
        S(lambda s, j=j: s.copy(lsq_c[:, j:j + 1], lssq[:]),
          sig=(s_ln, 1))
        ln_end.append((s_ln.h, s_ln.n))

    def emit_ln_stats():
        R.wait("vector", v_ln, v_ln.n)
        R.wait("vector", s_ln, s_ln.n)
        V(lambda v: v.tensor_tensor(lv1_c[:], lnm_c[:], lnm_c[:], ALU.mult))
        V(lambda v: v.tensor_scalar_mul(lv1_c[:], lv1_c[:], -1.0))
        V(lambda v: v.scalar_tensor_tensor(lv1_c[:], lsq_c[:], 1.0 / D,
                                           lv1_c[:], ALU.mult, ALU.add))
        V(lambda v: v.tensor_scalar_add(lv1_c[:], lv1_c[:], EPS))
        rsqrt_nr(lrs_c[:], lv1_c[:], lvi_c[:], lvi2_c[:], lv2_c[:], wtmp[:, 0:16])
        V(lambda v: v.engine_nop(), sig=(v_misc, 1))

    q8s_led = {}

    def emit_ln_pass2A(j):
        buf = x4[j % 3]
        src = d_in["xpre"] if j < 8 else d_in["xown"]
        row = (j % 8) * 128
        own = j >= 8
        dxs = d_x[j % 3]
        DMA("sync", lambda e, b=buf, s=src, r=row: e.dma_start(
            out=b[:], in_=s[r:r + 128, :]), sig=(dxs, 16),
            wait=ln_end[16 + j - 3] if j >= 3 else
            ln_end[{0: 15, 1: 13, 2: 14}[j % 3]])
        R.wait("vector", dxs, dxs.n)
        R.wait("scalar", dxs, dxs.n)
        V(lambda v, b=buf, j=j: v.tensor_scalar(
            b[:], b[:], lnm_c[:, j:j + 1], lrs_c[:, j:j + 1],
            ALU.add, ALU.mult), sig=(v_ln, 1))
        if own:
            wl = wrecl[j % 2]
            V(lambda v, b=buf: v.reduce_max(lrm[:], b[:], axis=AX.X,
                                            apply_absolute_value=True))
            V(lambda v, j=j: v.tensor_copy(rmax[:, j - 8:j - 7], lrm[:]))
            if j >= 10:
                R.wait("vector", s_ln, q8s_led[j - 2])
            recip_nr(wl[:], lrm[:], wn1[:])
            V(lambda v, w=wl: v.tensor_scalar_mul(w[:], w[:], 224.0),
              sig=(v_misc, 1))
            if j >= 10:
                R.wait("scalar", t_trx, trx_end[j - 2])
            S(lambda s, b=buf, bq=j % 2, w=wl: s.activation(
                q8s[bq][:], b[:], AF.Copy, scale=w[:]),
              wait=v_misc.cur(), sig=(s_ln, 1))
        q8s_led[j] = s_ln.n

    def emit_ln_pass2B(j):
        buf = x4[j % 3]
        tcol = j * 128
        own = j >= 8
        V(lambda v, b=buf: v.tensor_scalar(b[:], b[:], 256.0, MAGIC,
                                           ALU.mult, ALU.add),
          wait=(s_ln.h, q8s_led[j]) if own else None, sig=(v_ln, 1))
        if j >= 2:
            R.wait("scalar", t_trx, trx_end[j - 2])
        S(lambda s, b=buf, bx=j % 2: s.activation(
            x8s[bx][:], b[:], AF.Copy, scale=1.0 / 256.0, bias=-MAGIC / 256.0),
          wait=v_ln.cur(), sig=(s_ln, 1))
        sln_x8 = s_ln.n
        gb8 = j * 4
        gbq = (j - 8) * 4
        for kg in range(4):
            bank = pb_bf[kg % 2]
            g = gb8 + kg
            for dk in range(4):
                k = kg * 4 + dk
                w = None
                if dk == 0:
                    w = x8_bank_led.get(g - 2)
                    if kg == 0:
                        w = (s_ln.h, sln_x8)
                        if g >= 2:
                            bw = x8_bank_led[g - 2]
                            R.wait("tensor", bw[0], bw[1])
                T(lambda t, k=k, b=bank, bf=j % 2, dk=dk: t.matmul(
                    b[:, dk * 128:(dk + 1) * 128],
                    x8s[bf][:, k * 128:(k + 1) * 128], identb[:],
                    start=True, stop=True),
                  wait=w, sig=(t_trx, 1) if dk == 3 else None)
            if kg % 2 == 0:
                V(lambda v, kg=kg, b=bank, tc=tcol: v.tensor_copy(
                    x8T3[:, kg * 4:(kg + 1) * 4, tc:tc + 128], b[:]),
                  wait=t_trx.cur(), sig=(v_x8T, 1))
                x8_bank_led[g] = (v_x8T.h, v_x8T.n)
            else:
                S(lambda s_, kg=kg, b=bank, tc=tcol: s_.copy(
                    x8T3[:, kg * 4:(kg + 1) * 4, tc:tc + 128], b[:]),
                  wait=t_trx.cur(), sig=(s_x8T, 1))
                x8_bank_led[g] = (s_x8T.h, s_x8T.n)
        if own:
            for kg in range(4):
                qbank = pq_f[kg % 2]
                g = gbq + kg
                for dk in range(4):
                    k = kg * 4 + dk
                    w = None
                    if dk == 0:
                        w = qx_bank_led.get(g - 2)
                        if kg == 0:
                            w = (s_ln.h, q8s_led[j])
                            R.wait("tensor", v_pev, v_pev.n)
                            R.wait("tensor", s_pev, s_pev.n)
                            if g >= 2:
                                bw = qx_bank_led[g - 2]
                                R.wait("tensor", bw[0], bw[1])
                    T(lambda t, k=k, b=qbank, bq=j % 2, dk=dk: t.matmul(
                        b[:, dk * 128:(dk + 1) * 128],
                        q8s[bq][:, k * 128:(k + 1) * 128], ident8[:],
                        start=True, stop=True),
                      wait=w, sig=(t_trx, 1) if dk == 3 else None)
                if kg % 2 == 0:
                    V(lambda v, kg=kg, b=qbank, j=j: v.tensor_copy(
                        qxT3[:, kg * 4:(kg + 1) * 4, (j - 8) * 128:(j - 7) * 128],
                        b[:]), wait=t_trx.cur(), sig=(v_qxT, 1))
                    qx_bank_led[g] = (v_qxT.h, v_qxT.n)
                else:
                    S(lambda s_, kg=kg, b=qbank, j=j: s_.copy(
                        qxT3[:, kg * 4:(kg + 1) * 4, (j - 8) * 128:(j - 7) * 128],
                        b[:]), wait=t_trx.cur(), sig=(s_qxT, 1))
                    qx_bank_led[g] = (s_qxT.h, s_qxT.n)
        ln_end.append((s_ln.h, s_ln.n))
        trx_end.append(t_trx.n)
        ln_x8_led.append(((v_x8T.h, v_x8T.n), (s_x8T.h, s_x8T.n)))

    R.wait("tensor", g_id, 1)
    gu_i = 0
    for j in range(16):
        emit_ln_pass1(j)
    emit_ln_stats()
    for j in range(16):
        emit_ln_pass2A(j)
        if j >= 1:
            emit_ln_pass2B(j - 1)
    emit_ln_pass2B(15)
    esL.close()

    esA = ExitStack()
    sbA, psA = mk(esA)
    fbuf = sbA("fbuf", [128, 1024], F32)
    f2buf = sbA("f2buf", [128, 1024], F32)
    hpre = sbA("hpre", [128, PRE], F32)
    hsqb = [sbA(f"hsqb{i_}", [128, 1024], F32) for i_ in range(3)]
    Hrot = [sbA(f"Hrot{i_}", [128, TOK], F32) for i_ in range(3)]
    rstd_b = hsqb[0]
    pi_ps = [pbank[4], pbank[5]]
    pf_ps = [pbank[6], pbank[7]]
    ph2 = [pbank[0], pbank[1]]

    hsp_led = {}
    hrl_led = {}

    def emit_h2_sum(cc):
        R.wait("tensor", s_hsq, cc + 1)
        if cc == 0:
            R.wait("tensor", v_x8T, v_x8T.n)
            R.wait("tensor", s_x8T, s_x8T.n)
        for hh in range(2):
            T(lambda t, cc=cc, hh=hh: t.matmul(
                ph2[hh][0:1, :], ones_col[:],
                hsqb[cc % 3][:, hh * 512:(hh + 1) * 512],
                start=(cc == 0), stop=(cc == KC - 1), skip_group_check=True),
              sig=(t_h2, 1))

    def emit_attn_chunk(cc):
        bi, bf_ = wifb[(cc % 3) * 2], wifb[(cc % 3) * 2 + 1]
        dws = d_wif[cc % 3]
        wdma = (None if cc < 3 else t_if.lag(16))
        DMA("gpsimd", lambda e, cc=cc, b=bi: e.dma_start(
            out=b[:], in_=d_in["wi_sw"][cc]), sig=(dws, 16), wait=wdma)
        DMA("gpsimd", lambda e, cc=cc, b=bf_: e.dma_start(
            out=b[:], in_=d_in["wf_sw"][cc]), sig=(dws, 16))
        need_w = (dws, dws.n)
        for seg in range(2):
            fmm_led = None
            for wsel, wbuf, pss in ((1, bf_, pf_ps), (0, bi, pi_ps)):
                for hh in range(2):
                    colbase = seg * 1024 + hh * 512
                    bank = pss[hh]
                    for k in range(KC):
                        w = None
                        if k == 0:
                            R.wait("tensor", need_w[0], need_w[1])
                            blk = seg * 8 + hh * 4 + 3
                            if cc == 0:
                                R.wait("tensor", ln_x8_led[blk][0][0],
                                       ln_x8_led[blk][0][1])
                            R.wait("tensor", ln_x8_led[blk][1][0],
                                       ln_x8_led[blk][1][1])
                            w = v_att.cur() if wsel == 0 else s_att.cur()
                        T(lambda t, k=k, b=bank, wb=wbuf, cb=colbase: t.matmul(
                            b[:], wb[:, k * 128:(k + 1) * 128],
                            x8T3[:, k, cb:cb + 512],
                            start=(k == 0), stop=(k == KC - 1)),
                          wait=w if k == 0 else None,
                          sig=(t_if, 1) if k == KC - 1 else None)
                if wsel == 1:
                    fmm_led = (t_if.h, t_if.n)
            tif_now = t_if.n
            R.wait("scalar", t_if, fmm_led[1])
            R.wait("scalar", v_scan, v_scan.n)
            for hh in range(2):
                S(lambda s, cc=cc, hh=hh: s.activation(
                    fbuf[:, hh * 512:(hh + 1) * 512], pf_ps[hh][:], AF.Sigmoid,
                    scale=sf[:, cc:cc + 1]))
                S(lambda s, cc=cc, hh=hh: s.activation(
                    f2buf[:, hh * 512:(hh + 1) * 512], pf_ps[hh][:], AF.Sigmoid,
                    scale=nsf[:, cc:cc + 1]), sig=(s_att, 1))
            satt_now = s_att.n
            R.wait("vector", t_if, tif_now)
            R.wait("vector", s_att, satt_now)
            for hh in range(2):
                V(lambda v, cc=cc, hh=hh: v.scalar_tensor_tensor(
                    f2buf[:, hh * 512:(hh + 1) * 512], pi_ps[hh][:],
                    si[:, cc:cc + 1], f2buf[:, hh * 512:(hh + 1) * 512],
                    ALU.mult, ALU.mult), sig=(v_att, 1))
            if seg == 0:
                V(lambda v: v.tensor_tensor_scan(
                    hpre[:], fbuf[:], f2buf[:], 0.0, ALU.mult, ALU.add),
                  sig=(v_scan, 1))
            else:
                if cc >= 3:
                    R.wait("vector", d_hsp[cc % 3], hsp_led[cc - 3][1])
                V(lambda v, cc=cc: v.tensor_tensor_scan(
                    Hrot[cc % 3][:], fbuf[:], f2buf[:], hpre[:, PRE - 1:PRE],
                    ALU.mult, ALU.add), sig=(v_scan, 1))
        R.wait("scalar", t_h2, t_h2.n)
        S(lambda s, cc=cc: s.activation(hsqb[cc % 3][:], Hrot[cc % 3][:],
                                        AF.Square),
          wait=v_scan.cur(), sig=(s_hsq, 1))
        dhs = d_hsp[cc % 3]
        DMA("sync", lambda e, cc=cc: e.dma_start(
            out=h_sc[cc], in_=Hrot[cc % 3][:]), sig=(dhs, 16),
            wait=s_hsq.cur())
        hsp_led[cc] = (dhs.h, dhs.n)
        if cc >= 2:
            emit_h2_sum(cc - 2)

    for cc in range(KC):
        emit_attn_chunk(cc)
        if gu_i < len(gu_groups):
            emit_prep_group(*gu_groups[gu_i])
            gu_i += 1
        if cc < 6 and gu_i < len(gu_groups):
            emit_prep_group(*gu_groups[gu_i])
            gu_i += 1
    emit_h2_sum(KC - 2)
    emit_h2_sum(KC - 1)
    while gu_i < len(gu_groups):
        emit_prep_group(*gu_groups[gu_i])
        gu_i += 1
    flush_prep()

    def emit_ws_row(cols, nchunks, dr_idx):
        n = nchunks
        R.wait("vector", g_pmax, g_pmax.n)
        R.wait("vector", t_ptr, t_ptr.n)
        V(lambda v, n=n, cl=cols: v.tensor_copy(wtmp[:, 0:n], cl[:, 0:n]),
          sig=(v_misc, 1))
        T(lambda t, n=n: t.matmul(pp_tr[0][0:n, 0:128], wtmp[:, 0:n],
                                  identf[:], start=True, stop=True),
          wait=v_misc.cur(), sig=(t_ptr, 1))
        R.wait("vector", d_wsr, d_wsr.n)
        V(lambda v, n=n: v.tensor_copy(wst_sb[0:n, :], pp_tr[0][0:n, 0:128]),
          wait=t_ptr.cur(), sig=(v_pev, 1))
        DMA("sync", lambda e, n=n, di=dr_idx: e.dma_start(
            out=ws_dr[di][0:n, :], in_=wst_sb[0:n, :]), sig=(d_wsr, 16),
            wait=v_pev.cur())

    emit_ws_row(wsg_cols, IC, 0)
    emit_ws_row(wsu_cols, IC, 1)

    R.wait("vector", t_h2, t_h2.n)
    V(lambda v: v.tensor_copy(ssum[0:1, 0:512], ph2[0][0:1, :]))
    V(lambda v: v.tensor_copy(ssum[0:1, 512:1024], ph2[1][0:1, :]))
    V(lambda v: v.tensor_scalar(ssum[:], ssum[:], 1.0 / D, EPS, ALU.mult,
                                ALU.add))
    rsqrt_nr(rstd_r[:], ssum[:], ir1[:], ir2[:], hpre[0:1, :], rstd_b[0:1, :])
    V(lambda v: v.engine_nop(), sig=(v_misc, 1))
    vm_rstd = v_misc.n
    for hh in range(2):
        T(lambda t, hh=hh: t.matmul(
            ph2[hh][:, :], ones_row[:], rstd_r[0:1, hh * 512:(hh + 1) * 512],
            start=True, stop=True, skip_group_check=True),
          wait=(v_misc.h, vm_rstd), sig=(t_bc, 1))
        V(lambda v, hh=hh: v.tensor_copy(
            rstd_b[:, hh * 512:(hh + 1) * 512], ph2[hh][:, :]),
          wait=t_bc.cur(), sig=(v_rb, 1))

    sgbuf = fbuf
    g2buf = f2buf
    o1buf = hpre
    R.wait("scalar", v_scan, v_scan.n)

    def emit_epi_chunk(cc):
        b = wifb[cc % 2]
        dgs = d_wg[cc % 2]
        wdma = (t_if.cur() if cc < 2 else t_g.lag(2))
        DMA("gpsimd", lambda e, cc=cc, b=b: e.dma_start(
            out=b[:], in_=d_in["wg_sw"][cc]), sig=(dgs, 16), wait=wdma)
        need = (dgs, dgs.n)
        hb = Hrot[cc % 3]
        dhr = d_hrl[cc % 3]
        last_spill = {0: 15, 1: 13, 2: 14}[cc % 3]
        DMA("sync", lambda e, cc=cc, h=hb: e.dma_start(
            out=h[:], in_=h_sc[cc]), sig=(dhr, 16),
            wait=(hsp_led[last_spill] if cc < 3 else v_epi.lag(2)))
        hrl_led[cc] = (dhr.h, dhr.n)
        bankset = pi_ps if cc % 2 == 0 else pf_ps
        for hh in range(2):
            bank = bankset[hh]
            for k in range(KC):
                w = None
                if k == 0:
                    R.wait("tensor", need[0], need[1])
                    w = (v_epi.lag(1) if cc >= 2 else
                         ((v_att.h, v_att.n) if cc == 0 else (s_att.h, s_att.n)))
                T(lambda t, k=k, b2=bank, wb=b, hh=hh: t.matmul(
                    b2[:], wb[:, k * 128:(k + 1) * 128],
                    x8T3[:, k, 1024 + hh * 512:1024 + (hh + 1) * 512],
                    start=(k == 0), stop=(k == KC - 1)),
                  wait=w if k == 0 else None,
                  sig=(t_g, 1) if k == KC - 1 else None)
        tg_now = t_g.n
        R.wait("scalar", t_g, tg_now)
        R.wait("scalar", v_epi, v_epi.n)
        for hh in range(2):
            S(lambda s, cc=cc, hh=hh, bk=bankset[hh]: s.activation(
                sgbuf[:, hh * 512:(hh + 1) * 512], bk[:], AF.Sigmoid,
                scale=sgv[:, cc:cc + 1]), sig=(s_epi, 1))
        sep_now = s_epi.n
        R.wait("vector", t_g, tg_now)
        R.wait("vector", s_epi, sep_now)
        R.wait("vector", d_hrl[cc % 3], hrl_led[cc][1])
        for hh in range(2):
            V(lambda v, cc=cc, hh=hh, bk=bankset[hh]: v.tensor_scalar_mul(
                g2buf[:, hh * 512:(hh + 1) * 512], bk[:],
                sgv[:, cc:cc + 1]))
        V(lambda v: v.tensor_tensor(g2buf[:], g2buf[:], sgbuf[:], ALU.mult))
        V(lambda v, cc=cc, h=hb: v.scalar_tensor_tensor(
            o1buf[:], h[:], gso[:, cc:cc + 1], rstd_b[:], ALU.mult, ALU.mult),
          wait=(v_rb.h, 2) if cc == 0 else None)
        V(lambda v, cc=cc: v.tensor_tensor(
            x8T3[:, cc, 0:1024], o1buf[:], g2buf[:], ALU.mult), sig=(v_epi, 1))

    for cc in range(KC):
        emit_epi_chunk(cc)
    esA.close()
    esP.close()

    esB = ExitStack()
    sbB, psB = mk(esB)
    woq2 = [[sbB(f"woq_{p_}_{k}", [128, 512], BF16) for k in range(KC)]
            for p_ in range(2)]
    sst = [sbB(f"sst{i_}", [128, 512], F32) for i_ in range(2)]
    smg = sbB("smg", [128, 512], F32)
    po_ps = [pbank[0], pbank[1]]
    natd = [sbB("natd0", [128, D], F32), sbB("natd1", [128, D], F32),
            sbB("natd2", [128, 1536], F32)]
    q8d = [sbB(f"q8d{i_}", [128, I], FP8) for i_ in range(4)]
    pstgd = sbB("pstgd", [128, IC * 512], FP8)
    ppd = [pbank[2], pbank[3]]

    dq_spq = {}
    dtr_led = {}
    dsd_led = {}

    def emit_down_quant(gi):
        widths = [(0, 2048), (2048, 2048), (4096, 1536)]
        if gi >= 1:
            R.wait("vector", t_ptr, dtr_led[gi - 1])
        for j in range(4):
            c = gi * 4 + j
            led = []
            for pi_, (off, w) in enumerate(widths):
                buf = natd[pi_]
                dnd = d_natd[pi_ % 2] if pi_ < 2 else d_natd2
                if gi == 0 and j == 0:
                    wdma = v_epi.cur() if pi_ == 0 else None
                else:
                    wdma = s_pq.lag(2 - pi_) if pi_ < 2 else s_pq.cur()
                DMA("scalar", lambda e, b=buf, c=c, o=off, w=w: e.dma_start(
                    out=b[:, 0:w], in_=d_in["wdn_nat"][c][:, o:o + w]),
                    sig=(dnd, 16), wait=wdma)
                led.append((dnd.h, dnd.n))
            for pi_, (off, w) in enumerate(widths):
                buf = natd[pi_]
                R.wait("vector", led[pi_][0], led[pi_][1])
                if pi_ == 0:
                    R.wait("vector", s_pq, s_pq.n)
                    V(lambda v, b=buf, w=w: v.reduce_max(
                        wmaxc[:], b[:, 0:w], axis=AX.X,
                        apply_absolute_value=True), sig=(g_pmax, 1))
                else:
                    V(lambda v, b=buf, w=w: v.reduce_max(
                        wn2[:], b[:, 0:w], axis=AX.X,
                        apply_absolute_value=True))
                    V(lambda v: v.tensor_max(wmaxc[:], wmaxc[:], wn2[:]),
                      sig=(g_pmax, 1))
            V(lambda v, c=c: v.tensor_copy(wsd_cols[:, c:c + 1], wmaxc[:]),
              sig=(g_pmax, 1))
            recip_nr(wrec[:], wmaxc[:], wn1[:])
            V(lambda v: v.tensor_scalar_mul(wrec[:], wrec[:], 224.0),
              sig=(v_misc, 1))
            for pi_, (off, w) in enumerate(widths):
                buf = natd[pi_]
                V(lambda v, b=buf, j=j, o=off, w=w: v.tensor_scalar(
                    q8d[j][:, o:o + w], b[:, 0:w], wrec[:], None, ALU.mult),
                  sig=(s_pq, 1))
        dq_spq[gi] = s_pq.n

    def emit_down_tr(gi):
        for icc in range(IC):
            bank = ppd[icc % 2]
            for j in range(4):
                w = None
                if j == 0:
                    w = (s_pq.h, dq_spq[gi]) if icc == 0 else v_pev.lag(1)
                T(lambda t, j=j, icc=icc, b=bank: t.matmul(
                    b[:, j * 128:(j + 1) * 128],
                    q8d[j][:, icc * 128:(icc + 1) * 128], ident8[:],
                    start=True, stop=True),
                  wait=w, sig=(t_ptr, 1) if j == 3 else None)
            if icc % 11 == 0 and gi >= 1:
                wsem, wval = dsd_led[(gi - 1, icc // 11)]
                R.wait("vector", wsem, wval)
            V(lambda v, icc=icc, b=bank: v.tensor_copy(
                pstgd[:, icc * 512:(icc + 1) * 512], b[:]),
              wait=t_ptr.cur(), sig=(v_pev, 1))
            if icc % 11 == 10:
                p = icc // 11
                DMA("sync", lambda e, gi=gi, p=p: e.dma_start(
                    out=qwd_sc[gi][:, p * 5632:(p + 1) * 5632],
                    in_=pstgd[:, p * 5632:(p + 1) * 5632]),
                    sig=(d_scrd4[p], 16), wait=v_pev.cur())
                dsd_led[(gi, p)] = (d_scrd4[p], d_scrd4[p].n)
        dtr_led[gi] = t_ptr.n

    for _ds in d_scr2:
        R.wait("sync", _ds, _ds.n)
        R.wait("scalar", _ds, _ds.n)
        R.wait("vector", _ds, _ds.n)
        R.wait("gpsimd", _ds, _ds.n)
    R.wait("sync", d_wsr, d_wsr.n)
    for _ds in d_scrd4:
        R.wait("sync", _ds, _ds.n)
    R.wait("sync", v_epi, KC)
    R.wait("gpsimd", d_wsr, d_wsr.n)
    R.wait("gpsimd", v_epi, KC)
    oproj_sst_led = {}
    oproj_top_led = {}

    def emit_oproj_od(od):
        wq = woq2[od % 2]
        dwo = d_wo2[od % 2]
        for k in range(KC):
            wdma = ((v_epi.h, KC) if od < 2
                    else (t_op.h, oproj_top_led[od - 2]))
            DMA("gpsimd", lambda e, k=k, od=od, wq=wq: e.dma_start(
                out=wq[k][:],
                in_=d_in["woT_sw"][k][:, od * 512:(od + 1) * 512]),
                sig=(dwo, 16), wait=wdma)
        need = (dwo.h, dwo.n)
        sst_led = oproj_sst_led
        for tj in range(8):
            bank = po_ps[tj % 2]
            for cc in range(KC):
                w = None
                if cc == 0:
                    R.wait("tensor", need[0], need[1])
                    if od == 0 and tj == 0:
                        R.wait("tensor", v_epi, v_epi.n)
                        R.wait("tensor", v_rb, 2)
                    w = v_sev.lag(1)
                T(lambda t, cc=cc, b=bank, tj=tj: t.matmul(
                    b[:], x8T3[:, cc, tj * 128:(tj + 1) * 128], wq[cc][:],
                    start=(cc == 0), stop=(cc == KC - 1)),
                  wait=w if cc == 0 else None,
                  sig=(t_op, 1) if cc == KC - 1 else None)
            top_now = t_op.n
            R.wait("vector", t_op, top_now)
            gsst = od * 8 + tj
            if gsst >= 2:
                R.wait("vector", d_so[gsst % 2], sst_led[gsst - 2][1])
            V(lambda v, b=bank: v.tensor_scalar(
                smg[:], b[:], 256.0, MAGIC, ALU.mult, ALU.add))
            V(lambda v, tj=tj: v.tensor_scalar(
                sst[tj % 2][:], smg[:], -MAGIC, 1.0 / 256.0, ALU.add, ALU.mult),
              sig=(v_sev, 1))
            dso = d_so[gsst % 2]
            DMA("gpsimd", lambda e, tj=tj, od=od: e.dma_start(
                out=s_sc[tj * 128:(tj + 1) * 128, od * 512:(od + 1) * 512],
                in_=sst[tj % 2][:]), sig=(dso, 16), wait=v_sev.cur())
            sst_led[gsst] = (dso.h, dso.n)
        oproj_top_led[od] = t_op.n

    for od in range(4):
        emit_oproj_od(od)
        if od >= 1:
            emit_down_tr(od - 1)
        emit_down_quant(od)
    emit_down_tr(3)
    emit_ws_row(wsd_cols, KC, 2)

    esB.close()
    esX.close()

    esC = ExitStack()
    sbC, psC = mk(esC)
    Tt = [sbC(f"Tt{i_}", [128, I], F32) for i_ in range(4)]
    qt8r = [sbC(f"qt8_{i_}", [128, I], FP8) for i_ in range(2)]
    qtT = [sbC(f"qtT{i_}", [128, I], FP8) for i_ in range(4)]
    wgq = [sbC(f"wgq{i_}", [128, 2048], FP8) for i_ in range(4)]
    wuq = [sbC(f"wuq{i_}", [128, 2048], FP8) for i_ in range(4)]
    wdq = [sbC(f"wdq{i_}", [128, 2048], FP8) for i_ in range(4)]
    wsb = [sbC(f"wsb{i_}", [128, 512], F32) for i_ in range(2)]
    ayb = sbC("ayb", [128, 512], F32)
    y2b = sbC("y2b", [128, 512], F32)
    sgm8 = sbC("sgm8", [128, 512], F32)
    xrl = [sbC(f"xrl{i_}", [128, 512], F32) for i_ in range(2)]
    srl = [sbC(f"srl{i_}", [128, 512], F32) for i_ in range(2)]
    outb = [sbC(f"outb{i_}", [128, 512], F32) for i_ in range(2)]
    tm1 = sbC("tm1", [128, 1], F32)
    tm2l = [sbC(f"tm2_{i_}", [128, 1], F32) for i_ in range(2)]
    pgu = pbank

    for _ds in d_scr2 + d_scrd4:
        R.wait("sync", _ds, _ds.n)
    R.wait("sync", d_wsr, d_wsr.n)
    R.wait("sync", d_so[0], d_so[0].n)
    R.wait("sync", d_so[1], d_so[1].n)
    R.wait("sync", t_op, t_op.n)
    R.wait("sync", v_sev, v_sev.n)
    R.wait("vector", d_so[0], d_so[0].n)
    R.wait("vector", d_so[1], d_so[1].n)
    R.wait("vector", d_wsr, d_wsr.n)
    for _ds in d_scrd4:
        R.wait("vector", _ds, _ds.n)
    R.wait("vector", t_op, t_op.n)
    R.wait("vector", v_pev, v_pev.n)
    R.wait("vector", s_pev, s_pev.n)
    R.wait("scalar", t_op, t_op.n)
    R.wait("scalar", v_pev, v_pev.n)
    R.wait("tensor", v_sev, v_sev.n)
    R.wait("tensor", v_pev, v_pev.n)
    R.wait("tensor", s_pev, s_pev.n)
    R.wait("tensor", v_qxT, v_qxT.n)
    R.wait("tensor", s_qxT, s_qxT.n)
    for _ds in d_scrd4:
        R.wait("gpsimd", _ds, _ds.n)

    V(lambda v: v.tensor_scalar_mul(sgf[:], rmax[:], FQ))
    V(lambda v: v.tensor_tensor(sg2[:], sgf[:], sgf[:], ALU.mult),
      sig=(v_misc, 1))
    vm_sgf = v_misc.n

    bc_state = {"n": 0}
    bc_use_led = {}

    def bcast_ws(dr_idx, base):
        n = bc_state["n"]
        bc_state["n"] = n + 1
        p = n % 2
        drs = d_row[p]
        DMA("sync", lambda e, di=dr_idx, b=base, p=p: e.dma_start(
            out=wsb[p][:],
            in_=ws_dr[di:di + 1, :, :].rearrange("a b c -> a (b c)")
                [0:1, b:b + 512].partition_broadcast(128)),
            sig=(drs, 16), wait=bc_use_led.get(n - 2))
        return p, (drs.h, drs.n)

    def bc_mark_used(n_offset=0):
        bc_use_led[bc_state["n"] - 1] = (ch_v.h, ch_v.n)

    mm_led = {}
    dq_led = {}
    dn_led = {}
    dqd_led = {}
    qt_sqt_led = {}
    qt_trt_led = {}
    y_led = {}
    ag_led = {}
    up_led = {}
    v_ag = sem("v_ag")

    def load_quarter(wsel, gq):
        sc = qwg_sc if wsel == 0 else qwu_sc
        ring = wgq if wsel == 0 else wuq
        ic_ = (gq % (IB * 4)) // 4
        q = gq % 4
        wq = mm_led.get((wsel, gq - 4))
        dqs = d_q8[wsel][gq % 4]
        DMA("sync", lambda e, sc=sc, ic_=ic_, q=q, b=ring[gq % 4]: e.dma_start(
            out=b[:], in_=sc[ic_][:, q * 2048:(q + 1) * 2048]),
            sig=(dqs, 16), wait=wq)
        dq_led[(wsel, gq)] = (dqs.h, dqs.n)

    for wsel in range(2):
        for gq0 in range(4):
            load_quarter(wsel, gq0)

    GB, UB = pgu[0:4], pgu[4:8]
    for TS in range(2):
        R.wait("vector", s_qt, s_qt.n)
        R.wait("vector", t_dn, t_dn.n)
        R.wait("tensor", v_fin, v_fin.n)
        for ic in range(IB):
            bc_g = bcast_ws(0, ic * 512)
            bc_u = bcast_ws(1, ic * 512)
            for wsel, bufs, sem4, banks in ((0, wgq, t_g4, GB),
                                            (1, wuq, t_u4, UB)):
                glb = TS * IB + ic
                guard = (ag_led if wsel == 0 else up_led).get(glb - 1)
                if guard is not None:
                    R.wait("tensor", guard[0], guard[1])
                for k in range(KC):
                    gq = TS * IB * 4 + ic * 4 + k // 4
                    for tj in range(4):
                        if k % 4 == 0 and tj == 0:
                            R.wait("tensor", dq_led[(wsel, gq)][0],
                                   dq_led[(wsel, gq)][1])
                        last_q = (k % 4 == 3 and tj == 3)
                        T(lambda t, k=k, tj=tj, bu=bufs, sl=gq % 4, TS=TS,
                          bks=banks: t.matmul(
                            bks[tj][:],
                            qxT3[:, k, TS * 512 + tj * 128:TS * 512 + (tj + 1) * 128],
                            bu[sl][:, (k % 4) * 512:(k % 4 + 1) * 512],
                            start=(k == 0), stop=(k == KC - 1)),
                          sig=(sem4, 1) if last_q else None)
                        if last_q:
                            mm_led[(wsel, gq)] = (sem4.h, sem4.n)
                            if gq + 4 < 2 * IB * 4:
                                load_quarter(wsel, gq + 4)
            glb = TS * IB + ic
            R.wait("vector", t_g4, t_g4.n)
            R.wait("vector", bc_g[1][0], bc_g[1][1])
            for tj in range(4):
                V(lambda v, tj=tj, ic=ic, p=bc_g[0]: v.tensor_tensor(
                    Tt[tj][:, ic * 512:(ic + 1) * 512], GB[tj][:],
                    wsb[p][:], ALU.mult),
                  wait=(v_misc.h, vm_sgf) if (TS == 0 and ic == 0
                                              and tj == 0) else None,
                  sig=(v_ag, 1))
            ag_led[glb] = (v_ag.h, v_ag.n)
            bc_use_led[bc_state["n"] - 2] = (v_ag.h, v_ag.n)
            R.wait("scalar", t_u4, t_u4.n)
            R.wait("scalar", v_ffn, v_ffn.n)
            R.wait("vector", t_u4, t_u4.n)
            R.wait("vector", bc_u[1][0], bc_u[1][1])
            for tj in range(4):
                S(lambda s, tj=tj, ic=ic, TS=TS: s.activation(
                    sgm8[:], Tt[tj][:, ic * 512:(ic + 1) * 512],
                    AF.Sigmoid,
                    scale=sgf[:, TS * 4 + tj:TS * 4 + tj + 1]),
                  wait=v_ag.cur() if tj == 0 else v_ffn.cur(), sig=(s_ffn, 1))
                V(lambda v, tj=tj, p=bc_u[0]: v.tensor_tensor(
                    ayb[:], UB[tj][:], wsb[p][:], ALU.mult))
                V(lambda v, tj=tj, ic=ic, TS=TS: v.scalar_tensor_tensor(
                    y2b[:], Tt[tj][:, ic * 512:(ic + 1) * 512],
                    sg2[:, TS * 4 + tj:TS * 4 + tj + 1], ayb[:],
                    ALU.mult, ALU.mult))
                V(lambda v, tj=tj, ic=ic: v.tensor_tensor(
                    Tt[tj][:, ic * 512:(ic + 1) * 512], y2b[:], sgm8[:],
                    ALU.mult), wait=s_ffn.cur(), sig=(v_ffn, 1))
            up_led[glb] = (v_ffn.h, v_ffn.n)
            bc_use_led[bc_state["n"] - 1] = (v_ffn.h, v_ffn.n)
        def emit_t_quant(tj):
            tm2b = tm2l[tj % 2]
            V(lambda v, tj=tj: v.reduce_max(tm1[:], Tt[tj][:], axis=AX.X,
                                            apply_absolute_value=True))
            V(lambda v, tj=tj, TS=TS: v.tensor_scalar_mul(
                tyf[:, TS * 4 + tj:TS * 4 + tj + 1], tm1[:], FQ))
            pw = qt_sqt_led.get((TS, tj - 2))
            if pw is not None:
                R.wait("vector", s_qt, pw)
            recip_nr(tm2b[:], tm1[:], wn1[:])
            V(lambda v, b=tm2b: v.tensor_scalar_mul(b[:], b[:], 224.0),
              sig=(v_tmax, 1))
            ptr = qt_trt_led.get((TS, tj - 2), qt_trt_led.get((TS - 1, tj + 2)))
            if ptr is not None:
                R.wait("scalar", t_trt, ptr)
            S(lambda s, tj=tj, b=tm2b: s.activation(
                qt8r[tj % 2][:], Tt[tj][:], AF.Copy, scale=b[:]),
              wait=v_tmax.cur(), sig=(s_qt, 1))
            qt_sqt_led[(TS, tj)] = s_qt.n

        def emit_t_tr(tj):
            if TS == 0 and tj == 0:
                R.wait("tensor", v_ffn, v_ffn.n)
            for g4 in range(11):
                bank = pgu[g4 % 2]
                for dk in range(4):
                    icc = g4 * 4 + dk
                    w = None
                    if dk == 0:
                        bg = qtT_bank_led.get(g4 % 2)
                        if g4 == 0:
                            w = (s_qt.h, qt_sqt_led[(TS, tj)])
                            if bg is not None:
                                R.wait("tensor", bg[0], bg[1])
                        else:
                            w = bg
                    T(lambda t, icc=icc, b=bank, dk=dk, tj=tj: t.matmul(
                        b[:, dk * 128:(dk + 1) * 128],
                        qt8r[tj % 2][:, icc * 128:(icc + 1) * 128], ident8[:],
                        start=True, stop=True),
                      wait=w, sig=(t_trt, 1) if dk == 3 else None)
                V(lambda v, g4=g4, b=bank, tj=tj: v.tensor_copy(
                    qtT[tj][:, g4 * 512:(g4 + 1) * 512], b[:]),
                  wait=t_trt.cur(), sig=(v_qtT, 1))
                qtT_bank_led[g4 % 2] = (v_qtT.h, v_qtT.n)
            qt_trt_led[(TS, tj)] = t_trt.n

        qtT_bank_led = {}
        for tj in range(4):
            emit_t_quant(tj)
            if tj >= 1:
                emit_t_tr(tj - 1)
        emit_t_tr(3)
        for hc in range(4):
            bc_d = bcast_ws(2, hc * 512)
            for ib_ in range(IB):
                gb = (TS * 4 + hc) * IB + ib_
                dqd = d_q8d[gb % 4]
                DMA("sync", lambda e, ib_=ib_, hc=hc, b=wdq[gb % 4]:
                    e.dma_start(out=b[:],
                                in_=qwd_sc[hc][:, ib_ * 2048:(ib_ + 1) * 2048]),
                    sig=(dqd, 16), wait=dn_led.get(gb - 4))
                dqd_led[gb] = (dqd.h, dqd.n)
                R.wait("tensor", dqd_led[gb][0], dqd_led[gb][1])
                if ib_ == 0:
                    R.wait("tensor", v_qtT, v_qtT.n)
                    R.wait("tensor", v_fin, v_fin.n)
                for sl4 in range(4):
                    icc = ib_ * 4 + sl4
                    for tj in range(4):
                        T(lambda t, icc=icc, tj=tj, gb=gb, sl4=sl4: t.matmul(
                            UB[tj][:], qtT[tj][:, icc * 128:(icc + 1) * 128],
                            wdq[gb % 4][:, sl4 * 512:(sl4 + 1) * 512],
                            start=(icc == 0), stop=(icc == IC - 1)),
                          sig=(t_dn, 1))
                dn_led[gb] = (t_dn.h, t_dn.n)
            tdn_now = t_dn.n
            for tj in range(4):
                rowb = TS * 512 + tj * 128
                gfin = (TS * 4 + hc) * 4 + tj
                dsxs = d_sx[tj % 2]
                dsss = d_ss[tj % 2]
                DMA("scalar", lambda e, rowb=rowb, hc=hc, b=xrl[tj % 2]:
                    e.dma_start(out=b[:],
                                in_=d_in["xown"][rowb:rowb + 128,
                                                 hc * 512:(hc + 1) * 512]),
                    sig=(dsxs, 16), wait=v_fin.lag(1))
                DMA("scalar", lambda e, rowb=rowb, hc=hc, b=srl[tj % 2]:
                    e.dma_start(out=b[:],
                                in_=s_sc[rowb:rowb + 128,
                                         hc * 512:(hc + 1) * 512]),
                    sig=(dsss, 16), wait=v_fin.lag(1))
                dss_now = (dsss, dsss.n)
                dsx_now = (dsxs, dsxs.n)
                R.wait("vector", t_dn, tdn_now)
                R.wait("vector", bc_d[1][0], bc_d[1][1])
                V(lambda v, tj=tj, p=bc_d[0]: v.tensor_tensor(
                    ayb[:], UB[tj][:], wsb[p][:], ALU.mult))
                S(lambda s, tj=tj, TS=TS: s.activation(
                    sgm8[:], ayb[:], AF.Sigmoid,
                    scale=tyf[:, TS * 4 + tj:TS * 4 + tj + 1]),
                  wait=(ch_v.h, ch_v.n), sig=(s_ffn, 1))
                V(lambda v, tj=tj, TS=TS: v.tensor_scalar_mul(
                    y2b[:], ayb[:], tyf[:, TS * 4 + tj:TS * 4 + tj + 1]))
                V(lambda v: v.tensor_tensor(y2b[:], y2b[:], sgm8[:], ALU.mult),
                  wait=s_ffn.cur())
                R.wait("vector", dss_now[0], dss_now[1])
                R.wait("vector", dsx_now[0], dsx_now[1])
                V(lambda v, tj=tj: v.tensor_tensor(y2b[:], y2b[:],
                                                   srl[tj % 2][:], ALU.mult))
                V(lambda v, tj=tj: v.tensor_tensor(outb[tj % 2][:], y2b[:],
                                                   xrl[tj % 2][:], ALU.add),
                  wait=y_led.get(gfin - 2), sig=(v_fin, 1))
                dys = d_y[gfin % 2]
                DMA("gpsimd", lambda e, rowb=rowb, hc=hc, b=outb[tj % 2]:
                    e.dma_start(out=y_out[rowb:rowb + 128,
                                          hc * 512:(hc + 1) * 512], in_=b[:]),
                    sig=(dys, 16), wait=v_fin.cur())
                y_led[gfin] = (dys.h, dys.n)
            bc_use_led[bc_state["n"] - 1] = (ch_v.h, ch_v.n)

    R.wait("gpsimd", d_y[0], d_y[0].n)
    R.wait("gpsimd", d_y[1], d_y[1].n)
    R.wait("sync", d_y[0], d_y[0].n)
    R.wait("sync", d_y[1], d_y[1].n)

    with nc.Block() as block:
        @block.sync
        def _(e):
            R.replay("sync", e)

        @block.vector
        def _(e):
            R.replay("vector", e)

        @block.scalar
        def _(e):
            R.replay("scalar", e)

        @block.gpsimd
        def _(e):
            R.replay("gpsimd", e)

        @block.tensor
        def _(e):
            R.replay("tensor", e)

    esC.close()
    es0.close()
    return nc


def _host_prepare(ins):
    x = np.ascontiguousarray(ins["L0_attn_hidden_states"], dtype=np.float32)
    w_i, w_f, w_g, w_o = (np.asarray(ins[f"L0_attn_w_{k}"], dtype=np.float32)
                          for k in "ifgo")
    down_idx = np.asarray(ins["L0_ffn_down_reorder_index"]).astype(np.int64)

    def strip_sw(w):
        t = w.reshape(KC, 128, KC, 128)
        t = np.transpose(t, (2, 1, 0, 3))
        return np.ascontiguousarray(t.reshape(KC, 128, D), dtype=np.float32)

    def vec_img(v):
        return np.ascontiguousarray(
            np.asarray(v, dtype=np.float32).reshape(KC, 128).T)

    shared = {
        "wi_sw": strip_sw(w_i), "wf_sw": strip_sw(w_f), "wg_sw": strip_sw(w_g),
        "woT_sw": np.ascontiguousarray(w_o.T.reshape(KC, 128, D),
                                       dtype=np.float32),
        "si_img": vec_img(ins["L0_attn_s_i"]),
        "sf_img": vec_img(ins["L0_attn_s_f"]),
        "sg_img": vec_img(ins["L0_attn_s_g"]),
        "so_img": vec_img(ins["L0_attn_s_o"]),
        "gnw_img": vec_img(ins["L0_attn_g_norm_w"]),
        "wgat_nat": np.ascontiguousarray(
            np.asarray(ins["L0_ffn_w_gate"], dtype=np.float32)[down_idx]
            .reshape(IC, 128, D)),
        "wup_nat": np.ascontiguousarray(
            np.asarray(ins["L0_ffn_w_up"], dtype=np.float32)[down_idx]
            .reshape(IC, 128, D)),
        "wdn_nat": np.ascontiguousarray(
            np.asarray(ins["L0_ffn_w_down"], dtype=np.float32)
            .reshape(KC, 128, I)),
    }
    zeros_pre = np.zeros((PRE, D), dtype=np.float32)
    in_maps = []
    for core in range(8):
        b, half = core // 2, core % 2
        m = dict(shared)
        m["xown"] = np.ascontiguousarray(x[b, half * TOK:(half + 1) * TOK, :])
        m["xpre"] = (np.ascontiguousarray(x[b, 0:PRE, :]) if half == 1
                     else zeros_pre)
        in_maps.append(m)
    return in_maps


def kernel(**inputs):
    in_maps = _host_prepare(inputs)
    nc = build_program()
    res = run_bass_kernel_spmd(nc, in_maps, core_ids=list(range(8)))
    out = np.empty((4, 2048, D), dtype=np.float32)
    for core in range(8):
        b, half = core // 2, core % 2
        out[b, half * TOK:(half + 1) * TOK, :] = np.asarray(res.results[core]["y"])
    return out
